# version 10
# baseline (speedup 1.0000x reference)
import sys
import numpy as np
import ml_dtypes
import concourse.bass as bass
import concourse.mybir as mybir
from contextlib import ExitStack

F32 = mybir.dt.float32
BF16 = mybir.dt.bfloat16
ALU = mybir.AluOpType
AF = mybir.ActivationFunctionType
AX = mybir.AxisListType

T = 2048
D = 1024
NCH = 8
H = 16
HS = 64
INNER = 4096
ICH = 32
NT = 4
LNEPS = 1e-5

ENGS = ("sync", "vector", "scalar", "tensor", "gpsimd")

KB = 1024
R_CONSTS = 0 * KB
R_ROWS = 4 * KB
R_X = 28 * KB
R_D = 92 * KB
R_E = 156 * KB
POOL_END = 204 * KB


class MK:
    def __init__(self, n_phases=99, dump=None):
        self.n_phases = n_phases
        self.dump = dump
        self.nc = bass.Bass("TRN2", target_bir_lowering=False)
        self.ops = {e: [] for e in ENGS}
        self.cnt = {}
        self.sems = {}
        self.x_parked = False
        self.pending_adds = []
        self.prepass_gen = None
        self.bg_pst_mark = None
        self.qkv_budget = 64

    def op(self, eng, fn, inc=None, inc_by=1):
        ent = {"fn": fn, "inc": None}
        if inc is not None:
            self.cnt[inc] += inc_by
            ent["inc"] = (inc, inc_by, self.cnt[inc])
        self.ops[eng].append(ent)
        return self.cnt.get(inc)

    def wait(self, eng, sem_name, val=None):
        v = self.cnt[sem_name] if val is None else val
        if v > 0:
            self.ops[eng].append({"wait": (sem_name, v)})
        return v

    def val(self, sem_name):
        return self.cnt[sem_name]

    CHAIN = {"vector": "vch", "scalar": "sch", "tensor": "tch", "gpsimd": "gch"}

    def phase_barrier(self, engs=("sync", "vector", "scalar", "tensor", "gpsimd")):
        for eng in ("vector", "scalar", "tensor", "gpsimd"):
            if self.ops[eng] and any("fn" in x for x in self.ops[eng]):
                self.chain(eng)
        snap = {nm: self.cnt[nm] for nm in
                ("ldh", "ldg", "ldgA", "ldgB", "ldgC", "st", "sv", "ss", "stt",
                 "sg", "pst", "pss", "psa", "lha", "lhb", "vch", "sch", "tch", "gch")}
        for eng in engs:
            for nm, v in snap.items():
                self.wait(eng, nm, v)

    def chain(self, eng):
        for ent in reversed(self.ops[eng]):
            if "fn" in ent:
                last = ent
                break
        else:
            return
        if last["inc"] is None:
            nm = self.CHAIN[eng]
            self.cnt[nm] += 1
            last["inc"] = (nm, 1, self.cnt[nm])
        nm, _, v = last["inc"]
        self.ops[eng].append({"wait": (nm, v)})
        return (nm, v)

    def ldh(self, out, in_, sem="ldh"):
        return self.op("sync", lambda e, o=out, i=in_: e.dma_start(out=o, in_=i),
                       inc=sem, inc_by=16)

    def ldg(self, out, in_, sem="ldgA"):
        self.wait("gpsimd", sem, self.cnt[sem])
        return self.op("gpsimd", lambda e, o=out, i=in_: e.dma_start(out=o, in_=i),
                       inc=sem, inc_by=16)

    def sth(self, out, in_):
        self.wait("sync", "st", self.cnt["st"])
        return self.op("sync", lambda e, o=out, i=in_: e.dma_start(out=o, in_=i),
                       inc="st", inc_by=16)

    def alloc(self, name, nbytes, at=None, align=64):
        off = at if at is not None else (self.cursor + align - 1) // align * align
        assert off + nbytes <= self.POOLB, f"pool overflow: {name} @{off}+{nbytes}"
        self.spans[name] = (off, nbytes)
        self.cursor = max(self.cursor, off + nbytes) if at is None else self.cursor
        self.watermark = max(self.watermark, off + nbytes)
        return off

    def f32(self, name, cols=None):
        off, size = self.spans[name]
        n = size // 4 if cols is None else cols
        return self.pool.ap()[:, off // 4: off // 4 + n]

    def bf16(self, name, cols=None):
        off, size = self.spans[name]
        n = size // 2 if cols is None else cols
        return self.pool.ap().bitcast(BF16)[:, off // 2: off // 2 + n]

    def build(self):
        nc = self.nc
        dram = lambda nm, sh, dt, kind="ExternalInput": nc.dram_tensor(nm, sh, dt, kind=kind).ap()
        s = self
        s.x_in = dram("x_in", [T, D], F32)
        s.Wq = dram("Wq", [D, D], F32); s.Wk = dram("Wk", [D, D], F32)
        s.Wv = dram("Wv", [D, D], F32); s.Wm = dram("Wm", [D, D], F32)
        s.bq = dram("bq", [128, NCH], F32); s.bk = dram("bk", [128, NCH], F32)
        s.bv = dram("bv", [1, D], F32); s.bm = dram("bm", [128, NCH], F32)
        s.wgT = dram("wgT", [D, INNER], F32); s.wuT = dram("wuT", [D, INNER], F32)
        s.wdT = dram("wdT", [INNER, D], F32)
        s.Wqkv = dram("Wqkv", [D, 3 * D], F32)
        s.Wout = dram("Wout", [D, D], F32); s.bout = dram("bout", [128, NCH], F32)
        s.w1T = dram("w1T", [D, INNER], F32); s.w2T = dram("w2T", [D, INNER], F32)
        s.h1T = dram("h1T", [INNER, INNER], F32); s.h2T = dram("h2T", [INNER, INNER], F32)
        s.w3T = dram("w3T", [INNER, D], F32)
        s.s1c = dram("s1c", [128, ICH], F32); s.s2c = dram("s2c", [128, ICH], F32)
        s.sh1c = dram("sh1c", [128, ICH], F32); s.sh2c = dram("sh2c", [128, ICH], F32)
        s.s3c = dram("s3c", [128, NCH], F32)
        s.identd = dram("ident", [128, 128], F32)
        s.identbd = dram("identb", [128, 128], BF16)
        s.triud = dram("triu", [128, 128], BF16)
        s.sel8d = dram("sel8", [16, 1024], F32)
        s.ident4d = dram("ident4", [128, 512], F32)
        s.out = dram("out", [T, D], F32, "ExternalOutput")
        s.dbg = dram("dbg", [128, 17408], BF16, "ExternalOutput") if s.dump else None
        s.iwg_s = dram("iwg_s", [D, INNER], BF16, "Internal")
        s.iwu_s = dram("iwu_s", [D, INNER], BF16, "Internal")
        s.iwd_s = dram("iwd_s", [INNER, D], BF16, "Internal")
        s.xpark = dram("xpark", [128, NCH * T], F32, "Internal")
        s.attn_s = dram("attn_s", [128, NCH * T], F32, "Internal")
        s.h2_s = dram("h2_s", [128, ICH * T], BF16, "Internal")
        s.h3_s = dram("h3_s", [128, ICH * T], BF16, "Internal")

        ctx = ExitStack()
        with ctx:
            self.POOLB = 204 * 1024
            self.pool = ctx.enter_context(nc.sbuf_tensor("pool", [128, self.POOLB // 4], F32))
            self.spans, self.cursor, self.watermark = {}, 0, 0
            self.psumbig = ctx.enter_context(nc.psum_tensor("pb", [128, 4096], F32))
            self.psum = [self.psumbig.ap()[:, i * 512:(i + 1) * 512]
                         for i in range(8)]
            for nm in ("ldh", "ldg", "ldgA", "ldgB", "ldgC", "st", "sv", "ss",
                       "stt", "sg", "pst", "pss", "psa", "wb", "lha", "lhb",
                       "vch", "sch", "tch", "gch"):
                self.sems[nm] = ctx.enter_context(nc.semaphore(nm))
                self.cnt[nm] = 0
            blk = ctx.enter_context(nc.Block())

            self.alloc("consts", 4096, at=R_CONSTS)
            self.alloc("rows", 24576, at=R_ROWS)
            self.alloc("x", T * NCH * 4, at=R_X)

            self.setup_consts()
            self.p0_load_x()
            if self.n_phases >= 1:
                self.p_ln(0)
            if self.n_phases >= 2:
                if self.n_phases >= 4:
                    self.prepass_gen = self.prepass_generator()
                self.p_l0_qkv()
            if self.n_phases >= 3:
                self.p_l0_attn()
            if self.n_phases >= 4:
                self.p_l0_ffn()
            if self.n_phases >= 5:
                self.p_ln(1)
            if self.n_phases >= 6:
                self.p_l1_attn()
            if self.n_phases >= 7:
                self.p_l1_ffn()
            self.p_store()

            for eng in ENGS:
                def make(eng):
                    def body(e):
                        for ent in self.ops[eng]:
                            if "wait" in ent:
                                nm, v = ent["wait"]
                                e.wait_ge(self.sems[nm], v)
                            else:
                                i = ent["fn"](e)
                                if ent["inc"] is not None:
                                    nm, n, _ = ent["inc"]
                                    i.then_inc(self.sems[nm], n)
                    return body
                getattr(blk, eng)(make(eng))
        return nc

    def xc(self, c):
        return self.f32("x")[:, c * T:(c + 1) * T]

    def setup_consts(self):
        s = self
        co = s.spans["consts"][0]
        p, pb = s.pool.ap(), s.pool.ap().bitcast(BF16)
        s.ident = p[:, co // 4: co // 4 + 128]
        s.identb = pb[:, (co + 512) // 2:][:, :128]
        s.triu = pb[:, (co + 768) // 2:][:, :128]
        s.ones_col = p[:, (co + 1024) // 4:][:, :1]
        s.ones_row = p[0:1, (co + 1028) // 4:][:, :128]
        s.onesb_col = pb[:, (co + 1540) // 2:][:, :1]
        s.ones128 = p[:, (co + 2048) // 4:][:, :128]
        s.halfcol = p[:, (co + 2616) // 4:][:, :1]
        ro = s.spans["rows"][0]

        def colf32(n):
            nonlocal ro
            ap = p[:, ro // 4: ro // 4 + n]
            ro += n * 4
            return ap
        s.bq_sb = colf32(NCH); s.bk_sb = colf32(NCH); s.bm_sb = colf32(NCH)
        s.bout_sb = colf32(NCH)
        s.s1_sb = colf32(ICH); s.s2_sb = colf32(ICH)
        s.sh1_sb = colf32(ICH); s.sh2_sb = colf32(ICH); s.s3_sb = colf32(NCH)
        s.bqs_sb = colf32(NCH)
        s.bv_row = p[0:1, ro // 4: ro // 4 + D]; ro += D * 4
        s.mrow_off = ro
        s.mrow = p[0:1, ro // 4: ro // 4 + T]; ro += T * 4
        s.rrow = p[0:1, ro // 4: ro // 4 + T]; ro += T * 4
        assert ro <= s.spans["rows"][0] + 24576

        s.ldh(s.ident, s.identd)
        s.ldh(s.identb, s.identbd)
        s.ldh(s.triu, s.triud)
        for dst, src in ((s.bq_sb, s.bq), (s.bk_sb, s.bk), (s.bm_sb, s.bm),
                         (s.bout_sb, s.bout), (s.s1_sb, s.s1c), (s.s2_sb, s.s2c),
                         (s.sh1_sb, s.sh1c), (s.sh2_sb, s.sh2c), (s.s3_sb, s.s3c),
                         (s.bv_row, s.bv)):
            s.ldh(dst, src)

    def transpose_128(self, src_f32, dst, kidx_sem_base, k, evac_eng="vector", bf=False):
        s = self
        bank = s.psum[k % 2]
        s.wait("tensor", "pst", kidx_sem_base + k - 1)
        ident = s.identb if bf else s.ident
        s.op("tensor", lambda e, b=bank, sr=src_f32, idn=ident:
             e.transpose(out=b[:, 0:128], in_=sr, identity=idn), inc="stt")
        s.wait(evac_eng, "stt", s.val("stt"))
        s.op(evac_eng, lambda e, b=bank, d=dst: e.tensor_copy(out=d, in_=b[:, 0:128]),
             inc="pst")

    def p0_load_x(self):
        s = self
        s.alloc("stg", T * NCH * 4, at=R_D)
        x_tm = s.f32("stg")
        s.ldh(x_tm, s.x_in.rearrange("(a p) d -> p a d", p=128))
        ld_all = s.val("ldh")
        s.wait("vector", "ldh", ld_all)
        s.op("vector", lambda e: e.memset(s.ones_col, 1.0))
        s.op("vector", lambda e: e.memset(s.ones_row, 1.0))
        s.op("vector", lambda e: e.memset(s.onesb_col, 1.0))
        s.op("vector", lambda e: e.memset(s.ones128, 1.0))
        s.op("vector", lambda e: e.memset(s.halfcol, 0.5))
        s.op("vector", lambda e: e.tensor_scalar(out=s.bqs_sb, in0=s.bq_sb,
             scalar1=0.125, scalar2=None, op0=ALU.mult), inc="sv")
        s.wait("tensor", "ldh", ld_all)
        base = s.val("pst")
        for a in range(16):
            for c in range(NCH):
                src = x_tm[:, a * D + c * 128: a * D + (c + 1) * 128]
                dst = s.xc(c)[:, a * 128:(a + 1) * 128]
                s.transpose_128(src, dst, base, a * NCH + c)
        self.x_loaded_pst = s.val("pst")

    def stats_mb_rb(self, mb_ap, rb_ap, sq0, sq1):
        s = self
        sq = [sq0, sq1]
        chunk_mark = []
        for c in range(NCH):
            buf = sq[c % 2]
            if c >= 2:
                s.wait("scalar", "stt", chunk_mark[c - 2])
            s.op("scalar", lambda e, c=c, b=buf: e.activation(out=b, in_=s.xc(c), func=AF.Square), inc="ss")
            s.wait("tensor", "ss", s.val("ss"))
            for j in range(NT):
                s.op("tensor", lambda e, c=c, j=j: e.matmul(
                    out=s.psum[j][0:1, 0:512],
                    lhsT=s.ones_col, rhs=s.xc(c)[:, j * 512:(j + 1) * 512],
                    start=(c == 0), stop=(c == NCH - 1)))
            for j in range(NT):
                s.op("tensor", lambda e, c=c, j=j, b=buf: e.matmul(
                    out=s.psum[4 + j][0:1, 0:512],
                    lhsT=s.ones_col, rhs=b[:, j * 512:(j + 1) * 512],
                    start=(c == 0), stop=(c == NCH - 1)),
                    inc="stt" if j == NT - 1 else None)
            chunk_mark.append(s.val("stt"))
        tstats = s.val("stt")
        s.wait("vector", "stt", tstats)
        for j in range(NT):
            s.op("vector", lambda e, j=j: e.tensor_scalar(
                out=s.mrow[:, j * 512:(j + 1) * 512], in0=s.psum[j][0:1, 0:512],
                scalar1=1.0 / D, scalar2=None, op0=ALU.mult))
            s.op("vector", lambda e, j=j: e.tensor_scalar(
                out=s.rrow[:, j * 512:(j + 1) * 512], in0=s.psum[4 + j][0:1, 0:512],
                scalar1=1.0 / D, scalar2=None, op0=ALU.mult))
        s.chain("vector")
        s.op("vector", lambda e: e.tensor_tensor(
            out=sq[0][0:1, 0:T], in0=s.mrow, in1=s.mrow, op=ALU.mult))
        s.chain("vector")
        s.op("vector", lambda e: e.tensor_tensor(
            out=s.rrow, in0=s.rrow, in1=sq[0][0:1, 0:T], op=ALU.subtract), inc="sv")
        s.chain("vector")
        s.op("vector", lambda e: e.tensor_scalar(out=s.rrow, in0=s.rrow,
             scalar1=float(LNEPS), scalar2=None, op0=ALU.add), inc="sv")
        s.wait("scalar", "sv", s.val("sv"))
        s.op("scalar", lambda e: e.activation(out=s.rrow, in_=s.rrow, func=AF.Sqrt), inc="ss")
        s.wait("vector", "ss", s.val("ss"))
        s.op("vector", lambda e: e.reciprocal(out=s.rrow, in_=s.rrow), inc="sv")
        svr = s.val("sv")
        s.wait("tensor", "sv", svr)
        for j in range(NT):
            bm_, br_ = s.psum[j % 2], s.psum[2 + j % 2]
            s.op("tensor", lambda e, j=j, b=bm_: e.matmul(
                out=b[:, 0:512], lhsT=s.ones_row,
                rhs=s.mrow[:, j * 512:(j + 1) * 512], start=True, stop=True))
            s.op("tensor", lambda e, j=j, b=br_: e.matmul(
                out=b[:, 0:512], lhsT=s.ones_row,
                rhs=s.rrow[:, j * 512:(j + 1) * 512], start=True, stop=True), inc="stt")
            s.wait("vector", "stt", s.val("stt"))
            s.op("vector", lambda e, j=j, b=bm_: e.tensor_copy(
                out=mb_ap[:, j * 512:(j + 1) * 512], in_=b[:, 0:512]))
            s.op("vector", lambda e, j=j, b=br_: e.tensor_copy(
                out=rb_ap[:, j * 512:(j + 1) * 512], in_=b[:, 0:512]), inc="pst")
            s.wait("tensor", "pst", s.val("pst"))

    def round_pair(self, lanes):
        s = self
        tm = s.chain("vector")
        s.wait("scalar", tm[0], tm[1])
        smarks, vmarks = {}, {}
        for li, (t, v, d, o) in enumerate(lanes):
            s.op("scalar", lambda e, t=t, d=d: e.activation(
                out=d, in_=t, func=AF.Sign, bias=s.halfcol, scale=1.0),
                inc="ss")
            smarks[li] = s.val("ss")
        for li, (t, v, d, o) in enumerate(lanes):
            s.wait("vector", "ss", smarks[li])
            s.op("vector", lambda e, t=t, v=v, d=d: e.scalar_tensor_tensor(
                out=v, in0=d, scalar=-64.0, in1=t, op0=ALU.mult, op1=ALU.add),
                inc="vch")
            vmarks[li] = s.val("vch")
        for k in range(6, -1, -1):
            for li, (t, v, d, o) in enumerate(lanes):
                s.wait("scalar", "vch", vmarks[li])
                s.op("scalar", lambda e, v=v, d=d: e.activation(
                    out=d, in_=v, func=AF.Sign, bias=s.halfcol, scale=1.0),
                    inc="ss")
                smarks[li] = s.val("ss")
            for li, (t, v, d, o) in enumerate(lanes):
                s.wait("vector", "ss", smarks[li])
                s.op("vector", lambda e, v=v, d=d, k=k: e.scalar_tensor_tensor(
                    out=v, in0=d, scalar=-float(2.0 ** (k - 1)), in1=v,
                    op0=ALU.mult, op1=ALU.add), inc="vch")
                vmarks[li] = s.val("vch")
        for li, (t, v, d, o) in enumerate(lanes):
            s.wait("vector", "vch", vmarks[li])
            s.op("vector", lambda e, t=t, v=v, o=o: e.scalar_tensor_tensor(
                out=o, in0=t, scalar=-0.5, in1=v, op0=ALU.add,
                op1=ALU.subtract), inc="sv")
        s.chain("vector")

    def p_ln(self, layer):
        s = self
        s.phase_barrier()
        s.alloc("mb", T * 4, at=R_D)
        s.alloc("rb", T * 4, at=R_D + 8 * KB)
        s.alloc("sq0", T * 4, at=R_D + 16 * KB)
        s.alloc("sq1", T * 4, at=R_D + 24 * KB)
        s.alloc("xn", T * NCH * 2, at=R_D + 34 * KB)
        s.xn_ap, s.mb_ap, s.rb_ap = s.bf16("xn"), s.f32("mb"), s.f32("rb")
        sq = [s.f32("sq0"), s.f32("sq1")]


        chunk_mark = []
        for c in range(NCH):
            buf = sq[c % 2]
            if c >= 2:
                s.wait("scalar", "stt", chunk_mark[c - 2])
            s.op("scalar", lambda e, c=c, b=buf: e.activation(out=b, in_=s.xc(c), func=AF.Square), inc="ss")
            s.wait("tensor", "ss", s.val("ss"))
            for j in range(NT):
                s.op("tensor", lambda e, c=c, j=j: e.matmul(
                    out=s.psum[j][0:1, 0:512],
                    lhsT=s.ones_col, rhs=s.xc(c)[:, j * 512:(j + 1) * 512],
                    start=(c == 0), stop=(c == NCH - 1)))
            for j in range(NT):
                s.op("tensor", lambda e, c=c, j=j, b=buf: e.matmul(
                    out=s.psum[4 + j][0:1, 0:512],
                    lhsT=s.ones_col, rhs=b[:, j * 512:(j + 1) * 512],
                    start=(c == 0), stop=(c == NCH - 1)),
                    inc="stt" if j == NT - 1 else None)
            chunk_mark.append(s.val("stt"))
        tstats = s.val("stt")

        s.wait("vector", "stt", tstats)
        for j in range(NT):
            s.op("vector", lambda e, j=j: e.tensor_scalar(
                out=s.mrow[:, j * 512:(j + 1) * 512], in0=s.psum[j][0:1, 0:512],
                scalar1=1.0 / D, scalar2=None, op0=ALU.mult))
            s.op("vector", lambda e, j=j: e.tensor_scalar(
                out=s.rrow[:, j * 512:(j + 1) * 512], in0=s.psum[4 + j][0:1, 0:512],
                scalar1=1.0 / D, scalar2=None, op0=ALU.mult))
        s.chain("vector")
        s.op("vector", lambda e: e.tensor_tensor(
            out=sq[0][0:1, 0:T], in0=s.mrow, in1=s.mrow, op=ALU.mult))
        s.chain("vector")
        s.op("vector", lambda e: e.tensor_tensor(
            out=s.rrow, in0=s.rrow, in1=sq[0][0:1, 0:T], op=ALU.subtract), inc="sv")
        s.chain("vector")
        s.op("vector", lambda e: e.tensor_scalar(out=s.rrow, in0=s.rrow,
             scalar1=float(LNEPS), scalar2=None, op0=ALU.add), inc="sv")
        s.wait("scalar", "sv", s.val("sv"))
        s.op("scalar", lambda e: e.activation(out=s.rrow, in_=s.rrow, func=AF.Sqrt), inc="ss")
        s.wait("vector", "ss", s.val("ss"))
        s.op("vector", lambda e: e.reciprocal(out=s.rrow, in_=s.rrow), inc="sv")
        svr = s.val("sv")
        s.wait("tensor", "sv", svr)
        for j in range(NT):
            bm_, br_ = s.psum[j % 2], s.psum[2 + j % 2]
            s.op("tensor", lambda e, j=j, b=bm_: e.matmul(
                out=b[:, 0:512], lhsT=s.ones_row,
                rhs=s.mrow[:, j * 512:(j + 1) * 512], start=True, stop=True))
            s.op("tensor", lambda e, j=j, b=br_: e.matmul(
                out=b[:, 0:512], lhsT=s.ones_row,
                rhs=s.rrow[:, j * 512:(j + 1) * 512], start=True, stop=True), inc="stt")
            s.wait("vector", "stt", s.val("stt"))
            s.op("vector", lambda e, j=j, b=bm_: e.tensor_copy(
                out=s.mb_ap[:, j * 512:(j + 1) * 512], in_=b[:, 0:512]))
            s.op("vector", lambda e, j=j, b=br_: e.tensor_copy(
                out=s.rb_ap[:, j * 512:(j + 1) * 512], in_=b[:, 0:512]), inc="pst")
            if j >= 1:
                s.wait("tensor", "pst", s.val("pst") - 1)
        s.chain("vector")
        for c in range(NCH):
            buf = sq[c % 2]
            s.op("vector", lambda e, c=c, b=buf: e.tensor_tensor(
                out=b, in0=s.xc(c), in1=s.mb_ap, op=ALU.subtract))
            s.chain("vector")
            s.op("vector", lambda e, c=c, b=buf: e.tensor_tensor(
                out=s.xn_ap[:, c * T:(c + 1) * T], in0=b, in1=s.rb_ap,
                op=ALU.mult), inc="sv")
            s.chain("vector")
        s.ln_done_sv = s.val("sv")
        if s.dump == f"xn{layer}":
            s.wait("sync", "sv", s.ln_done_sv)
            s.sth(s.dbg, s.xn_ap)

    def p_l0_qkv(self):
        s = self
        s.phase_barrier()
        s.sth(s.xpark, s.f32("x"))
        s.x_parked = True
        park_v = s.val("st")
        s.wait("scalar", "st", park_v)
        s.wait("vector", "st", park_v)
        s.alloc("q", T * NCH * 2, at=R_X)
        s.alloc("k", T * NCH * 2, at=R_X + 32 * KB)
        s.alloc("v", 16 * 1040 * 2, at=R_D)
        s.alloc("wbufA", D * NCH * 2, at=R_E + 2 * KB)
        s.alloc("wbufB", D * NCH * 2, at=R_E + 18 * KB)
        s.alloc("bvb", D * 4, at=R_E + 34 * KB)
        q, k, v = s.bf16("q"), s.bf16("k"), s.bf16("v")
        s.q_ap, s.k_ap, s.v_ap = q, k, v
        wA, wB = s.bf16("wbufA"), s.bf16("wbufB")
        bvb = s.f32("bvb")

        s.ldg(wA, s.Wq.rearrange("(c p) o -> p c o", p=128), sem="ldgA")
        ldq = ("ldgA", s.val("ldgA"))
        s.ldg(wB, s.Wk.rearrange("(c p) o -> p c o", p=128), sem="ldgB")
        ldk = ("ldgB", s.val("ldgB"))

        for half in range(2):
            b = s.psum[6 + half]
            s.op("tensor", lambda e, h=half, b=b: e.matmul(
                out=b[:, 0:512], lhsT=s.ones_row, rhs=s.bv_row[:, h * 512:(h + 1) * 512],
                start=True, stop=True), inc="stt")
        s.wait("vector", "stt", s.val("stt"))
        for half in range(2):
            s.op("vector", lambda e, h=half: e.tensor_copy(
                out=bvb[:, h * 512:(h + 1) * 512], in_=s.psum[6 + h][:, 0:512]), inc="pst")
        s.op("vector", lambda e: e.memset(v[:, 0:16 * 1040], 1.0))
        s.chain("vector")
        bvb_pst = s.val("pst")

        def proj_fm(dst, wbuf, bias_col, scale, ld_mark, eng):
            s.wait("tensor", ld_mark[0], ld_mark[1])
            pvbase = s.val("pss")
            kidx = 0
            for oc in range(NCH):
                for j in range(NT):
                    bank = s.psum[kidx % 2]
                    s.wait("tensor", "pss", pvbase + kidx - 1)
                    for kc in range(NCH):
                        s.op("tensor", lambda e, oc=oc, j=j, kc=kc, b=bank, w=wbuf: e.matmul(
                            out=b[:, 0:512],
                            lhsT=w[:, kc * D + oc * 128:kc * D + (oc + 1) * 128],
                            rhs=s.xn_ap[:, kc * T + j * 512:kc * T + (j + 1) * 512],
                            start=(kc == 0), stop=(kc == NCH - 1)),
                            inc="stt" if kc == NCH - 1 else None)
                    s.wait(eng, "stt", s.val("stt"))
                    if eng == "scalar":
                        s.op("scalar", lambda e, oc=oc, j=j, b=bank, bc=bias_col, sc=scale: e.activation(
                            out=dst[:, oc * T + j * 512:oc * T + (j + 1) * 512],
                            in_=b[:, 0:512], func=AF.Identity,
                            bias=bc[:, oc:oc + 1], scale=sc), inc="pss")
                    for _ in range(2):
                        if s.prepass_gen is None or s.qkv_budget <= 0:
                            break
                        s.qkv_budget -= 1
                        if next(s.prepass_gen, "END") == "END":
                            s.prepass_gen = None
                    kidx += 1
            return s.val("pss")

        proj_fm(q, wA, s.bqs_sb, 0.125, ldq, "scalar")
        mark = s.chain("tensor")
        s.wait("gpsimd", mark[0], mark[1])
        s.ldg(wA, s.Wv.rearrange("(c p) o -> p c o", p=128), sem="ldgA")
        ldv = ("ldgA", s.val("ldgA"))
        proj_fm(k, wB, s.bk_sb, 1.0, ldk, "scalar")

        s.wait("tensor", ldv[0], ldv[1])
        s.wait("vector", "pst", bvb_pst)
        pvbase = s.val("pst")
        kidx = 0
        for a in range(16):
            for half in range(2):
                bank = s.psum[2 + kidx % 2]
                s.wait("tensor", "pst", pvbase + kidx - 1)
                for kc in range(NCH):
                    s.op("tensor", lambda e, a=a, hf=half, kc=kc, b=bank: e.matmul(
                        out=b[:, 0:512],
                        lhsT=s.xn_ap[:, kc * T + a * 128:kc * T + (a + 1) * 128],
                        rhs=wA[:, kc * D + hf * 512:kc * D + (hf + 1) * 512],
                        start=(kc == 0), stop=(kc == NCH - 1)),
                        inc="stt" if kc == NCH - 1 else None)
                s.wait("vector", "stt", s.val("stt"))
                for hh in range(8):
                    head = half * 8 + hh
                    s.op("vector", lambda e, a=a, hf=half, hh=hh, head=head, b=bank: e.tensor_tensor(
                        out=v[:, a * 1040 + 65 * head: a * 1040 + 65 * head + 64],
                        in0=b[:, hh * 64:(hh + 1) * 64],
                        in1=bvb[:, hf * 512 + hh * 64: hf * 512 + (hh + 1) * 64],
                        op=ALU.add), inc="pst" if hh == 7 else None)
                kidx += 1
        s.qkv_done = s.val("pst")
        if s.dump in ("q0", "k0", "v0"):
            tgt = {"q0": q, "k0": k, "v0": v}[s.dump]
            s.wait("sync", "pst", s.val("pst"))
            s.wait("sync", "pss", s.val("pss"))
            s.sth(s.dbg.rearrange("p n -> p n")[:, 0:tgt.shape[1]], tgt)


    def p_l0_attn(self):
        s = self
        s.phase_barrier()
        s.alloc("wm", D * NCH * 2, at=R_D + 33 * KB)
        s.alloc("probsA", T * 2, at=R_D + 49 * KB)
        s.alloc("probsB", T * 2, at=R_D + 56 * KB)
        s.alloc("rc", 4, at=R_D + 53 * KB)
        s.alloc("atm", 1024 * 2, at=R_D + 54 * KB)
        s.alloc("afm", 512 * NCH * 2, at=R_E + 2 * KB)
        s.alloc("ost", 8 * 512 * 4, at=R_E + 10 * KB)
        q, k, v = s.q_ap, s.k_ap, s.v_ap
        wm = s.bf16("wm")
        probs2 = [s.bf16("probsA"), s.bf16("probsB")]
        rc = s.f32("rc")
        atm = s.bf16("atm")
        afm = s.bf16("afm")
        ost = s.f32("ost")

        s.ldg(wm, s.Wm.rearrange("(c p) o -> p c o", p=128), sem="ldgB")
        s.wait("tensor", "ldgB")

        def qh(h, cols):
            return q[:, (h // 2) * T:][:, cols][64 * (h % 2):64 * (h % 2) + 64, :]

        def kh(h, cols):
            return k[:, (h // 2) * T:][:, cols][64 * (h % 2):64 * (h % 2) + 64, :]

        exp_marks = {}
        pv_marks = []
        pv_stt = {}
        sv_marks = {}
        for qt in range(16):
            nblk = qt + 1
            s.wait("tensor", "pst", s.val("pst"))
            pend = None
            for h in range(H + 1):
                if h < H:
                    probsT = probs2[h % 2]
                    half1 = min(nblk, 8)
                    for hf in range(2 if nblk > 8 else 1):
                        b0, b1 = (0, half1) if hf == 0 else (8, nblk)
                        if exp_marks.get(hf) is not None:
                            s.wait("tensor", exp_marks[hf][0], exp_marks[hf][1])
                        for kt in range(b0, b1):
                            bank = s.psum[2 + kt // 4]
                            st_ = (kt % 4 == 0)
                            sp_ = (kt % 4 == 3 or kt == nblk - 1)
                            s.op("tensor", lambda e, h=h, qt=qt, kt=kt, b=bank,
                                 st_=st_, sp_=sp_: e.matmul(
                                out=b[:, (kt % 4) * 128:(kt % 4) * 128 + 128],
                                lhsT=kh(h, slice(kt * 128, (kt + 1) * 128)),
                                rhs=qh(h, slice(qt * 128, (qt + 1) * 128)),
                                start=st_, stop=sp_),
                                inc="stt" if kt == b1 - 1 else None)
                        s.wait("scalar", "stt", s.val("stt"))
                        if hf == 0 and pv_stt.get(h % 2) is not None:
                            s.wait("scalar", "stt", pv_stt[h % 2])
                        w0, wq = b0 * 128, (b1 - b0) * 128
                        s.op("scalar", lambda e, w0=w0, wq=wq, pt=probsT: e.activation(
                            out=pt[:, w0:w0 + wq],
                            in_=s.psumbig.ap()[:, 1024 + w0:1024 + w0 + wq],
                            func=AF.Exp), inc="pss")
                        exp_marks[hf] = ("pss", s.val("pss"))
                    s.wait("vector", "pss", s.val("pss"))
                    s.op("vector", lambda e, qt=qt, pt=probsT: e.tensor_tensor(
                        out=pt[:, qt * 128:(qt + 1) * 128],
                        in0=pt[:, qt * 128:(qt + 1) * 128], in1=s.triu,
                        op=ALU.mult), inc="sv")
                    sv_marks[h] = ("sv", s.val("sv"))
                if pend is not None:
                    ph, pprobs = pend
                    s.wait("tensor", sv_marks[ph][0], sv_marks[ph][1])
                    pvb = s.psum[6 + ph % 2]
                    if len(pv_marks) >= 2:
                        s.wait("tensor", "pst", pv_marks[-2])
                    for kt in range(nblk):
                        sp_ = (kt == nblk - 1)
                        s.op("tensor", lambda e, ph=ph, kt=kt, b=pvb, sp_=sp_,
                             pt=pprobs: e.matmul(
                            out=b[:, 0:65],
                            lhsT=pt[:, kt * 128:(kt + 1) * 128],
                            rhs=v[:, kt * 1040 + 65 * ph: kt * 1040 + 65 * ph + 65],
                            start=(kt == 0), stop=sp_),
                            inc="stt" if sp_ else None)
                    pv_stt[ph % 2] = s.val("stt")
                    s.wait("vector", "stt", s.val("stt"))
                    s.wait("vector", "pst", s.val("pst"))
                    s.op("vector", lambda e, b=pvb: e.reciprocal(out=rc, in_=b[:, 64:65]))
                    s.chain("vector")
                    s.op("vector", lambda e, ph=ph, b=pvb: e.tensor_scalar(
                        out=atm[:, ph * 64:(ph + 1) * 64], in0=b[:, 0:64],
                        scalar1=rc, scalar2=None, op0=ALU.mult), inc="pst")
                    pv_marks.append(s.val("pst"))
                    if s.prepass_gen is not None:
                        if next(s.prepass_gen, "END") == "END":
                            s.prepass_gen = None
                        if s.bg_pst_mark is not None:
                            pv_marks += [s.bg_pst_mark, s.bg_pst_mark]
                            s.bg_pst_mark = None
                pend = (h, probs2[h % 2]) if h < H else None

            s.chain("vector")
            for pc in range(NCH):
                bank = s.psum[pc % 2]
                bb = bank.bitcast(BF16)
                if pc >= 2:
                    s.wait("tensor", "pst", s.val("pst") - 1)
                s.op("tensor", lambda e, pc=pc, b=bb: e.transpose(
                    out=b[:, 0:128], in_=atm[:, pc * 128:(pc + 1) * 128],
                    identity=s.identb), inc="stt")
                s.wait("vector", "stt", s.val("stt"))
                s.op("vector", lambda e, pc=pc, qt=qt, b=bb: e.tensor_copy(
                    out=afm[:, pc * 512 + (qt % 4) * 128:
                            pc * 512 + (qt % 4) * 128 + 128],
                    in_=b[:, 0:128]), inc="pst")

            if qt % 4 == 3:
                ntg = qt // 4
                s.chain("vector")
                s.wait("vector", "st", s.val("st"))
                s.wait("tensor", "pst", s.val("pst"))
                for oc in range(NCH):
                    bank = s.psum[oc]
                    for kc in range(NCH):
                        s.op("tensor", lambda e, oc=oc, kc=kc, b=bank: e.matmul(
                            out=b[:, 0:512],
                            lhsT=wm[:, kc * D + oc * 128:kc * D + (oc + 1) * 128],
                            rhs=afm[:, kc * 512:(kc + 1) * 512],
                            start=(kc == 0), stop=(kc == NCH - 1)),
                            inc="stt" if kc == NCH - 1 else None)
                    s.wait("vector", "stt", s.val("stt"))
                    s.op("vector", lambda e, oc=oc, b=bank: e.tensor_scalar(
                        out=ost[:, oc * 512:(oc + 1) * 512], in0=b[:, 0:512],
                        scalar1=s.bm_sb[:, oc:oc + 1], scalar2=None,
                        op0=ALU.add), inc="pst")
                s.wait("sync", "pst", s.val("pst"))
                s.sth(s.attn_s.rearrange("p (c t) -> p c t", c=NCH)[:, :, ntg * 512:(ntg + 1) * 512],
                      ost.rearrange("p (c t) -> p c t", c=NCH))
                s.wait("tensor", "pst", s.val("pst"))
                exp_marks = {}
                s.wait("tensor", "pss", s.val("pss"))
        s.attn_stores = s.val("st")
        s.pending_adds.append(s.attn_s)

    def prepass_generator(self):
        s = self
        p = s.pool.ap()
        co = s.spans["consts"][0]
        base = co + 2560
        thr_cols, nthr_cols, mw_cols = [], [], []
        for i in range(3):
            thr_cols.append(p[:, (base + i * 12) // 4:][:, :1])
            nthr_cols.append(p[:, (base + i * 12 + 4) // 4:][:, :1])
            mw_cols.append(p[:, (base + i * 12 + 8) // 4:][:, :1])
        accc = p[:, (base + 40) // 4:][:, :1]
        cell = p[0:1, (base + 44) // 4:][:, :1]
        cell2 = p[0:1, (base + 48) // 4:][:, :1]
        redc = p[:, (base + 52) // 4:][:, :1]
        s.thr_cols, s.nthr_cols, s.mw_cols = thr_cols, nthr_cols, mw_cols
        mroff = s.mrow_off
        ws = [p[:, (mroff + i * 4 * KB) // 4:][:, :1024] for i in range(3)]
        gA = p[:, (198 * KB) // 4:][:, :1024]
        pb = s.pool.ap().bitcast(BF16)
        iqb = [pb[:, (194 * KB) // 2:][:, :1024], pb[:, (202 * KB) // 2:][:, :1024]]
        sems3 = ("ldgA", "ldgB", "ldgC")

        accc_w = [accc, p[:, (base + 60) // 4:][:, :1], p[:, (base + 64) // 4:][:, :1]]
        weights = [(s.wgT, s.iwg_s, 8, 4, D * INNER),
                   (s.wuT, s.iwu_s, 8, 4, D * INNER),
                   (s.wdT, s.iwd_s, 32, 1, D * INNER)]
        chunks = []
        for ps in (1, 2):
            for wi in range(3):
                nn, no = weights[wi][2], weights[wi][3]
                for n in range(nn):
                    for o in range(no):
                        chunks.append((wi, n, o))
        NLC = len(chunks)

        def chunk_ap(t, wi, n, o):
            r = t.rearrange("(n p) o -> p n o", p=128)
            return r[:, n, :][:, o * 1024:(o + 1) * 1024]

        state = {"issued": 0}
        load_marks = [None] * NLC
        cons_marks = [None, None, None]
        st_marks = [None, None]

        def issue_one():
            i = state["issued"]
            if i >= NLC:
                return
            b = i % 3
            wi, n, o = chunks[i]
            if cons_marks[b] is not None:
                s.wait("gpsimd", cons_marks[b][0], cons_marks[b][1])
            pv = s.val(sems3[b])
            if pv > 0:
                s.wait("gpsimd", sems3[b], pv)
            s.ldg(ws[b], chunk_ap(weights[wi][0], wi, n, o), sem=sems3[b])
            load_marks[i] = (sems3[b], s.val(sems3[b]))
            state["issued"] = i + 1

        consumed = 0
        for wi in range(3):
            wsrc, wdst, nn, no, nel = weights[wi]
            acw = accc_w[wi]
            for ci in range(nn * no):
                while state["issued"] - consumed < 3 and state["issued"] < NLC:
                    issue_one()
                b = consumed % 3
                m = load_marks[consumed]
                s.wait("vector", m[0], m[1])
                if ci == 0:
                    s.op("vector", lambda e, bb=ws[b], acw=acw: e.tensor_reduce(
                        out=acw, in_=bb, axis=AX.X, op=ALU.add,
                        apply_absolute_value=True))
                else:
                    s.op("vector", lambda e, bb=ws[b]: e.tensor_reduce(
                        out=redc, in_=bb, axis=AX.X, op=ALU.add,
                        apply_absolute_value=True))
                    s.chain("vector")
                    s.op("vector", lambda e, acw=acw: e.tensor_tensor(
                        out=acw, in0=acw, in1=redc, op=ALU.add))
                cons_marks[b] = s.chain("vector")
                consumed += 1
                yield
        for wi in range(3):
            wsrc, wdst, nn, no, nel = weights[wi]
            acw = accc_w[wi]
            s.wait("tensor", "vch", s.val("vch"))
            s.wait("tensor", "pst", s.val("pst"))
            s.op("tensor", lambda e, acw=acw: e.matmul(
                out=s.psum[7][0:1, 0:1], lhsT=acw, rhs=s.ones_col,
                start=True, stop=True), inc="stt")
            s.wait("vector", "stt", s.val("stt"))
            s.op("vector", lambda e, n=nel: e.tensor_scalar(
                out=cell, in0=s.psum[7][0:1, 0:1], scalar1=1.0 / n,
                scalar2=1e-5, op0=ALU.mult, op1=ALU.max))
            s.chain("vector")
            s.op("vector", lambda e: e.tensor_scalar(
                out=cell2, in0=cell, scalar1=0.5, scalar2=None,
                op0=ALU.mult), inc="sv")
            s.wait("tensor", "sv", s.val("sv"))
            s.op("tensor", lambda e: e.matmul(
                out=s.psum[7][:, 1:2], lhsT=s.ones_row, rhs=cell2,
                start=True, stop=True))
            s.op("tensor", lambda e: e.matmul(
                out=s.psum[7][:, 2:3], lhsT=s.ones_row, rhs=cell,
                start=True, stop=True), inc="stt")
            s.wait("vector", "stt", s.val("stt"))
            s.op("vector", lambda e, wi=wi: e.tensor_copy(
                out=thr_cols[wi], in_=s.psum[7][:, 1:2]))
            s.op("vector", lambda e, wi=wi: e.tensor_copy(
                out=mw_cols[wi], in_=s.psum[7][:, 2:3]))
            s.chain("vector")
            s.op("vector", lambda e, wi=wi: e.tensor_scalar(
                out=nthr_cols[wi], in0=thr_cols[wi], scalar1=-1.0,
                scalar2=None, op0=ALU.mult), inc="pst")
            s.bg_pst_mark = s.val("pst")
            nthr_mark = ("pst", s.val("pst"))
            yield
            for ci in range(nn * no):
                while state["issued"] - consumed < 3 and state["issued"] < NLC:
                    issue_one()
                b = consumed % 3
                qb = consumed % 2
                wi2, n, o = chunks[consumed]
                m = load_marks[consumed]
                for hf in range(2):
                    cs = slice(hf * 512, hf * 512 + 512)
                    s.wait("vector", m[0], m[1])
                    if ci == 0 and hf == 0:
                        s.wait("vector", nthr_mark[0], nthr_mark[1])
                    if hf == 0 and st_marks[qb] is not None:
                        s.wait("vector", "st", st_marks[qb])
                    s.op("vector", lambda e, bb=ws[b], wi=wi, cs=cs: e.tensor_scalar(
                        out=gA[:, cs], in0=bb[:, cs], scalar1=nthr_cols[wi],
                        scalar2=None, op0=ALU.is_lt))
                    s.chain("vector")
                    s.op("vector", lambda e, bb=ws[b], wi=wi, q=qb, cs=cs:
                         e.scalar_tensor_tensor(
                            out=iqb[q][:, cs], in0=bb[:, cs], scalar=thr_cols[wi],
                            in1=gA[:, cs], op0=ALU.is_gt, op1=ALU.subtract))
                    cmk = s.chain("vector")
                    if hf == 1:
                        cons_marks[b] = cmk
                        s.wait("sync", cmk[0], cmk[1])
                        s.sth(chunk_ap(wdst, wi, n, o), iqb[qb])
                        st_marks[qb] = s.val("st")
                        consumed += 1
                    yield

    def p_l0_ffn(self):
        s = self
        s.phase_barrier()
        s.alloc("mb2", T * 4, at=R_D)
        s.alloc("rb2", T * 4, at=R_D + 8 * KB)
        s.alloc("oscx", 512 * 4, at=R_D + 16 * KB)
        s.alloc("scx", 512 * 4, at=R_D + 18 * KB)
        s.alloc("ixs", 8 * 512 * 2, at=R_D + 20 * KB)
        s.alloc("am", 512 * 4, at=R_D + 28 * KB)
        s.alloc("cm", 512 * 4, at=R_D + 30 * KB)
        s.alloc("hbuf", 32 * 512 * 4, at=R_D + 32 * KB)
        s.alloc("scb", 512 * 4, at=188 * KB)
        s.alloc("oscb", 512 * 4, at=190 * KB)
        s.alloc("vt", 1024 * 4, at=192 * KB)
        s.alloc("dt", 1024 * 4, at=196 * KB)
        s.alloc("ihg", 1024 * 2, at=200 * KB)
        s.alloc("rep0", 512, at=202 * KB)
        s.alloc("rep1", 512, at=202 * KB + 512)
        s.alloc("sccols", 64, at=203 * KB)
        mb, rb = s.f32("mb2"), s.f32("rb2")
        oscx, scx = s.f32("oscx"), s.f32("scx")
        ixs = s.bf16("ixs")
        am, cm = s.f32("am"), s.f32("cm")
        hb = s.f32("hbuf")
        scb, oscb = s.f32("scb"), s.f32("oscb")
        vt, dt = s.f32("vt"), s.f32("dt")
        ihg = s.bf16("ihg")
        s.rep_aps = [s.f32("rep0"), s.f32("rep1")]
        s.dbg_sb_tmp = s.bf16("ixs") if False else s.pool.ap().bitcast(BF16)[:, (R_D + 20 * KB) // 2:][:, :16 * 512]
        sc_span = s.f32("sccols")
        s.sccol = sc_span[:, 0:1]
        s.sccol2 = sc_span[:, 1:2]
        p = s.pool.ap()
        mroff = s.mrow_off
        gbuf = [s.pool.ap().bitcast(BF16)[:, (mroff + i * 2048) // 2:][:, :1024]
                for i in range(4)]
        dbuf = [s.pool.ap().bitcast(BF16)[:, (mroff + i * 8192) // 2:][:, :4096]
                for i in range(2)]

        s.ldh(s.f32("x"), s.xpark)
        v = s.val("ldh")
        s.wait("vector", "ldh", v)
        s.wait("scalar", "ldh", v)
        s.wait("tensor", "ldh", v)
        s.x_parked = False

        s.stats_mb_rb(mb, rb, hb[:, 0:T], hb[:, T:2 * T])

        if s.prepass_gen is not None:
            for _ in s.prepass_gen:
                pass
            s.prepass_gen = None
        for sm in ("ldgA", "ldgB", "ldgC", "st"):
            v = s.val(sm)
            s.wait("vector", sm, v)
            s.wait("scalar", sm, v)
            s.wait("sync", sm, v)
            s.wait("tensor", sm, v)
        s.wait("tensor", "pst", s.val("pst"))
        mw_cols = s.mw_cols

        evac_marks = []
        for ti in range(NT):
            tsl = slice(ti * 512, (ti + 1) * 512)
            s.chain("vector")
            for c in range(NCH):
                st_ap = hb[:, c * 512:(c + 1) * 512]
                s.op("vector", lambda e, c=c, a=st_ap, tsl=tsl: e.tensor_tensor(
                    out=a, in0=s.xc(c)[:, tsl], in1=mb[:, tsl], op=ALU.subtract))
                s.chain("vector")
                amn = dt[:, 0:512]
                if c == 0:
                    s.op("vector", lambda e, a=st_ap: e.tensor_copy(out=am, in_=a))
                    s.chain("vector")
                    s.op("vector", lambda e, a=st_ap, amn=amn: e.tensor_copy(out=amn, in_=a))
                else:
                    s.op("vector", lambda e, a=st_ap: e.tensor_tensor(
                        out=am, in0=am, in1=a, op=ALU.max))
                    s.chain("vector")
                    s.op("vector", lambda e, a=st_ap, amn=amn: e.tensor_tensor(
                        out=amn, in0=amn, in1=a, op=ALU.min))
                s.chain("vector")
            amn = dt[:, 0:512]
            s.op("vector", lambda e, amn=amn: e.tensor_scalar(
                out=amn, in0=amn, scalar1=-1.0, scalar2=None, op0=ALU.mult))
            s.chain("vector")
            s.op("vector", lambda e, amn=amn: e.tensor_tensor(
                out=am, in0=am, in1=amn, op=ALU.max))
            amk = s.chain("vector")
            self.scales_from_colmax(am, scx, oscx, amk)
            s.op("vector", lambda e, tsl=tsl: e.tensor_tensor(
                out=oscx, in0=oscx, in1=rb[:, tsl], op=ALU.mult))
            s.chain("vector")
            s.op("vector", lambda e: e.tensor_scalar(
                out=oscx, in0=oscx, scalar1=1e-5, scalar2=1.0 / 127.0,
                op0=ALU.max, op1=ALU.mult))
            s.chain("vector")
            dAx = s.pool.ap().bitcast(BF16)[:, (R_D + 48 * KB) // 2:][:, :1024]
            dBx = s.pool.ap().bitcast(BF16)[:, (R_D + 50 * KB) // 2:][:, :1024]
            for pr in range(2):
                for c in range(4 * pr, 4 * pr + 4):
                    a = hb[:, c * 512:(c + 1) * 512]
                    s.op("vector", lambda e, c=c, a=a: e.tensor_tensor(
                        out=a, in0=a, in1=scx, op=ALU.mult))
                    s.chain("vector")
                g0, g1 = 2 * pr, 2 * pr + 1
                s.round_pair([
                    (hb[:, g0 * 1024:(g0 + 1) * 1024], vt, dAx,
                     ixs[:, g0 * 1024:(g0 + 1) * 1024]),
                    (hb[:, g1 * 1024:(g1 + 1) * 1024], dt, dBx,
                     ixs[:, g1 * 1024:(g1 + 1) * 1024])])
            ix_done = s.chain("vector")
            s.wait("tensor", ix_done[0], ix_done[1])
            s.wait("scalar", ix_done[0], ix_done[1])
            if s.dump == "ix0" and ti == 0:
                s.wait("sync", ix_done[0], ix_done[1])
                s.sth(s.dbg[:, 0:8 * 512], ixs)
                s.wait("vector", "st", s.val("st"))
                s.wait("tensor", "st", s.val("st"))

            s.op("vector", lambda e: e.memset(cm, 0.0))
            s.chain("vector")
            cmn = dt[:, 512:1024]
            s.op("vector", lambda e, cmn=cmn: e.memset(cmn, 0.0))
            s.chain("vector")
            ldmarks = {}
            ssm = {}
            for oc in range(ICH + 1):
                if oc < ICH:
                    gb, ub = gbuf[2 * (oc % 2)], gbuf[2 * (oc % 2) + 1]
                    if oc >= 2:
                        s.wait("sync", ldmarks[oc - 2][0], ldmarks[oc - 2][1])
                    else:
                        mk = s.chain("tensor")
                        s.wait("sync", mk[0], mk[1])
                    psem = "lha" if oc % 2 == 0 else "lhb"
                    s.ldh(gb, s.iwg_s.rearrange("(n p) o -> p n o", p=128)[:, :, oc * 128:(oc + 1) * 128], sem=psem)
                    s.ldh(ub, s.iwu_s.rearrange("(n p) o -> p n o", p=128)[:, :, oc * 128:(oc + 1) * 128], sem=psem)
                    s.wait("tensor", psem, s.val(psem))
                    if len(evac_marks) >= 2:
                        s.wait("tensor", "pst", evac_marks[-2])
                    bg, bu = s.psum[oc % 3], s.psum[3 + oc % 3]
                    for kc in range(NCH):
                        s.op("tensor", lambda e, kc=kc, b=bg, gb=gb: e.matmul(
                            out=b[:, 0:512], lhsT=gb[:, kc * 128:(kc + 1) * 128],
                            rhs=ixs[:, kc * 512:(kc + 1) * 512],
                            start=(kc == 0), stop=(kc == NCH - 1)))
                    for kc in range(NCH):
                        s.op("tensor", lambda e, kc=kc, b=bu, ub=ub: e.matmul(
                            out=b[:, 0:512], lhsT=ub[:, kc * 128:(kc + 1) * 128],
                            rhs=ixs[:, kc * 512:(kc + 1) * 512],
                            start=(kc == 0), stop=(kc == NCH - 1)),
                            inc="stt" if kc == NCH - 1 else None)
                    mk = s.chain("tensor")
                    ldmarks[oc] = mk
                    s.wait("vector", "stt", s.val("stt"))
                    gs = vt[:, (oc % 2) * 512:(oc % 2) * 512 + 512]
                    sil = scb if oc % 2 == 0 else oscb
                    s.op("vector", lambda e, b=bg, gs=gs: e.scalar_tensor_tensor(
                        out=gs, in0=b[:, 0:512], scalar=mw_cols[0], in1=oscx,
                        op0=ALU.mult, op1=ALU.mult), inc="pst")
                    mkv = s.chain("vector")
                    s.wait("scalar", mkv[0], mkv[1])
                    s.op("scalar", lambda e, gs=gs, sil=sil: e.activation(
                        out=sil, in_=gs, func=AF.Silu), inc="ss")
                    ssm[oc] = s.val("ss")
                if oc >= 1:
                    po = oc - 1
                    pbu = s.psum[3 + po % 3]
                    psil = scb if po % 2 == 0 else oscb
                    s.wait("vector", "ss", ssm[po])
                    s.op("vector", lambda e, po=po, b=pbu, sil=psil: e.tensor_tensor(
                        out=hb[:, po * 512:(po + 1) * 512], in0=b[:, 0:512],
                        in1=sil, op=ALU.mult), inc="pst")
                    evac_marks.append(s.val("pst"))
                    s.chain("vector")
                    s.op("vector", lambda e, po=po: e.tensor_tensor(
                        out=cm, in0=cm, in1=hb[:, po * 512:(po + 1) * 512], op=ALU.max))
                    s.chain("vector")
                    s.op("vector", lambda e, po=po, cmn=cmn: e.tensor_tensor(
                        out=cmn, in0=cmn, in1=hb[:, po * 512:(po + 1) * 512], op=ALU.min))
                    s.chain("vector")
            s.op("vector", lambda e, cmn=cmn: e.tensor_scalar(
                out=cmn, in0=cmn, scalar1=-1.0, scalar2=None, op0=ALU.mult))
            s.chain("vector")
            s.op("vector", lambda e, cmn=cmn: e.tensor_tensor(
                out=cm, in0=cm, in1=cmn, op=ALU.max))
            cmk = s.chain("vector")
            if s.dump == "h0" and ti == 0:
                s.wait("sync", cmk[0], cmk[1])
                s.op("vector", lambda e: e.tensor_copy(
                    out=s.dbg_sb_tmp, in_=hb[:, 0:16 * 512]), inc="sv")
                s.chain("vector")
                s.wait("sync", "sv", s.val("sv"))
                s.sth(s.dbg[:, 0:16 * 512], s.dbg_sb_tmp)
                s.wait("vector", "st", s.val("st"))
                s.wait("tensor", "st", s.val("st"))
            self.scales_from_colmax(cm, scb, oscb, cmk)
            s.op("vector", lambda e: e.tensor_tensor(
                out=oscb, in0=oscb, in1=oscx, op=ALU.mult))
            s.chain("vector")
            s.op("vector", lambda e: e.tensor_scalar(
                out=oscb, in0=oscb, scalar1=mw_cols[1], scalar2=1e-5,
                op0=ALU.mult, op1=ALU.max))
            s.chain("vector")
            s.op("vector", lambda e: e.tensor_scalar(
                out=oscb, in0=oscb, scalar1=mw_cols[2], scalar2=1.0 / 127.0,
                op0=ALU.mult, op1=ALU.mult))
            s.chain("vector")

            dstream_marks = {}
            pb_ = s.pool.ap().bitcast(BF16)
            dAh = pb_[:, (R_D + 20 * KB) // 2:][:, :1024]
            dBh = pb_[:, (R_D + 22 * KB) // 2:][:, :1024]
            ihgB = pb_[:, (R_D + 24 * KB) // 2:][:, :1024]
            for g in range(NCH):
                db = dbuf[g % 2]
                if g >= 2:
                    s.wait("sync", dstream_marks[g - 2][0], dstream_marks[g - 2][1])
                else:
                    mk = s.chain("tensor")
                    s.wait("sync", mk[0], mk[1])
                dsem = "lha" if g % 2 == 0 else "lhb"
                s.ldh(db, s.iwd_s.rearrange("(n p) o -> p n o", p=128)[:, 4 * g:4 * g + 4, :], sem=dsem)
                dlw = (dsem, s.val(dsem))
                for kc in range(4 * g, 4 * g + 4):
                    a = hb[:, kc * 512:(kc + 1) * 512]
                    s.op("vector", lambda e, a=a: e.tensor_tensor(
                        out=a, in0=a, in1=scb, op=ALU.mult))
                    s.chain("vector")
                mk = s.chain("tensor")
                s.wait("vector", mk[0], mk[1])
                s.round_pair([
                    (hb[:, (4 * g) * 512:(4 * g + 2) * 512], vt, dAh, ihg),
                    (hb[:, (4 * g + 2) * 512:(4 * g + 4) * 512], dt, dBh, ihgB)])
                rg = s.chain("vector")
                if s.dump == "ih0" and ti == 0 and g == 0:
                    s.wait("sync", rg[0], rg[1])
                    s.sth(s.dbg[:, 0:1024], ihg)
                    s.wait("vector", "st", s.val("st"))
                    s.wait("tensor", "st", s.val("st"))
                s.wait("tensor", rg[0], rg[1])
                s.wait("tensor", dlw[0], dlw[1])
                for oc8 in range(NCH):
                    for kci in range(4):
                        last = (g == NCH - 1 and kci == 3)
                        stt_ = (g == 0 and kci == 0)
                        rhs_ = ihg if kci < 2 else ihgB
                        c2 = kci % 2
                        s.op("tensor", lambda e, oc8=oc8, kci=kci, db=db, c2=c2,
                             rhs_=rhs_, st_=stt_, last=last: e.matmul(
                            out=s.psum[oc8][:, 0:512],
                            lhsT=db[:, kci * 1024 + oc8 * 128:kci * 1024 + (oc8 + 1) * 128],
                            rhs=rhs_[:, c2 * 512:(c2 + 1) * 512],
                            start=st_, stop=last),
                            inc="stt" if (oc8 == NCH - 1 and kci == 3) else None)
                mk = s.chain("tensor")
                dstream_marks[g] = mk
            s.wait("vector", "stt", s.val("stt"))
            if s.dump == "dps0" and ti == 0:
                s.op("vector", lambda e: e.tensor_copy(
                    out=s.dbg_sb_tmp[:, 0:512], in_=s.psum[0][:, 0:512]), inc="sv")
                s.chain("vector")
                s.wait("sync", "sv", s.val("sv"))
                s.sth(s.dbg[:, 0:512], s.dbg_sb_tmp[:, 0:512])
                s.wait("vector", "st", s.val("st"))
            for oc8 in range(NCH):
                s.op("vector", lambda e, oc8=oc8: e.tensor_tensor(
                    out=vt[:, 0:512], in0=s.psum[oc8][:, 0:512], in1=oscb, op=ALU.mult))
                s.chain("vector")
                s.op("vector", lambda e, oc8=oc8, tsl=tsl: e.tensor_tensor(
                    out=s.xc(oc8)[:, tsl], in0=s.xc(oc8)[:, tsl], in1=vt[:, 0:512],
                    op=ALU.add), inc="pst")
                s.chain("vector")
            evac_marks.append(s.val("pst"))
            dn = s.chain("vector")
            s.wait("tensor", dn[0], dn[1])

        mkv = s.chain("vector")
        s.wait("sync", mkv[0], mkv[1])
        s.ldh(hb[:, 0:NCH * T], s.attn_s)
        v2 = s.val("ldh")
        s.wait("vector", "ldh", v2)
        for c in range(NCH):
            s.op("vector", lambda e, c=c: e.tensor_tensor(
                out=s.xc(c), in0=s.xc(c), in1=hb[:, c * T:(c + 1) * T],
                op=ALU.add), inc="sv")
        s.chain("vector")
        s.pending_adds = []

    def scales_from_colmax(self, am_ap, sc_ap, osc_ap, amk):
        s = self
        s.wait("tensor", amk[0], amk[1])
        for t4 in range(4):
            bank = s.psum[4 + t4 % 2]
            s.op("tensor", lambda e, t4=t4, b=bank: e.transpose(
                out=b[:, 0:128], in_=am_ap[:, t4 * 128:(t4 + 1) * 128],
                identity=s.ident), inc="stt")
            s.wait("vector", "stt", s.val("stt"))
            s.op("vector", lambda e, b=bank: e.tensor_reduce(
                out=s.sccol, in_=b[:, 0:128], axis=AX.X, op=ALU.max))
            s.chain("vector")
            s.op("vector", lambda e: e.tensor_copy(out=s.sccol2, in_=s.sccol))
            s.chain("vector")
            s.op("vector", lambda e: e.tensor_scalar(
                out=s.sccol, in0=s.sccol, scalar1=1e-20, scalar2=None, op0=ALU.max))
            s.chain("vector")
            s.op("vector", lambda e: e.reciprocal(out=s.sccol, in_=s.sccol))
            s.chain("vector")
            s.op("vector", lambda e: e.tensor_scalar(
                out=s.sccol, in0=s.sccol, scalar1=127.0, scalar2=None, op0=ALU.mult))
            s.chain("vector")
            s.op("vector", lambda e: e.tensor_scalar(
                out=s.rep_aps[0], in0=s.ones128, scalar1=s.sccol, scalar2=None,
                op0=ALU.mult))
            s.chain("vector")
            s.op("vector", lambda e: e.tensor_scalar(
                out=s.rep_aps[1], in0=s.ones128, scalar1=s.sccol2, scalar2=None,
                op0=ALU.mult), inc="sv")
            rv = s.val("sv")
            s.wait("tensor", "sv", rv)
            b2 = s.psum[6]
            s.op("tensor", lambda e, b=b2: e.transpose(
                out=b[:, 0:128], in_=s.rep_aps[0], identity=s.ident))
            s.op("tensor", lambda e, b=b2: e.transpose(
                out=b[:, 128:256], in_=s.rep_aps[1], identity=s.ident), inc="stt")
            s.wait("vector", "stt", s.val("stt"))
            s.op("vector", lambda e, t4=t4, b=b2: e.tensor_copy(
                out=sc_ap[:, t4 * 128:(t4 + 1) * 128], in_=b[:, 0:128]))
            s.op("vector", lambda e, t4=t4, b=b2: e.tensor_copy(
                out=osc_ap[:, t4 * 128:(t4 + 1) * 128], in_=b[:, 128:256]), inc="pst")
            s.wait("tensor", "pst", s.val("pst"))
        s.chain("vector")


    def p_l1_attn(self):
        s = self
        s.phase_barrier()
        s.sth(s.xpark, s.f32("x"))
        s.x_parked = True
        pk = s.val("st")
        s.wait("scalar", "st", pk)
        s.wait("vector", "st", pk)
        s.alloc("qe", T * NCH * 2, at=R_X)
        s.alloc("ke", 16 * D * 2, at=R_X + 32 * KB)
        s.alloc("v1", 16 * D * 2, at=R_D)
        s.alloc("wstr", D * NCH * 2, at=R_E)
        s.alloc("sel8", 1024 * 4, at=R_E + 16 * KB)
        s.alloc("kvsb", 512 * 2, at=R_E + 20 * KB)
        s.alloc("ks", 8 * 16 * 2, at=R_E + 21 * KB)
        s.alloc("qks", T * 4, at=R_E + 22 * KB)
        s.alloc("rden", T * 4, at=R_E + 30 * KB)
        s.alloc("elu0", 512 * 4, at=R_E + 38 * KB)
        s.alloc("elu1", 512 * 4, at=R_E + 40 * KB)
        s.alloc("denb", 512 * 4, at=R_E + 42 * KB)
        s.alloc("ost2", 8 * 512 * 4, at=R_D)
        qe, ke, v1 = s.bf16("qe"), s.bf16("ke"), s.bf16("v1")
        wstr = s.bf16("wstr")
        sel8 = s.f32("sel8")[0:16, :]
        kvsb, ks = s.bf16("kvsb"), s.bf16("ks")
        qks = s.f32("qks")[0:16, :]
        rden = s.f32("rden")[0:16, :]
        m0, e1 = s.f32("elu0"), s.f32("elu1")
        denb = s.f32("denb")
        s.ldh(sel8, s.sel8d)
        s.wait("tensor", "ldh", s.val("ldh"))

        def elu_evac(bank, dst, w):
            s.op("vector", lambda e, b=bank, w=w: e.tensor_scalar(
                out=m0[:, 0:w], in0=b[:, 0:w], scalar1=0.0, scalar2=None,
                op0=ALU.min), inc="pst")
            mk = s.chain("vector")
            s.wait("scalar", mk[0], mk[1])
            s.op("scalar", lambda e, w=w: e.activation(
                out=e1[:, 0:w], in_=m0[:, 0:w], func=AF.Exp), inc="ss")
            s.wait("vector", "ss", s.val("ss"))
            s.op("vector", lambda e, b=bank, d=dst, w=w: e.scalar_tensor_tensor(
                out=d, in0=b[:, 0:w], scalar=0.0, in1=e1[:, 0:w],
                op0=ALU.max, op1=ALU.add), inc="pst")
            s.chain("vector")

        s.ldg(wstr, s.Wqkv.rearrange("(c p) o -> p c o", p=128)[:, :, 0:D], sem="ldgA")
        s.wait("tensor", "ldgA")
        kidx = 0
        for oc in range(NCH):
            for j in range(NT):
                bank = s.psum[kidx % 2]
                s.wait("tensor", "pst", s.val("pst") - 2)
                for kc in range(NCH):
                    s.op("tensor", lambda e, oc=oc, j=j, kc=kc, b=bank: e.matmul(
                        out=b[:, 0:512],
                        lhsT=wstr[:, kc * D + oc * 128:kc * D + (oc + 1) * 128],
                        rhs=s.xn_ap[:, kc * T + j * 512:kc * T + (j + 1) * 512],
                        start=(kc == 0), stop=(kc == NCH - 1)),
                        inc="stt" if kc == NCH - 1 else None)
                s.wait("vector", "stt", s.val("stt"))
                elu_evac(bank, qe[:, oc * T + j * 512:oc * T + (j + 1) * 512], 512)
                kidx += 1
        mk = s.chain("tensor")
        s.wait("gpsimd", mk[0], mk[1])
        s.ldg(wstr, s.Wqkv.rearrange("(c p) o -> p c o", p=128)[:, :, D:2 * D], sem="ldgB")
        s.wait("tensor", "ldgB")
        kidx = 0
        for a in range(16):
            for half in range(2):
                bank = s.psum[2 + kidx % 2]
                s.wait("tensor", "pst", s.val("pst") - 2)
                for kc in range(NCH):
                    s.op("tensor", lambda e, a=a, hf=half, kc=kc, b=bank: e.matmul(
                        out=b[:, 0:512],
                        lhsT=s.xn_ap[:, kc * T + a * 128:kc * T + (a + 1) * 128],
                        rhs=wstr[:, kc * D + hf * 512:kc * D + (hf + 1) * 512],
                        start=(kc == 0), stop=(kc == NCH - 1)),
                        inc="stt" if kc == NCH - 1 else None)
                s.wait("vector", "stt", s.val("stt"))
                elu_evac(bank, ke[:, a * D + half * 512:a * D + (half + 1) * 512], 512)
                kidx += 1
        mk = s.chain("tensor")
        s.wait("gpsimd", mk[0], mk[1])
        s.ldg(wstr, s.Wqkv.rearrange("(c p) o -> p c o", p=128)[:, :, 2 * D:3 * D], sem="ldgA")
        s.wait("tensor", "ldgA")
        kidx = 0
        for a in range(16):
            for half in range(2):
                bank = s.psum[2 + kidx % 2]
                s.wait("tensor", "pst", s.val("pst") - 1)
                for kc in range(NCH):
                    s.op("tensor", lambda e, a=a, hf=half, kc=kc, b=bank: e.matmul(
                        out=b[:, 0:512],
                        lhsT=s.xn_ap[:, kc * T + a * 128:kc * T + (a + 1) * 128],
                        rhs=wstr[:, kc * D + hf * 512:kc * D + (hf + 1) * 512],
                        start=(kc == 0), stop=(kc == NCH - 1)),
                        inc="stt" if kc == NCH - 1 else None)
                s.wait("vector", "stt", s.val("stt"))
                s.op("vector", lambda e, a=a, hf=half, b=bank: e.tensor_copy(
                    out=v1[:, a * D + hf * 512:a * D + (hf + 1) * 512],
                    in_=b[:, 0:512]), inc="pst")
                kidx += 1
        prj = s.chain("vector")
        s.wait("tensor", prj[0], prj[1])

        s.op("vector", lambda e: e.memset(ks, 0.0), inc="pst")
        s.chain("vector")
        s.wait("tensor", "pst", s.val("pst"))
        for hp in range(8):
            bank = s.psum[4 + hp % 2]
            for parity in range(2):
                h = 2 * hp + parity
                base = 64 * parity
                for kt in range(16):
                    s.op("tensor", lambda e, h=h, kt=kt, b=bank, base=base: e.matmul(
                        out=b[base:base + 64, 0:64],
                        lhsT=ke[:, kt * D + h * 64:kt * D + h * 64 + 64],
                        rhs=v1[:, kt * D + h * 64:kt * D + h * 64 + 64],
                        start=(kt == 0), stop=(kt == 15)),
                        inc="stt" if kt == 15 else None)
            s.wait("vector", "stt", s.val("stt"))
            s.op("vector", lambda e, hp=hp, b=bank: e.tensor_copy(
                out=kvsb[:, hp * 64:(hp + 1) * 64], in_=b[:, 0:64]), inc="pst")
            if hp >= 1:
                s.wait("tensor", "pst", s.val("pst") - 1)
        for h in range(H):
            bank = s.psum[6]
            base = 64 * (h % 2)
            for kt in range(16):
                s.op("tensor", lambda e, h=h, kt=kt, b=bank, base=base: e.matmul(
                    out=b[base:base + 64, h:h + 1],
                    lhsT=ke[:, kt * D + h * 64:kt * D + h * 64 + 64],
                    rhs=s.onesb_col,
                    start=(kt == 0), stop=(kt == 15)),
                    inc="stt" if kt == 15 else None)
            s.wait("vector", "stt", s.val("stt"))
            s.op("vector", lambda e, h=h, b=bank, base=base: e.tensor_copy(
                out=ks[base:base + 64, (h // 2) * 16 + h:(h // 2) * 16 + h + 1],
                in_=b[base:base + 64, h:h + 1]), inc="pst")
            s.wait("tensor", "pst", s.val("pst"))
        for j in range(NT):
            bank = s.psum[7]
            for kc in range(NCH):
                s.op("tensor", lambda e, j=j, kc=kc, b=bank: e.matmul(
                    out=b[0:16, 0:512],
                    lhsT=ks[:, kc * 16:(kc + 1) * 16],
                    rhs=qe[:, kc * T + j * 512:kc * T + (j + 1) * 512],
                    start=(kc == 0), stop=(kc == NCH - 1)),
                    inc="stt" if kc == NCH - 1 else None)
            s.wait("vector", "stt", s.val("stt"))
            s.op("vector", lambda e, j=j, b=bank: e.tensor_scalar(
                out=qks[:, j * 512:(j + 1) * 512], in0=b[0:16, 0:512],
                scalar1=1e-6, scalar2=None, op0=ALU.add), inc="pst")
            s.wait("tensor", "pst", s.val("pst"))
        s.chain("vector")
        s.op("vector", lambda e: e.reciprocal(out=rden, in_=qks), inc="sv")
        s.chain("vector")
        rdv = s.val("sv")
        s.wait("tensor", "sv", rdv)

        af1 = s.bf16("ke")
        kidx = 0
        for hp in range(8):
            for j in range(NT):
                bank = s.psum[kidx % 2]
                bden = s.psum[2 + kidx % 2]
                s.wait("tensor", "pst", s.val("pst") - 2)
                for parity in range(2):
                    h = 2 * hp + parity
                    base = 64 * parity
                    s.op("tensor", lambda e, h=h, hp=hp, j=j, b=bank, base=base: e.matmul(
                        out=b[base:base + 64, 0:512],
                        lhsT=kvsb[base:base + 64, hp * 64:(hp + 1) * 64],
                        rhs=qe[base:base + 64, (h // 2) * T + j * 512:(h // 2) * T + (j + 1) * 512],
                        start=True, stop=True))
                s.op("tensor", lambda e, hp=hp, j=j, b=bden: e.matmul(
                    out=b[:, 0:512], lhsT=sel8[:, hp * 128:(hp + 1) * 128],
                    rhs=rden[:, j * 512:(j + 1) * 512],
                    start=True, stop=True), inc="stt")
                s.wait("vector", "stt", s.val("stt"))
                s.op("vector", lambda e, b=bden: e.tensor_copy(out=denb, in_=b[:, 0:512]), inc="pst")
                s.chain("vector")
                s.op("vector", lambda e, hp=hp, j=j, b=bank: e.tensor_tensor(
                    out=af1[:, hp * T + j * 512:hp * T + (j + 1) * 512],
                    in0=b[:, 0:512], in1=denb, op=ALU.mult), inc="pst")
                s.chain("vector")
                kidx += 1
        afd = s.chain("vector")
        s.wait("tensor", afd[0], afd[1])
        s.wait("gpsimd", afd[0], afd[1])

        s.ldg(wstr, s.Wout.rearrange("(c p) o -> p c o", p=128), sem="ldgB")
        s.wait("tensor", "ldgB")
        ost2 = s.f32("ost2")
        for j in range(NT):
            s.wait("vector", "st", s.val("st"))
            for oc in range(NCH):
                bank = s.psum[oc]
                for kc in range(NCH):
                    s.op("tensor", lambda e, oc=oc, j=j, kc=kc, b=bank: e.matmul(
                        out=b[:, 0:512],
                        lhsT=wstr[:, kc * D + oc * 128:kc * D + (oc + 1) * 128],
                        rhs=af1[:, kc * T + j * 512:kc * T + (j + 1) * 512],
                        start=(kc == 0), stop=(kc == NCH - 1)),
                        inc="stt" if kc == NCH - 1 else None)
                s.wait("vector", "stt", s.val("stt"))
                s.op("vector", lambda e, oc=oc, b=bank: e.tensor_scalar(
                    out=ost2[:, oc * 512:(oc + 1) * 512], in0=b[:, 0:512],
                    scalar1=s.bout_sb[:, oc:oc + 1], scalar2=None,
                    op0=ALU.add), inc="pst")
            s.wait("sync", "pst", s.val("pst"))
            s.sth(s.attn_s.rearrange("p (c t) -> p c t", c=NCH)[:, :, j * 512:(j + 1) * 512],
                  ost2.rearrange("p (c t) -> p c t", c=NCH))
            s.wait("tensor", "pst", s.val("pst"))
        s.pending_adds.append(s.attn_s)
        s.l1attn_st = s.val("st")


    def p_l1_ffn(self):
        s = self
        s.phase_barrier()

        def hchunk(kc):
            off = (R_X + kc * 4 * KB) if kc < 24 else (158 * KB + (kc - 24) * 4 * KB)
            return s.pool.ap().bitcast(BF16)[:, off // 2: off // 2 + T]

        bfspan = lambda off, n: s.pool.ap().bitcast(BF16)[:, off // 2: off // 2 + n]
        fspan = lambda off, n: s.pool.ap()[:, off // 4: off // 4 + n]
        wb1 = [bfspan(190 * KB + i * 2 * KB, 1024) for i in range(4)]
        sil = fspan(198 * KB, 512)
        us = fspan(200 * KB, 512)

        tmarks = {}
        emarks = []
        for oc in range(ICH):
            g0, u0 = wb1[2 * (oc % 2)], wb1[2 * (oc % 2) + 1]
            psem = "ldgA" if oc % 2 == 0 else "ldgB"
            if oc >= 2:
                s.wait("gpsimd", tmarks[oc - 2][0], tmarks[oc - 2][1])
            s.ldg(g0, s.w1T.rearrange("(c p) o -> p c o", p=128)[:, :, oc * 128:(oc + 1) * 128], sem=psem)
            s.ldg(u0, s.w2T.rearrange("(c p) o -> p c o", p=128)[:, :, oc * 128:(oc + 1) * 128], sem=psem)
            s.wait("tensor", psem, s.val(psem))
            for j in range(NT):
                n = oc * NT + j
                if n >= 2:
                    s.wait("tensor", "pss", emarks[n - 2][0])
                    s.wait("tensor", "pst", emarks[n - 2][1])
                bg, bu = s.psum[2 * (j % 2)], s.psum[2 * (j % 2) + 1]
                for kc in range(NCH):
                    s.op("tensor", lambda e, kc=kc, j=j, b=bg, g0=g0: e.matmul(
                        out=b[:, 0:512], lhsT=g0[:, kc * 128:(kc + 1) * 128],
                        rhs=s.xn_ap[:, kc * T + j * 512:kc * T + (j + 1) * 512],
                        start=(kc == 0), stop=(kc == NCH - 1)))
                for kc in range(NCH):
                    s.op("tensor", lambda e, kc=kc, j=j, b=bu, u0=u0: e.matmul(
                        out=b[:, 0:512], lhsT=u0[:, kc * 128:(kc + 1) * 128],
                        rhs=s.xn_ap[:, kc * T + j * 512:kc * T + (j + 1) * 512],
                        start=(kc == 0), stop=(kc == NCH - 1)),
                        inc="stt" if kc == NCH - 1 else None)
                s.wait("scalar", "stt", s.val("stt"))
                s.wait("scalar", "sv", s.val("sv"))
                s.op("scalar", lambda e, oc=oc, b=bg: e.activation(
                    out=sil, in_=b[:, 0:512], func=AF.Silu,
                    scale=s.s1_sb[:, oc:oc + 1]), inc="pss")
                pssv = s.val("pss")
                mks = s.chain("scalar")
                s.wait("vector", mks[0], mks[1])
                s.op("vector", lambda e, oc=oc, b=bu: e.tensor_scalar(
                    out=us, in0=b[:, 0:512], scalar1=s.s2_sb[:, oc:oc + 1],
                    scalar2=None, op0=ALU.mult), inc="pst")
                pstv = s.val("pst")
                emarks.append((pssv, pstv))
                s.chain("vector")
                s.op("vector", lambda e, oc=oc, j=j: e.tensor_tensor(
                    out=hchunk(oc)[:, j * 512:(j + 1) * 512], in0=sil, in1=us,
                    op=ALU.mult), inc="sv")
                s.chain("vector")
            mk = s.chain("tensor")
            tmarks[oc] = mk
        h1d = s.chain("vector")
        for eng in ("tensor", "sync", "gpsimd", "scalar"):
            s.wait(eng, h1d[0], h1d[1])

        def hidden_stage(wT, scol, dst_dram, buf_base):
            hb0 = bfspan(buf_base, 4096)
            hb1_ = bfspan(buf_base + 8 * KB, 4096)
            stg_ = bfspan(buf_base + 16 * KB, 2048)
            tm = {}
            gm = []
            for oc in range(ICH):
                hb_ = hb0 if oc % 2 == 0 else hb1_
                psem = "ldgA" if oc % 2 == 0 else "ldgB"
                if oc >= 2:
                    s.wait("gpsimd", tm[oc - 2][0], tm[oc - 2][1])
                s.ldg(hb_, wT.rearrange("(c p) o -> p c o", p=128)[:, :, oc * 128:(oc + 1) * 128], sem=psem)
                s.wait("tensor", psem, s.val(psem))
                for j in range(NT):
                    n = oc * NT + j
                    if n >= 2:
                        s.wait("tensor", "pss", gm[n - 2])
                    bank = s.psum[j % 2]
                    for kc in range(ICH):
                        s.op("tensor", lambda e, kc=kc, j=j, b=bank, hb_=hb_: e.matmul(
                            out=b[:, 0:512], lhsT=hb_[:, kc * 128:(kc + 1) * 128],
                            rhs=hchunk(kc)[:, j * 512:(j + 1) * 512],
                            start=(kc == 0), stop=(kc == ICH - 1)),
                            inc="stt" if kc == ICH - 1 else None)
                    s.wait("scalar", "stt", s.val("stt"))
                    s.op("scalar", lambda e, oc=oc, j=j, b=bank, sc=scol, stg_=stg_: e.activation(
                        out=stg_[:, j * 512:(j + 1) * 512], in_=b[:, 0:512],
                        func=AF.Gelu, scale=sc[:, oc:oc + 1]), inc="pss")
                    gm.append(s.val("pss"))
                mks = s.chain("scalar")
                s.wait("sync", mks[0], mks[1])
                s.sth(dst_dram.rearrange("p (c t) -> p c t", c=ICH)[:, oc, :], stg_)
                s.wait("scalar", "st", s.val("st"))
                mk = s.chain("tensor")
                tm[oc] = mk
            mkx = s.chain("scalar")
            for eng in ("sync", "vector", "tensor", "gpsimd"):
                s.wait(eng, mkx[0], mkx[1])

        hidden_stage(s.h1T, s.sh1_sb, s.h2_s, 126 * KB)
        s.wait("sync", "st", s.val("st"))
        s.ldh(bfspan(R_X, 24 * T), s.h2_s[:, 0:24 * T])
        s.ldh(bfspan(158 * KB, 8 * T), s.h2_s[:, 24 * T:32 * T])
        lv = s.val("ldh")
        s.wait("tensor", "ldh", lv)
        s.wait("sync", "ldh", lv)
        hidden_stage(s.h2T, s.sh2_sb, s.h3_s, 126 * KB)

        xh = fspan(R_D, NCH * 1024)
        ah = fspan(R_D + 32 * KB, NCH * 1024)

        def h3c(kc):
            return bfspan(R_X + kc * 2 * KB, 1024)

        for half in range(2):
            hsl = slice(half * 1024, (half + 1) * 1024)
            s.wait("sync", "st", s.val("st"))
            s.ldh(xh.rearrange("p (c t) -> p c t", c=NCH),
                  s.xpark.rearrange("p (c t) -> p c t", c=NCH)[:, :, hsl])
            s.ldh(ah.rearrange("p (c t) -> p c t", c=NCH),
                  s.attn_s.rearrange("p (c t) -> p c t", c=NCH)[:, :, hsl])
            for kc in range(ICH):
                s.ldh(h3c(kc), s.h3_s[:, kc * T + half * 1024:kc * T + half * 1024 + 1024],
                      sem="lha")
            lha_v = s.val("lha")
            s.wait("tensor", "lha", lha_v)
            ldv = s.val("ldh")
            s.wait("vector", "ldh", ldv)
            s.wait("tensor", "ldh", ldv)
            mkt = s.chain("tensor")
            s.wait("gpsimd", mkt[0], mkt[1])
            wmks = {}
            smarks = []
            for oc in range(NCH):
                hb_ = bfspan(156 * KB, 4096) if oc % 2 == 0 else bfspan(164 * KB, 4096)
                psem = "ldgA" if oc % 2 == 0 else "ldgB"
                if oc >= 2:
                    s.wait("gpsimd", wmks[oc - 2][0], wmks[oc - 2][1])
                s.ldg(hb_, s.w3T.rearrange("(c p) o -> p c o", p=128)[:, :, oc * 128:(oc + 1) * 128], sem=psem)
                s.wait("tensor", psem, s.val(psem))
                for j in range(2):
                    n = oc * 2 + j
                    if n >= 2:
                        s.wait("tensor", "pst", smarks[n - 2])
                    bank = s.psum[j % 2]
                    for kc in range(ICH):
                        s.op("tensor", lambda e, kc=kc, j=j, b=bank, hb_=hb_: e.matmul(
                            out=b[:, 0:512], lhsT=hb_[:, kc * 128:(kc + 1) * 128],
                            rhs=h3c(kc)[:, j * 512:(j + 1) * 512],
                            start=(kc == 0), stop=(kc == ICH - 1)),
                            inc="stt" if kc == ICH - 1 else None)
                    s.wait("vector", "stt", s.val("stt"))
                    s.op("vector", lambda e, oc=oc, j=j, b=bank, xh=xh: e.scalar_tensor_tensor(
                        out=xh[:, oc * 1024 + j * 512:oc * 1024 + (j + 1) * 512],
                        in0=b[:, 0:512], scalar=s.s3_sb[:, oc:oc + 1],
                        in1=xh[:, oc * 1024 + j * 512:oc * 1024 + (j + 1) * 512],
                        op0=ALU.mult, op1=ALU.add), inc="pst")
                    smarks.append(s.val("pst"))
                    s.chain("vector")
                    s.op("vector", lambda e, oc=oc, j=j, xh=xh, ah=ah: e.tensor_tensor(
                        out=xh[:, oc * 1024 + j * 512:oc * 1024 + (j + 1) * 512],
                        in0=xh[:, oc * 1024 + j * 512:oc * 1024 + (j + 1) * 512],
                        in1=ah[:, oc * 1024 + j * 512:oc * 1024 + (j + 1) * 512],
                        op=ALU.add), inc="sv")
                    s.chain("vector")
                mk = s.chain("tensor")
                wmks[oc] = mk
            fin = s.chain("vector")
            s.wait("sync", fin[0], fin[1])
            s.sth(s.xpark.rearrange("p (c t) -> p c t", c=NCH)[:, :, hsl],
                  xh.rearrange("p (c t) -> p c t", c=NCH))
            stv = s.val("st")
            s.wait("vector", "st", stv)
            s.wait("tensor", "st", stv)
            s.wait("sync", "st", stv)
        s.phase_barrier()
        s.ldh(s.f32("x"), s.xpark)
        lv2 = s.val("ldh")
        for eng in ("vector", "scalar", "tensor", "gpsimd"):
            s.wait(eng, "ldh", lv2)
        s.x_parked = False
        s.pending_adds = []

    def p_store(self):
        s = self
        s.phase_barrier()
        s.alloc("ostg", T * NCH * 4, at=R_D)
        o_tm = s.f32("ostg")
        if s.x_parked:
            s.ldh(s.f32("x"), s.xpark)
            for da in s.pending_adds:
                s.ldh(o_tm, da)
                v = s.val("ldh")
                s.wait("vector", "ldh", v)
                for c in range(NCH):
                    s.op("vector", lambda e, c=c: e.tensor_tensor(
                        out=s.xc(c), in0=s.xc(c), in1=o_tm[:, c * T:(c + 1) * T],
                        op=ALU.add), inc="sv")
                s.chain("vector")
            s.x_parked = False
            s.pending_adds = []
            s.wait("tensor", "sv", s.val("sv"))
            s.wait("tensor", "ldh", s.val("ldh"))
        base = s.val("pst")
        for a in range(16):
            for c in range(NCH):
                src = s.xc(c)[:, a * 128:(a + 1) * 128]
                dst = o_tm[:, a * D + c * 128: a * D + (c + 1) * 128]
                s.transpose_128(src, dst, base, a * NCH + c)
        s.wait("sync", "pst", s.val("pst"))
        s.sth(s.out.rearrange("(a p) d -> p a d", p=128), o_tm)
        s.wait("sync", "st", s.val("st"))


def host_inputs(ins, core):
    f = np.float32
    col = lambda v, nch: np.ascontiguousarray(np.asarray(v).reshape(nch, 128).T).astype(f)
    tr = lambda v: np.ascontiguousarray(np.asarray(v).T).astype(f)
    m = {
        "x_in": np.ascontiguousarray(np.asarray(ins["L0_attn_x"][core])).astype(f),
        "Wq": np.asarray(ins["L0_attn_Wq"], f), "Wk": np.asarray(ins["L0_attn_Wk"], f),
        "Wv": np.asarray(ins["L0_attn_Wv"], f), "Wm": np.asarray(ins["L0_attn_Wm"], f),
        "bq": col(ins["L0_attn_bq"], NCH), "bk": col(ins["L0_attn_bk"], NCH),
        "bv": np.asarray(ins["L0_attn_bv"], f).reshape(1, D), "bm": col(ins["L0_attn_bm"], NCH),
        "wgT": tr(ins["L0_ffn_w_gate"]), "wuT": tr(ins["L0_ffn_w_up"]),
        "wdT": tr(ins["L0_ffn_w_down"]),
        "Wqkv": np.asarray(ins["L1_attn_Wqkv"], f), "Wout": np.asarray(ins["L1_attn_Wout"], f),
        "bout": col(ins["L1_attn_bout"], NCH),
        "w1T": tr(ins["L1_ffn_w1_w"]), "w2T": tr(ins["L1_ffn_w2_w"]),
        "h1T": tr(ins["L1_ffn_h1_w"]), "h2T": tr(ins["L1_ffn_h2_w"]),
        "w3T": tr(ins["L1_ffn_w3_w"]),
        "s1c": col(ins["L1_ffn_w1_s"], ICH), "s2c": col(ins["L1_ffn_w2_s"], ICH),
        "sh1c": col(ins["L1_ffn_h1_s"], ICH), "sh2c": col(ins["L1_ffn_h2_s"], ICH),
        "s3c": col(ins["L1_ffn_w3_s"], NCH),
        "ident": np.eye(128, dtype=f),
        "identb": np.eye(128, dtype=ml_dtypes.bfloat16),
        "triu": np.triu(np.ones((128, 128), np.float32)).astype(ml_dtypes.bfloat16),
        "sel8": _sel8(),
        "ident4": np.tile(np.eye(128, dtype=f), (1, 4)),
    }
    return m


def np_softmax(x, axis=-1):
    m = np.max(x, axis=axis, keepdims=True)
    e = np.exp(x - m)
    return e / e.sum(axis=axis, keepdims=True)


def np_silu(x):
    return x / (1 + np.exp(-x))


def np_gelu_exact(x):
    from scipy.special import erf
    return 0.5 * x * (1 + erf(x / np.sqrt(2)))


def np_layernorm(x, eps=1e-5):
    m = x.mean(-1, keepdims=True)
    v = x.var(-1, keepdims=True)
    return (x - m) / np.sqrt(v + eps)


def ref_l0_attn(x, Wq, bq, Wk, bk, Wv, bv, Wm, bm):
    S = x.shape[0]
    def heads(w, b):
        return (x @ w + b).reshape(S, H, HS).transpose(1, 0, 2)
    q, k, v = heads(Wq, bq), heads(Wk, bk), heads(Wv, bv)
    sc = np.einsum("hqd,hkd->hqk", q, k) / np.sqrt(HS)
    mask = np.arange(S)[None, :] > np.arange(S)[:, None]
    sc = np.where(mask[None], -1e12, sc)
    p = np_softmax(sc, -1)
    a = np.einsum("hqk,hkd->hqd", p, v)
    a = a.transpose(1, 0, 2).reshape(S, D)
    return a @ Wm + bm


def ref_l0_ffn(x, w_gate, w_up, w_down):
    def wq_(w):
        sc = 1.0 / np.clip(np.mean(np.abs(w)), 1e-5, None)
        return np.clip(np.round(w * sc), -1, 1) / sc
    def aq_(t):
        sc = 127.0 / np.clip(np.max(np.abs(t), -1, keepdims=True), 1e-5, None)
        return np.clip(np.round(t * sc), -128, 127) / sc
    bl = lambda t, w: aq_(t) @ wq_(w).T
    g = np_silu(bl(x, w_gate))
    u = bl(x, w_up)
    return bl(g * u, w_down)


def ref_l1_attn(x, Wqkv, Wout, bout):
    S = x.shape[0]
    qkv = x @ Wqkv
    q, k, v = np.split(qkv, 3, -1)
    rs = lambda t: t.reshape(S, H, HS).transpose(1, 0, 2)
    q, k, v = rs(q), rs(k), rs(v)
    el = lambda t: np.where(t > 0, t + 1.0, np.exp(t))
    q, k = el(q), el(k)
    kv = np.einsum("hnd,hnf->hdf", k, v)
    ks = k.sum(1)
    qo = np.einsum("hnd,hdf->hnf", q, kv)
    qks = np.einsum("hnd,hd->hn", q, ks)
    o = qo / (qks[..., None] + 1e-6)
    o = o.transpose(1, 0, 2).reshape(S, D)
    return o @ Wout + bout


def ref_l1_ffn(x, ws):
    tl = lambda t, w, sc: (t @ w.T) * sc
    h = np_silu(tl(x, ws["w1_w"], ws["w1_s"])) * tl(x, ws["w2_w"], ws["w2_s"])
    h = np_gelu_exact(tl(h, ws["h1_w"], ws["h1_s"]))
    h = np_gelu_exact(tl(h, ws["h2_w"], ws["h2_s"]))
    return tl(h, ws["w3_w"], ws["w3_s"])


def _sel8():
    m = np.zeros((16, 1024), np.float32)
    for hp in range(8):
        for p in range(128):
            h = 2 * hp + (1 if p >= 64 else 0)
            m[h, 128 * hp + p] = 1.0
    return m


def kernel(**inputs):
    mk = MK(7, dump=None)
    nc = mk.build()
    shared = host_shared(inputs)
    in_maps = []
    for core in range(8):
        m = dict(shared)
        m["x_in"] = np.ascontiguousarray(np.asarray(inputs["L0_attn_x"][core])).astype(np.float32)
        in_maps.append(m)
    from concourse.bass_utils import run_bass_kernel_spmd
    res = run_bass_kernel_spmd(nc, in_maps, core_ids=list(range(8)))
    out = np.stack([np.asarray(r["out"], np.float32) for r in res.results], axis=0)
    return out


def host_shared(ins):
    f = np.float32
    col = lambda v, nch: np.ascontiguousarray(np.asarray(v).reshape(nch, 128).T).astype(f)
    tr = lambda v: np.ascontiguousarray(np.asarray(v).T).astype(f)
    return {
        "Wq": np.asarray(ins["L0_attn_Wq"], f), "Wk": np.asarray(ins["L0_attn_Wk"], f),
        "Wv": np.asarray(ins["L0_attn_Wv"], f), "Wm": np.asarray(ins["L0_attn_Wm"], f),
        "bq": col(ins["L0_attn_bq"], NCH), "bk": col(ins["L0_attn_bk"], NCH),
        "bv": np.asarray(ins["L0_attn_bv"], f).reshape(1, D), "bm": col(ins["L0_attn_bm"], NCH),
        "wgT": tr(ins["L0_ffn_w_gate"]), "wuT": tr(ins["L0_ffn_w_up"]),
        "wdT": tr(ins["L0_ffn_w_down"]),
        "Wqkv": np.asarray(ins["L1_attn_Wqkv"], f), "Wout": np.asarray(ins["L1_attn_Wout"], f),
        "bout": col(ins["L1_attn_bout"], NCH),
        "w1T": tr(ins["L1_ffn_w1_w"]), "w2T": tr(ins["L1_ffn_w2_w"]),
        "h1T": tr(ins["L1_ffn_h1_w"]), "h2T": tr(ins["L1_ffn_h2_w"]),
        "w3T": tr(ins["L1_ffn_w3_w"]),
        "s1c": col(ins["L1_ffn_w1_s"], ICH), "s2c": col(ins["L1_ffn_w2_s"], ICH),
        "sh1c": col(ins["L1_ffn_h1_s"], ICH), "sh2c": col(ins["L1_ffn_h2_s"], ICH),
        "s3c": col(ins["L1_ffn_w3_s"], NCH),
        "ident": np.eye(128, dtype=f),
        "identb": np.eye(128, dtype=ml_dtypes.bfloat16),
        "triu": np.triu(np.ones((128, 128), np.float32)).astype(ml_dtypes.bfloat16),
        "sel8": _sel8(),
        "ident4": np.tile(np.eye(128, dtype=f), (1, 4)),
    }


# version 11
# speedup vs baseline: 1.0031x; 1.0031x over previous
import sys
import numpy as np
import ml_dtypes
import concourse.bass as bass
import concourse.mybir as mybir
from contextlib import ExitStack

F32 = mybir.dt.float32
BF16 = mybir.dt.bfloat16
ALU = mybir.AluOpType
AF = mybir.ActivationFunctionType
AX = mybir.AxisListType

T = 2048
D = 1024
NCH = 8
H = 16
HS = 64
INNER = 4096
ICH = 32
NT = 4
LNEPS = 1e-5

ENGS = ("sync", "vector", "scalar", "tensor", "gpsimd")

KB = 1024
R_CONSTS = 0 * KB
R_ROWS = 4 * KB
R_X = 28 * KB
R_D = 92 * KB
R_E = 156 * KB
POOL_END = 204 * KB


class MK:
    def __init__(self, n_phases=99, dump=None):
        self.n_phases = n_phases
        self.dump = dump
        self.nc = bass.Bass("TRN2", target_bir_lowering=False)
        self.ops = {e: [] for e in ENGS}
        self.cnt = {}
        self.sems = {}
        self.x_parked = False
        self.pending_adds = []
        self.prepass_gen = None
        self.bg_pst_mark = None
        self.qkv_budget = 64

    def op(self, eng, fn, inc=None, inc_by=1):
        ent = {"fn": fn, "inc": None}
        if inc is not None:
            self.cnt[inc] += inc_by
            ent["inc"] = (inc, inc_by, self.cnt[inc])
        self.ops[eng].append(ent)
        return self.cnt.get(inc)

    def wait(self, eng, sem_name, val=None):
        v = self.cnt[sem_name] if val is None else val
        if v > 0:
            self.ops[eng].append({"wait": (sem_name, v)})
        return v

    def val(self, sem_name):
        return self.cnt[sem_name]

    CHAIN = {"vector": "vch", "scalar": "sch", "tensor": "tch", "gpsimd": "gch"}

    def phase_barrier(self, engs=("sync", "vector", "scalar", "tensor", "gpsimd")):
        for eng in ("vector", "scalar", "tensor", "gpsimd"):
            if self.ops[eng] and any("fn" in x for x in self.ops[eng]):
                self.chain(eng)
        snap = {nm: self.cnt[nm] for nm in
                ("ldh", "ldg", "ldgA", "ldgB", "ldgC", "st", "sv", "ss", "stt",
                 "sg", "pst", "pss", "psa", "lha", "lhb", "vch", "sch", "tch", "gch")}
        for eng in engs:
            for nm, v in snap.items():
                self.wait(eng, nm, v)

    def chain(self, eng):
        for ent in reversed(self.ops[eng]):
            if "fn" in ent:
                last = ent
                break
        else:
            return
        if last["inc"] is None:
            nm = self.CHAIN[eng]
            self.cnt[nm] += 1
            last["inc"] = (nm, 1, self.cnt[nm])
        nm, _, v = last["inc"]
        self.ops[eng].append({"wait": (nm, v)})
        return (nm, v)

    def ldh(self, out, in_, sem="ldh"):
        return self.op("sync", lambda e, o=out, i=in_: e.dma_start(out=o, in_=i),
                       inc=sem, inc_by=16)

    def ldg(self, out, in_, sem="ldgA"):
        self.wait("gpsimd", sem, self.cnt[sem])
        return self.op("gpsimd", lambda e, o=out, i=in_: e.dma_start(out=o, in_=i),
                       inc=sem, inc_by=16)

    def sth(self, out, in_):
        self.wait("sync", "st", self.cnt["st"])
        return self.op("sync", lambda e, o=out, i=in_: e.dma_start(out=o, in_=i),
                       inc="st", inc_by=16)

    def alloc(self, name, nbytes, at=None, align=64):
        off = at if at is not None else (self.cursor + align - 1) // align * align
        assert off + nbytes <= self.POOLB, f"pool overflow: {name} @{off}+{nbytes}"
        self.spans[name] = (off, nbytes)
        self.cursor = max(self.cursor, off + nbytes) if at is None else self.cursor
        self.watermark = max(self.watermark, off + nbytes)
        return off

    def f32(self, name, cols=None):
        off, size = self.spans[name]
        n = size // 4 if cols is None else cols
        return self.pool.ap()[:, off // 4: off // 4 + n]

    def bf16(self, name, cols=None):
        off, size = self.spans[name]
        n = size // 2 if cols is None else cols
        return self.pool.ap().bitcast(BF16)[:, off // 2: off // 2 + n]

    def build(self):
        nc = self.nc
        dram = lambda nm, sh, dt, kind="ExternalInput": nc.dram_tensor(nm, sh, dt, kind=kind).ap()
        s = self
        s.x_in = dram("x_in", [T, D], F32)
        s.Wq = dram("Wq", [D, D], F32); s.Wk = dram("Wk", [D, D], F32)
        s.Wv = dram("Wv", [D, D], F32); s.Wm = dram("Wm", [D, D], F32)
        s.bq = dram("bq", [128, NCH], F32); s.bk = dram("bk", [128, NCH], F32)
        s.bv = dram("bv", [1, D], F32); s.bm = dram("bm", [128, NCH], F32)
        s.wgT = dram("wgT", [D, INNER], F32); s.wuT = dram("wuT", [D, INNER], F32)
        s.wdT = dram("wdT", [INNER, D], F32)
        s.Wqkv = dram("Wqkv", [D, 3 * D], F32)
        s.Wout = dram("Wout", [D, D], F32); s.bout = dram("bout", [128, NCH], F32)
        s.w1T = dram("w1T", [D, INNER], F32); s.w2T = dram("w2T", [D, INNER], F32)
        s.h1T = dram("h1T", [INNER, INNER], F32); s.h2T = dram("h2T", [INNER, INNER], F32)
        s.w3T = dram("w3T", [INNER, D], F32)
        s.s1c = dram("s1c", [128, ICH], F32); s.s2c = dram("s2c", [128, ICH], F32)
        s.sh1c = dram("sh1c", [128, ICH], F32); s.sh2c = dram("sh2c", [128, ICH], F32)
        s.s3c = dram("s3c", [128, NCH], F32)
        s.identd = dram("ident", [128, 128], F32)
        s.identbd = dram("identb", [128, 128], BF16)
        s.triud = dram("triu", [128, 128], BF16)
        s.sel8d = dram("sel8", [16, 1024], F32)
        s.ident4d = dram("ident4", [128, 512], F32)
        s.out = dram("out", [T, D], F32, "ExternalOutput")
        s.dbg = dram("dbg", [128, 17408], BF16, "ExternalOutput") if s.dump else None
        s.iwg_s = dram("iwg_s", [D, INNER], BF16, "Internal")
        s.iwu_s = dram("iwu_s", [D, INNER], BF16, "Internal")
        s.iwd_s = dram("iwd_s", [INNER, D], BF16, "Internal")
        s.xpark = dram("xpark", [128, NCH * T], F32, "Internal")
        s.attn_s = dram("attn_s", [128, NCH * T], F32, "Internal")
        s.h2_s = dram("h2_s", [128, ICH * T], BF16, "Internal")
        s.h3_s = dram("h3_s", [128, ICH * T], BF16, "Internal")

        ctx = ExitStack()
        with ctx:
            self.POOLB = 204 * 1024
            self.pool = ctx.enter_context(nc.sbuf_tensor("pool", [128, self.POOLB // 4], F32))
            self.spans, self.cursor, self.watermark = {}, 0, 0
            self.psumbig = ctx.enter_context(nc.psum_tensor("pb", [128, 4096], F32))
            self.psum = [self.psumbig.ap()[:, i * 512:(i + 1) * 512]
                         for i in range(8)]
            for nm in ("ldh", "ldg", "ldgA", "ldgB", "ldgC", "st", "sv", "ss",
                       "stt", "sg", "pst", "pss", "psa", "wb", "lha", "lhb",
                       "vch", "sch", "tch", "gch"):
                self.sems[nm] = ctx.enter_context(nc.semaphore(nm))
                self.cnt[nm] = 0
            blk = ctx.enter_context(nc.Block())

            self.alloc("consts", 4096, at=R_CONSTS)
            self.alloc("rows", 24576, at=R_ROWS)
            self.alloc("x", T * NCH * 4, at=R_X)

            self.setup_consts()
            self.p0_load_x()
            if self.n_phases >= 1:
                self.p_ln(0)
            if self.n_phases >= 2:
                if self.n_phases >= 4:
                    self.prepass_gen = self.prepass_generator()
                self.p_l0_qkv()
            if self.n_phases >= 3:
                self.p_l0_attn()
            if self.n_phases >= 4:
                self.p_l0_ffn()
            if self.n_phases >= 5:
                self.p_ln(1)
            if self.n_phases >= 6:
                self.p_l1_attn()
            if self.n_phases >= 7:
                self.p_l1_ffn()
            self.p_store()

            for eng in ENGS:
                def make(eng):
                    def body(e):
                        for ent in self.ops[eng]:
                            if "wait" in ent:
                                nm, v = ent["wait"]
                                e.wait_ge(self.sems[nm], v)
                            else:
                                i = ent["fn"](e)
                                if ent["inc"] is not None:
                                    nm, n, _ = ent["inc"]
                                    i.then_inc(self.sems[nm], n)
                    return body
                getattr(blk, eng)(make(eng))
        return nc

    def xc(self, c):
        return self.f32("x")[:, c * T:(c + 1) * T]

    def setup_consts(self):
        s = self
        co = s.spans["consts"][0]
        p, pb = s.pool.ap(), s.pool.ap().bitcast(BF16)
        s.ident = p[:, co // 4: co // 4 + 128]
        s.identb = pb[:, (co + 512) // 2:][:, :128]
        s.triu = pb[:, (co + 768) // 2:][:, :128]
        s.ones_col = p[:, (co + 1024) // 4:][:, :1]
        s.ones_row = p[0:1, (co + 1028) // 4:][:, :128]
        s.onesb_col = pb[:, (co + 1540) // 2:][:, :1]
        s.ones128 = p[:, (co + 2048) // 4:][:, :128]
        s.halfcol = p[:, (co + 2616) // 4:][:, :1]
        ro = s.spans["rows"][0]

        def colf32(n):
            nonlocal ro
            ap = p[:, ro // 4: ro // 4 + n]
            ro += n * 4
            return ap
        s.bq_sb = colf32(NCH); s.bk_sb = colf32(NCH); s.bm_sb = colf32(NCH)
        s.bout_sb = colf32(NCH)
        s.s1_sb = colf32(ICH); s.s2_sb = colf32(ICH)
        s.sh1_sb = colf32(ICH); s.sh2_sb = colf32(ICH); s.s3_sb = colf32(NCH)
        s.bqs_sb = colf32(NCH)
        s.bv_row = p[0:1, ro // 4: ro // 4 + D]; ro += D * 4
        s.mrow_off = ro
        s.mrow = p[0:1, ro // 4: ro // 4 + T]; ro += T * 4
        s.rrow = p[0:1, ro // 4: ro // 4 + T]; ro += T * 4
        assert ro <= s.spans["rows"][0] + 24576

        s.ldh(s.ident, s.identd)
        s.ldh(s.identb, s.identbd)
        s.ldh(s.triu, s.triud)
        for dst, src in ((s.bq_sb, s.bq), (s.bk_sb, s.bk), (s.bm_sb, s.bm),
                         (s.bout_sb, s.bout), (s.s1_sb, s.s1c), (s.s2_sb, s.s2c),
                         (s.sh1_sb, s.sh1c), (s.sh2_sb, s.sh2c), (s.s3_sb, s.s3c),
                         (s.bv_row, s.bv)):
            s.ldh(dst, src)

    def transpose_128(self, src_f32, dst, kidx_sem_base, k, evac_eng="vector", bf=False):
        s = self
        bank = s.psum[k % 2]
        s.wait("tensor", "pst", kidx_sem_base + k - 1)
        ident = s.identb if bf else s.ident
        s.op("tensor", lambda e, b=bank, sr=src_f32, idn=ident:
             e.transpose(out=b[:, 0:128], in_=sr, identity=idn), inc="stt")
        s.wait(evac_eng, "stt", s.val("stt"))
        s.op(evac_eng, lambda e, b=bank, d=dst: e.tensor_copy(out=d, in_=b[:, 0:128]),
             inc="pst")

    def p0_load_x(self):
        s = self
        s.alloc("stg", T * NCH * 4, at=R_D)
        x_tm = s.f32("stg")
        pre = s.val("ldh")
        xsrc = s.x_in.rearrange("(a p) d -> p a d", p=128)
        wave_marks = []
        for wv in range(4):
            s.wait("sync", "ldh", s.val("ldh"))
            s.ldh(x_tm[:, wv * 4 * D:(wv + 1) * 4 * D], xsrc[:, wv * 4:(wv + 1) * 4, :])
            wave_marks.append(s.val("ldh"))
        s.wait("vector", "ldh", pre)
        s.op("vector", lambda e: e.memset(s.ones_col, 1.0))
        s.op("vector", lambda e: e.memset(s.ones_row, 1.0))
        s.op("vector", lambda e: e.memset(s.onesb_col, 1.0))
        s.op("vector", lambda e: e.memset(s.ones128, 1.0))
        s.op("vector", lambda e: e.memset(s.halfcol, 0.5))
        s.op("vector", lambda e: e.tensor_scalar(out=s.bqs_sb, in0=s.bq_sb,
             scalar1=0.125, scalar2=None, op0=ALU.mult), inc="sv")
        base = s.val("pst")
        for a in range(16):
            if a % 4 == 0:
                s.wait("tensor", "ldh", wave_marks[a // 4])
            for c in range(NCH):
                src = x_tm[:, a * D + c * 128: a * D + (c + 1) * 128]
                dst = s.xc(c)[:, a * 128:(a + 1) * 128]
                s.transpose_128(src, dst, base, a * NCH + c)
        self.x_loaded_pst = s.val("pst")

    def stats_mb_rb(self, mb_ap, rb_ap, sq0, sq1):
        s = self
        sq = [sq0, sq1]
        chunk_mark = []
        for c in range(NCH):
            buf = sq[c % 2]
            if c >= 2:
                s.wait("scalar", "stt", chunk_mark[c - 2])
            s.op("scalar", lambda e, c=c, b=buf: e.activation(out=b, in_=s.xc(c), func=AF.Square), inc="ss")
            s.wait("tensor", "ss", s.val("ss"))
            for j in range(NT):
                s.op("tensor", lambda e, c=c, j=j: e.matmul(
                    out=s.psum[j][0:1, 0:512],
                    lhsT=s.ones_col, rhs=s.xc(c)[:, j * 512:(j + 1) * 512],
                    start=(c == 0), stop=(c == NCH - 1)))
            for j in range(NT):
                s.op("tensor", lambda e, c=c, j=j, b=buf: e.matmul(
                    out=s.psum[4 + j][0:1, 0:512],
                    lhsT=s.ones_col, rhs=b[:, j * 512:(j + 1) * 512],
                    start=(c == 0), stop=(c == NCH - 1)),
                    inc="stt" if j == NT - 1 else None)
            chunk_mark.append(s.val("stt"))
        tstats = s.val("stt")
        s.wait("vector", "stt", tstats)
        for j in range(NT):
            s.op("vector", lambda e, j=j: e.tensor_scalar(
                out=s.mrow[:, j * 512:(j + 1) * 512], in0=s.psum[j][0:1, 0:512],
                scalar1=1.0 / D, scalar2=None, op0=ALU.mult))
            s.op("vector", lambda e, j=j: e.tensor_scalar(
                out=s.rrow[:, j * 512:(j + 1) * 512], in0=s.psum[4 + j][0:1, 0:512],
                scalar1=1.0 / D, scalar2=None, op0=ALU.mult))
        s.chain("vector")
        s.op("vector", lambda e: e.tensor_tensor(
            out=sq[0][0:1, 0:T], in0=s.mrow, in1=s.mrow, op=ALU.mult))
        s.chain("vector")
        s.op("vector", lambda e: e.tensor_tensor(
            out=s.rrow, in0=s.rrow, in1=sq[0][0:1, 0:T], op=ALU.subtract), inc="sv")
        s.chain("vector")
        s.op("vector", lambda e: e.tensor_scalar(out=s.rrow, in0=s.rrow,
             scalar1=float(LNEPS), scalar2=None, op0=ALU.add), inc="sv")
        s.wait("scalar", "sv", s.val("sv"))
        s.op("scalar", lambda e: e.activation(out=s.rrow, in_=s.rrow, func=AF.Sqrt), inc="ss")
        s.wait("vector", "ss", s.val("ss"))
        s.op("vector", lambda e: e.reciprocal(out=s.rrow, in_=s.rrow), inc="sv")
        svr = s.val("sv")
        s.wait("tensor", "sv", svr)
        for j in range(NT):
            bm_, br_ = s.psum[j % 2], s.psum[2 + j % 2]
            s.op("tensor", lambda e, j=j, b=bm_: e.matmul(
                out=b[:, 0:512], lhsT=s.ones_row,
                rhs=s.mrow[:, j * 512:(j + 1) * 512], start=True, stop=True))
            s.op("tensor", lambda e, j=j, b=br_: e.matmul(
                out=b[:, 0:512], lhsT=s.ones_row,
                rhs=s.rrow[:, j * 512:(j + 1) * 512], start=True, stop=True), inc="stt")
            s.wait("vector", "stt", s.val("stt"))
            s.op("vector", lambda e, j=j, b=bm_: e.tensor_copy(
                out=mb_ap[:, j * 512:(j + 1) * 512], in_=b[:, 0:512]))
            s.op("vector", lambda e, j=j, b=br_: e.tensor_copy(
                out=rb_ap[:, j * 512:(j + 1) * 512], in_=b[:, 0:512]), inc="pst")
            s.wait("tensor", "pst", s.val("pst"))

    def round_pair(self, lanes):
        s = self
        tm = s.chain("vector")
        s.wait("scalar", tm[0], tm[1])
        smarks, vmarks = {}, {}
        for li, (t, v, d, o) in enumerate(lanes):
            s.op("scalar", lambda e, t=t, d=d: e.activation(
                out=d, in_=t, func=AF.Sign, bias=s.halfcol, scale=1.0),
                inc="ss")
            smarks[li] = s.val("ss")
        for li, (t, v, d, o) in enumerate(lanes):
            s.wait("vector", "ss", smarks[li])
            s.op("vector", lambda e, t=t, v=v, d=d: e.scalar_tensor_tensor(
                out=v, in0=d, scalar=-64.0, in1=t, op0=ALU.mult, op1=ALU.add),
                inc="vch")
            vmarks[li] = s.val("vch")
        for k in range(6, -1, -1):
            for li, (t, v, d, o) in enumerate(lanes):
                s.wait("scalar", "vch", vmarks[li])
                s.op("scalar", lambda e, v=v, d=d: e.activation(
                    out=d, in_=v, func=AF.Sign, bias=s.halfcol, scale=1.0),
                    inc="ss")
                smarks[li] = s.val("ss")
            for li, (t, v, d, o) in enumerate(lanes):
                s.wait("vector", "ss", smarks[li])
                s.op("vector", lambda e, v=v, d=d, k=k: e.scalar_tensor_tensor(
                    out=v, in0=d, scalar=-float(2.0 ** (k - 1)), in1=v,
                    op0=ALU.mult, op1=ALU.add), inc="vch")
                vmarks[li] = s.val("vch")
        for li, (t, v, d, o) in enumerate(lanes):
            s.wait("vector", "vch", vmarks[li])
            s.op("vector", lambda e, t=t, v=v, o=o: e.scalar_tensor_tensor(
                out=o, in0=t, scalar=-0.5, in1=v, op0=ALU.add,
                op1=ALU.subtract), inc="sv")
        s.chain("vector")

    def p_ln(self, layer):
        s = self
        s.phase_barrier()
        s.alloc("mb", T * 4, at=R_D)
        s.alloc("rb", T * 4, at=R_D + 8 * KB)
        s.alloc("sq0", T * 4, at=R_D + 16 * KB)
        s.alloc("sq1", T * 4, at=R_D + 24 * KB)
        s.alloc("xn", T * NCH * 2, at=R_D + 34 * KB)
        s.xn_ap, s.mb_ap, s.rb_ap = s.bf16("xn"), s.f32("mb"), s.f32("rb")
        sq = [s.f32("sq0"), s.f32("sq1")]


        chunk_mark = []
        for c in range(NCH):
            buf = sq[c % 2]
            if c >= 2:
                s.wait("scalar", "stt", chunk_mark[c - 2])
            s.op("scalar", lambda e, c=c, b=buf: e.activation(out=b, in_=s.xc(c), func=AF.Square), inc="ss")
            s.wait("tensor", "ss", s.val("ss"))
            for j in range(NT):
                s.op("tensor", lambda e, c=c, j=j: e.matmul(
                    out=s.psum[j][0:1, 0:512],
                    lhsT=s.ones_col, rhs=s.xc(c)[:, j * 512:(j + 1) * 512],
                    start=(c == 0), stop=(c == NCH - 1)))
            for j in range(NT):
                s.op("tensor", lambda e, c=c, j=j, b=buf: e.matmul(
                    out=s.psum[4 + j][0:1, 0:512],
                    lhsT=s.ones_col, rhs=b[:, j * 512:(j + 1) * 512],
                    start=(c == 0), stop=(c == NCH - 1)),
                    inc="stt" if j == NT - 1 else None)
            chunk_mark.append(s.val("stt"))
        tstats = s.val("stt")

        s.wait("vector", "stt", tstats)
        for j in range(NT):
            s.op("vector", lambda e, j=j: e.tensor_scalar(
                out=s.mrow[:, j * 512:(j + 1) * 512], in0=s.psum[j][0:1, 0:512],
                scalar1=1.0 / D, scalar2=None, op0=ALU.mult))
            s.op("vector", lambda e, j=j: e.tensor_scalar(
                out=s.rrow[:, j * 512:(j + 1) * 512], in0=s.psum[4 + j][0:1, 0:512],
                scalar1=1.0 / D, scalar2=None, op0=ALU.mult))
        s.chain("vector")
        s.op("vector", lambda e: e.tensor_tensor(
            out=sq[0][0:1, 0:T], in0=s.mrow, in1=s.mrow, op=ALU.mult))
        s.chain("vector")
        s.op("vector", lambda e: e.tensor_tensor(
            out=s.rrow, in0=s.rrow, in1=sq[0][0:1, 0:T], op=ALU.subtract), inc="sv")
        s.chain("vector")
        s.op("vector", lambda e: e.tensor_scalar(out=s.rrow, in0=s.rrow,
             scalar1=float(LNEPS), scalar2=None, op0=ALU.add), inc="sv")
        s.wait("scalar", "sv", s.val("sv"))
        s.op("scalar", lambda e: e.activation(out=s.rrow, in_=s.rrow, func=AF.Sqrt), inc="ss")
        s.wait("vector", "ss", s.val("ss"))
        s.op("vector", lambda e: e.reciprocal(out=s.rrow, in_=s.rrow), inc="sv")
        svr = s.val("sv")
        s.wait("tensor", "sv", svr)
        for j in range(NT):
            bm_, br_ = s.psum[j % 2], s.psum[2 + j % 2]
            s.op("tensor", lambda e, j=j, b=bm_: e.matmul(
                out=b[:, 0:512], lhsT=s.ones_row,
                rhs=s.mrow[:, j * 512:(j + 1) * 512], start=True, stop=True))
            s.op("tensor", lambda e, j=j, b=br_: e.matmul(
                out=b[:, 0:512], lhsT=s.ones_row,
                rhs=s.rrow[:, j * 512:(j + 1) * 512], start=True, stop=True), inc="stt")
            s.wait("vector", "stt", s.val("stt"))
            s.op("vector", lambda e, j=j, b=bm_: e.tensor_copy(
                out=s.mb_ap[:, j * 512:(j + 1) * 512], in_=b[:, 0:512]))
            s.op("vector", lambda e, j=j, b=br_: e.tensor_copy(
                out=s.rb_ap[:, j * 512:(j + 1) * 512], in_=b[:, 0:512]), inc="pst")
            if j >= 1:
                s.wait("tensor", "pst", s.val("pst") - 1)
        s.chain("vector")
        for c in range(NCH):
            buf = sq[c % 2]
            s.op("vector", lambda e, c=c, b=buf: e.tensor_tensor(
                out=b, in0=s.xc(c), in1=s.mb_ap, op=ALU.subtract))
            s.chain("vector")
            s.op("vector", lambda e, c=c, b=buf: e.tensor_tensor(
                out=s.xn_ap[:, c * T:(c + 1) * T], in0=b, in1=s.rb_ap,
                op=ALU.mult), inc="sv")
            s.chain("vector")
        s.ln_done_sv = s.val("sv")
        if s.dump == f"xn{layer}":
            s.wait("sync", "sv", s.ln_done_sv)
            s.sth(s.dbg, s.xn_ap)

    def p_l0_qkv(self):
        s = self
        s.phase_barrier()
        s.sth(s.xpark, s.f32("x"))
        s.x_parked = True
        park_v = s.val("st")
        s.wait("scalar", "st", park_v)
        s.wait("vector", "st", park_v)
        s.alloc("q", T * NCH * 2, at=R_X)
        s.alloc("k", T * NCH * 2, at=R_X + 32 * KB)
        s.alloc("v", 16 * 1040 * 2, at=R_D)
        s.alloc("wbufA", D * NCH * 2, at=R_E + 2 * KB)
        s.alloc("wbufB", D * NCH * 2, at=R_E + 18 * KB)
        s.alloc("bvb", D * 4, at=R_E + 34 * KB)
        q, k, v = s.bf16("q"), s.bf16("k"), s.bf16("v")
        s.q_ap, s.k_ap, s.v_ap = q, k, v
        wA, wB = s.bf16("wbufA"), s.bf16("wbufB")
        bvb = s.f32("bvb")

        s.ldg(wA, s.Wq.rearrange("(c p) o -> p c o", p=128), sem="ldgA")
        ldq = ("ldgA", s.val("ldgA"))
        s.ldg(wB, s.Wk.rearrange("(c p) o -> p c o", p=128), sem="ldgB")
        ldk = ("ldgB", s.val("ldgB"))

        for half in range(2):
            b = s.psum[6 + half]
            s.op("tensor", lambda e, h=half, b=b: e.matmul(
                out=b[:, 0:512], lhsT=s.ones_row, rhs=s.bv_row[:, h * 512:(h + 1) * 512],
                start=True, stop=True), inc="stt")
        s.wait("vector", "stt", s.val("stt"))
        for half in range(2):
            s.op("vector", lambda e, h=half: e.tensor_copy(
                out=bvb[:, h * 512:(h + 1) * 512], in_=s.psum[6 + h][:, 0:512]), inc="pst")
        s.op("vector", lambda e: e.memset(v[:, 0:16 * 1040], 1.0))
        s.chain("vector")
        bvb_pst = s.val("pst")

        def proj_fm(dst, wbuf, bias_col, scale, ld_mark, eng):
            s.wait("tensor", ld_mark[0], ld_mark[1])
            pvbase = s.val("pss")
            kidx = 0
            for oc in range(NCH):
                for j in range(NT):
                    bank = s.psum[kidx % 2]
                    s.wait("tensor", "pss", pvbase + kidx - 1)
                    for kc in range(NCH):
                        s.op("tensor", lambda e, oc=oc, j=j, kc=kc, b=bank, w=wbuf: e.matmul(
                            out=b[:, 0:512],
                            lhsT=w[:, kc * D + oc * 128:kc * D + (oc + 1) * 128],
                            rhs=s.xn_ap[:, kc * T + j * 512:kc * T + (j + 1) * 512],
                            start=(kc == 0), stop=(kc == NCH - 1)),
                            inc="stt" if kc == NCH - 1 else None)
                    s.wait(eng, "stt", s.val("stt"))
                    if eng == "scalar":
                        s.op("scalar", lambda e, oc=oc, j=j, b=bank, bc=bias_col, sc=scale: e.activation(
                            out=dst[:, oc * T + j * 512:oc * T + (j + 1) * 512],
                            in_=b[:, 0:512], func=AF.Identity,
                            bias=bc[:, oc:oc + 1], scale=sc), inc="pss")
                    for _ in range(2):
                        if s.prepass_gen is None or s.qkv_budget <= 0:
                            break
                        s.qkv_budget -= 1
                        if next(s.prepass_gen, "END") == "END":
                            s.prepass_gen = None
                    kidx += 1
            return s.val("pss")

        proj_fm(q, wA, s.bqs_sb, 0.125, ldq, "scalar")
        mark = s.chain("tensor")
        s.wait("gpsimd", mark[0], mark[1])
        s.ldg(wA, s.Wv.rearrange("(c p) o -> p c o", p=128), sem="ldgA")
        ldv = ("ldgA", s.val("ldgA"))
        proj_fm(k, wB, s.bk_sb, 1.0, ldk, "scalar")

        s.wait("tensor", ldv[0], ldv[1])
        s.wait("vector", "pst", bvb_pst)
        pvbase = s.val("pst")
        kidx = 0
        for a in range(16):
            for half in range(2):
                bank = s.psum[2 + kidx % 2]
                s.wait("tensor", "pst", pvbase + kidx - 1)
                for kc in range(NCH):
                    s.op("tensor", lambda e, a=a, hf=half, kc=kc, b=bank: e.matmul(
                        out=b[:, 0:512],
                        lhsT=s.xn_ap[:, kc * T + a * 128:kc * T + (a + 1) * 128],
                        rhs=wA[:, kc * D + hf * 512:kc * D + (hf + 1) * 512],
                        start=(kc == 0), stop=(kc == NCH - 1)),
                        inc="stt" if kc == NCH - 1 else None)
                s.wait("vector", "stt", s.val("stt"))
                for hh in range(8):
                    head = half * 8 + hh
                    s.op("vector", lambda e, a=a, hf=half, hh=hh, head=head, b=bank: e.tensor_tensor(
                        out=v[:, a * 1040 + 65 * head: a * 1040 + 65 * head + 64],
                        in0=b[:, hh * 64:(hh + 1) * 64],
                        in1=bvb[:, hf * 512 + hh * 64: hf * 512 + (hh + 1) * 64],
                        op=ALU.add), inc="pst" if hh == 7 else None)
                kidx += 1
        s.qkv_done = s.val("pst")
        if s.dump in ("q0", "k0", "v0"):
            tgt = {"q0": q, "k0": k, "v0": v}[s.dump]
            s.wait("sync", "pst", s.val("pst"))
            s.wait("sync", "pss", s.val("pss"))
            s.sth(s.dbg.rearrange("p n -> p n")[:, 0:tgt.shape[1]], tgt)


    def p_l0_attn(self):
        s = self
        s.phase_barrier()
        s.alloc("wm", D * NCH * 2, at=R_D + 33 * KB)
        s.alloc("probsA", T * 2, at=R_D + 49 * KB)
        s.alloc("probsB", T * 2, at=R_D + 56 * KB)
        s.alloc("rc", 4, at=R_D + 53 * KB)
        s.alloc("atm", 1024 * 2, at=R_D + 54 * KB)
        s.alloc("afm", 512 * NCH * 2, at=R_E + 2 * KB)
        s.alloc("ost", 8 * 512 * 4, at=R_E + 10 * KB)
        q, k, v = s.q_ap, s.k_ap, s.v_ap
        wm = s.bf16("wm")
        probs2 = [s.bf16("probsA"), s.bf16("probsB")]
        rc = s.f32("rc")
        atm = s.bf16("atm")
        afm = s.bf16("afm")
        ost = s.f32("ost")

        s.ldg(wm, s.Wm.rearrange("(c p) o -> p c o", p=128), sem="ldgB")
        s.wait("tensor", "ldgB")

        def qh(h, cols):
            return q[:, (h // 2) * T:][:, cols][64 * (h % 2):64 * (h % 2) + 64, :]

        def kh(h, cols):
            return k[:, (h // 2) * T:][:, cols][64 * (h % 2):64 * (h % 2) + 64, :]

        exp_marks = {}
        pv_marks = []
        pv_stt = {}
        sv_marks = {}
        for qt in range(16):
            nblk = qt + 1
            s.wait("tensor", "pst", s.val("pst"))
            pend = None
            for h in range(H + 1):
                if h < H:
                    probsT = probs2[h % 2]
                    half1 = min(nblk, 8)
                    for hf in range(2 if nblk > 8 else 1):
                        b0, b1 = (0, half1) if hf == 0 else (8, nblk)
                        if exp_marks.get(hf) is not None:
                            s.wait("tensor", exp_marks[hf][0], exp_marks[hf][1])
                        for kt in range(b0, b1):
                            bank = s.psum[2 + kt // 4]
                            st_ = (kt % 4 == 0)
                            sp_ = (kt % 4 == 3 or kt == nblk - 1)
                            s.op("tensor", lambda e, h=h, qt=qt, kt=kt, b=bank,
                                 st_=st_, sp_=sp_: e.matmul(
                                out=b[:, (kt % 4) * 128:(kt % 4) * 128 + 128],
                                lhsT=kh(h, slice(kt * 128, (kt + 1) * 128)),
                                rhs=qh(h, slice(qt * 128, (qt + 1) * 128)),
                                start=st_, stop=sp_),
                                inc="stt" if kt == b1 - 1 else None)
                        s.wait("scalar", "stt", s.val("stt"))
                        if hf == 0 and pv_stt.get(h % 2) is not None:
                            s.wait("scalar", "stt", pv_stt[h % 2])
                        w0, wq = b0 * 128, (b1 - b0) * 128
                        s.op("scalar", lambda e, w0=w0, wq=wq, pt=probsT: e.activation(
                            out=pt[:, w0:w0 + wq],
                            in_=s.psumbig.ap()[:, 1024 + w0:1024 + w0 + wq],
                            func=AF.Exp), inc="pss")
                        exp_marks[hf] = ("pss", s.val("pss"))
                    s.wait("vector", "pss", s.val("pss"))
                    s.op("vector", lambda e, qt=qt, pt=probsT: e.tensor_tensor(
                        out=pt[:, qt * 128:(qt + 1) * 128],
                        in0=pt[:, qt * 128:(qt + 1) * 128], in1=s.triu,
                        op=ALU.mult), inc="sv")
                    sv_marks[h] = ("sv", s.val("sv"))
                if pend is not None:
                    ph, pprobs = pend
                    s.wait("tensor", sv_marks[ph][0], sv_marks[ph][1])
                    pvb = s.psum[6 + ph % 2]
                    if len(pv_marks) >= 2:
                        s.wait("tensor", "pst", pv_marks[-2])
                    for kt in range(nblk):
                        sp_ = (kt == nblk - 1)
                        s.op("tensor", lambda e, ph=ph, kt=kt, b=pvb, sp_=sp_,
                             pt=pprobs: e.matmul(
                            out=b[:, 0:65],
                            lhsT=pt[:, kt * 128:(kt + 1) * 128],
                            rhs=v[:, kt * 1040 + 65 * ph: kt * 1040 + 65 * ph + 65],
                            start=(kt == 0), stop=sp_),
                            inc="stt" if sp_ else None)
                    pv_stt[ph % 2] = s.val("stt")
                    s.wait("vector", "stt", s.val("stt"))
                    s.wait("vector", "pst", s.val("pst"))
                    s.op("vector", lambda e, b=pvb: e.reciprocal(out=rc, in_=b[:, 64:65]))
                    s.chain("vector")
                    s.op("vector", lambda e, ph=ph, b=pvb: e.tensor_scalar(
                        out=atm[:, ph * 64:(ph + 1) * 64], in0=b[:, 0:64],
                        scalar1=rc, scalar2=None, op0=ALU.mult), inc="pst")
                    pv_marks.append(s.val("pst"))
                    if s.prepass_gen is not None:
                        if next(s.prepass_gen, "END") == "END":
                            s.prepass_gen = None
                        if s.bg_pst_mark is not None:
                            pv_marks += [s.bg_pst_mark, s.bg_pst_mark]
                            s.bg_pst_mark = None
                pend = (h, probs2[h % 2]) if h < H else None

            s.chain("vector")
            for pc in range(NCH):
                bank = s.psum[pc % 2]
                bb = bank.bitcast(BF16)
                if pc >= 2:
                    s.wait("tensor", "pst", s.val("pst") - 1)
                s.op("tensor", lambda e, pc=pc, b=bb: e.transpose(
                    out=b[:, 0:128], in_=atm[:, pc * 128:(pc + 1) * 128],
                    identity=s.identb), inc="stt")
                s.wait("vector", "stt", s.val("stt"))
                s.op("vector", lambda e, pc=pc, qt=qt, b=bb: e.tensor_copy(
                    out=afm[:, pc * 512 + (qt % 4) * 128:
                            pc * 512 + (qt % 4) * 128 + 128],
                    in_=b[:, 0:128]), inc="pst")

            if qt % 4 == 3:
                ntg = qt // 4
                s.chain("vector")
                s.wait("vector", "st", s.val("st"))
                s.wait("tensor", "pst", s.val("pst"))
                for oc in range(NCH):
                    bank = s.psum[oc]
                    for kc in range(NCH):
                        s.op("tensor", lambda e, oc=oc, kc=kc, b=bank: e.matmul(
                            out=b[:, 0:512],
                            lhsT=wm[:, kc * D + oc * 128:kc * D + (oc + 1) * 128],
                            rhs=afm[:, kc * 512:(kc + 1) * 512],
                            start=(kc == 0), stop=(kc == NCH - 1)),
                            inc="stt" if kc == NCH - 1 else None)
                    s.wait("vector", "stt", s.val("stt"))
                    s.op("vector", lambda e, oc=oc, b=bank: e.tensor_scalar(
                        out=ost[:, oc * 512:(oc + 1) * 512], in0=b[:, 0:512],
                        scalar1=s.bm_sb[:, oc:oc + 1], scalar2=None,
                        op0=ALU.add), inc="pst")
                s.wait("sync", "pst", s.val("pst"))
                s.sth(s.attn_s.rearrange("p (c t) -> p c t", c=NCH)[:, :, ntg * 512:(ntg + 1) * 512],
                      ost.rearrange("p (c t) -> p c t", c=NCH))
                s.wait("tensor", "pst", s.val("pst"))
                exp_marks = {}
                s.wait("tensor", "pss", s.val("pss"))
        s.attn_stores = s.val("st")
        s.pending_adds.append(s.attn_s)

    def prepass_generator(self):
        s = self
        p = s.pool.ap()
        co = s.spans["consts"][0]
        base = co + 2560
        thr_cols, nthr_cols, mw_cols = [], [], []
        for i in range(3):
            thr_cols.append(p[:, (base + i * 12) // 4:][:, :1])
            nthr_cols.append(p[:, (base + i * 12 + 4) // 4:][:, :1])
            mw_cols.append(p[:, (base + i * 12 + 8) // 4:][:, :1])
        accc = p[:, (base + 40) // 4:][:, :1]
        cell = p[0:1, (base + 44) // 4:][:, :1]
        cell2 = p[0:1, (base + 48) // 4:][:, :1]
        redc = p[:, (base + 52) // 4:][:, :1]
        s.thr_cols, s.nthr_cols, s.mw_cols = thr_cols, nthr_cols, mw_cols
        mroff = s.mrow_off
        ws = [p[:, (mroff + i * 4 * KB) // 4:][:, :1024] for i in range(3)]
        gA = p[:, (198 * KB) // 4:][:, :1024]
        pb = s.pool.ap().bitcast(BF16)
        iqb = [pb[:, (194 * KB) // 2:][:, :1024], pb[:, (202 * KB) // 2:][:, :1024]]
        sems3 = ("ldgA", "ldgB", "ldgC")

        accc_w = [accc, p[:, (base + 60) // 4:][:, :1], p[:, (base + 64) // 4:][:, :1]]
        weights = [(s.wgT, s.iwg_s, 8, 4, D * INNER),
                   (s.wuT, s.iwu_s, 8, 4, D * INNER),
                   (s.wdT, s.iwd_s, 32, 1, D * INNER)]
        chunks = []
        for ps in (1, 2):
            for wi in range(3):
                nn, no = weights[wi][2], weights[wi][3]
                for n in range(nn):
                    for o in range(no):
                        chunks.append((wi, n, o))
        NLC = len(chunks)

        def chunk_ap(t, wi, n, o):
            r = t.rearrange("(n p) o -> p n o", p=128)
            return r[:, n, :][:, o * 1024:(o + 1) * 1024]

        state = {"issued": 0}
        load_marks = [None] * NLC
        cons_marks = [None, None, None]
        st_marks = [None, None]

        def issue_one():
            i = state["issued"]
            if i >= NLC:
                return
            b = i % 3
            wi, n, o = chunks[i]
            if cons_marks[b] is not None:
                s.wait("gpsimd", cons_marks[b][0], cons_marks[b][1])
            pv = s.val(sems3[b])
            if pv > 0:
                s.wait("gpsimd", sems3[b], pv)
            s.ldg(ws[b], chunk_ap(weights[wi][0], wi, n, o), sem=sems3[b])
            load_marks[i] = (sems3[b], s.val(sems3[b]))
            state["issued"] = i + 1

        consumed = 0
        for wi in range(3):
            wsrc, wdst, nn, no, nel = weights[wi]
            acw = accc_w[wi]
            for ci in range(nn * no):
                while state["issued"] - consumed < 3 and state["issued"] < NLC:
                    issue_one()
                b = consumed % 3
                m = load_marks[consumed]
                s.wait("vector", m[0], m[1])
                if ci == 0:
                    s.op("vector", lambda e, bb=ws[b], acw=acw: e.tensor_reduce(
                        out=acw, in_=bb, axis=AX.X, op=ALU.add,
                        apply_absolute_value=True))
                else:
                    s.op("vector", lambda e, bb=ws[b]: e.tensor_reduce(
                        out=redc, in_=bb, axis=AX.X, op=ALU.add,
                        apply_absolute_value=True))
                    s.chain("vector")
                    s.op("vector", lambda e, acw=acw: e.tensor_tensor(
                        out=acw, in0=acw, in1=redc, op=ALU.add))
                cons_marks[b] = s.chain("vector")
                consumed += 1
                yield
        for wi in range(3):
            wsrc, wdst, nn, no, nel = weights[wi]
            acw = accc_w[wi]
            s.wait("tensor", "vch", s.val("vch"))
            s.wait("tensor", "pst", s.val("pst"))
            s.op("tensor", lambda e, acw=acw: e.matmul(
                out=s.psum[7][0:1, 0:1], lhsT=acw, rhs=s.ones_col,
                start=True, stop=True), inc="stt")
            s.wait("vector", "stt", s.val("stt"))
            s.op("vector", lambda e, n=nel: e.tensor_scalar(
                out=cell, in0=s.psum[7][0:1, 0:1], scalar1=1.0 / n,
                scalar2=1e-5, op0=ALU.mult, op1=ALU.max))
            s.chain("vector")
            s.op("vector", lambda e: e.tensor_scalar(
                out=cell2, in0=cell, scalar1=0.5, scalar2=None,
                op0=ALU.mult), inc="sv")
            s.wait("tensor", "sv", s.val("sv"))
            s.op("tensor", lambda e: e.matmul(
                out=s.psum[7][:, 1:2], lhsT=s.ones_row, rhs=cell2,
                start=True, stop=True))
            s.op("tensor", lambda e: e.matmul(
                out=s.psum[7][:, 2:3], lhsT=s.ones_row, rhs=cell,
                start=True, stop=True), inc="stt")
            s.wait("vector", "stt", s.val("stt"))
            s.op("vector", lambda e, wi=wi: e.tensor_copy(
                out=thr_cols[wi], in_=s.psum[7][:, 1:2]))
            s.op("vector", lambda e, wi=wi: e.tensor_copy(
                out=mw_cols[wi], in_=s.psum[7][:, 2:3]))
            s.chain("vector")
            s.op("vector", lambda e, wi=wi: e.tensor_scalar(
                out=nthr_cols[wi], in0=thr_cols[wi], scalar1=-1.0,
                scalar2=None, op0=ALU.mult), inc="pst")
            s.bg_pst_mark = s.val("pst")
            nthr_mark = ("pst", s.val("pst"))
            yield
            for ci in range(nn * no):
                while state["issued"] - consumed < 3 and state["issued"] < NLC:
                    issue_one()
                b = consumed % 3
                qb = consumed % 2
                wi2, n, o = chunks[consumed]
                m = load_marks[consumed]
                for hf in range(2):
                    cs = slice(hf * 512, hf * 512 + 512)
                    s.wait("vector", m[0], m[1])
                    if ci == 0 and hf == 0:
                        s.wait("vector", nthr_mark[0], nthr_mark[1])
                    if hf == 0 and st_marks[qb] is not None:
                        s.wait("vector", "st", st_marks[qb])
                    s.op("vector", lambda e, bb=ws[b], wi=wi, cs=cs: e.tensor_scalar(
                        out=gA[:, cs], in0=bb[:, cs], scalar1=nthr_cols[wi],
                        scalar2=None, op0=ALU.is_lt))
                    s.chain("vector")
                    s.op("vector", lambda e, bb=ws[b], wi=wi, q=qb, cs=cs:
                         e.scalar_tensor_tensor(
                            out=iqb[q][:, cs], in0=bb[:, cs], scalar=thr_cols[wi],
                            in1=gA[:, cs], op0=ALU.is_gt, op1=ALU.subtract))
                    cmk = s.chain("vector")
                    if hf == 1:
                        cons_marks[b] = cmk
                        s.wait("sync", cmk[0], cmk[1])
                        s.sth(chunk_ap(wdst, wi, n, o), iqb[qb])
                        st_marks[qb] = s.val("st")
                        consumed += 1
                    yield

    def p_l0_ffn(self):
        s = self
        s.phase_barrier()
        s.alloc("mb2", T * 4, at=R_D)
        s.alloc("rb2", T * 4, at=R_D + 8 * KB)
        s.alloc("oscx", 512 * 4, at=R_D + 16 * KB)
        s.alloc("scx", 512 * 4, at=R_D + 18 * KB)
        s.alloc("ixs", 8 * 512 * 2, at=R_D + 20 * KB)
        s.alloc("am", 512 * 4, at=R_D + 28 * KB)
        s.alloc("cm", 512 * 4, at=R_D + 30 * KB)
        s.alloc("hbuf", 32 * 512 * 4, at=R_D + 32 * KB)
        s.alloc("scb", 512 * 4, at=188 * KB)
        s.alloc("oscb", 512 * 4, at=190 * KB)
        s.alloc("vt", 1024 * 4, at=192 * KB)
        s.alloc("dt", 1024 * 4, at=196 * KB)
        s.alloc("ihg", 1024 * 2, at=200 * KB)
        s.alloc("rep0", 512, at=202 * KB)
        s.alloc("rep1", 512, at=202 * KB + 512)
        s.alloc("sccols", 64, at=203 * KB)
        mb, rb = s.f32("mb2"), s.f32("rb2")
        oscx, scx = s.f32("oscx"), s.f32("scx")
        ixs = s.bf16("ixs")
        am, cm = s.f32("am"), s.f32("cm")
        hb = s.f32("hbuf")
        scb, oscb = s.f32("scb"), s.f32("oscb")
        vt, dt = s.f32("vt"), s.f32("dt")
        ihg = s.bf16("ihg")
        s.rep_aps = [s.f32("rep0"), s.f32("rep1")]
        s.dbg_sb_tmp = s.bf16("ixs") if False else s.pool.ap().bitcast(BF16)[:, (R_D + 20 * KB) // 2:][:, :16 * 512]
        sc_span = s.f32("sccols")
        s.sccol = sc_span[:, 0:1]
        s.sccol2 = sc_span[:, 1:2]
        p = s.pool.ap()
        mroff = s.mrow_off
        gbuf = [s.pool.ap().bitcast(BF16)[:, (mroff + i * 2048) // 2:][:, :1024]
                for i in range(4)]
        dbuf = [s.pool.ap().bitcast(BF16)[:, (mroff + i * 8192) // 2:][:, :4096]
                for i in range(2)]

        s.ldh(s.f32("x"), s.xpark)
        v = s.val("ldh")
        s.wait("vector", "ldh", v)
        s.wait("scalar", "ldh", v)
        s.wait("tensor", "ldh", v)
        s.x_parked = False

        s.stats_mb_rb(mb, rb, hb[:, 0:T], hb[:, T:2 * T])

        if s.prepass_gen is not None:
            for _ in s.prepass_gen:
                pass
            s.prepass_gen = None
        for sm in ("ldgA", "ldgB", "ldgC", "st"):
            v = s.val(sm)
            s.wait("vector", sm, v)
            s.wait("scalar", sm, v)
            s.wait("sync", sm, v)
            s.wait("tensor", sm, v)
        s.wait("tensor", "pst", s.val("pst"))
        mw_cols = s.mw_cols

        evac_marks = []
        for ti in range(NT):
            tsl = slice(ti * 512, (ti + 1) * 512)
            s.chain("vector")
            for c in range(NCH):
                st_ap = hb[:, c * 512:(c + 1) * 512]
                s.op("vector", lambda e, c=c, a=st_ap, tsl=tsl: e.tensor_tensor(
                    out=a, in0=s.xc(c)[:, tsl], in1=mb[:, tsl], op=ALU.subtract))
                s.chain("vector")
                amn = dt[:, 0:512]
                if c == 0:
                    s.op("vector", lambda e, a=st_ap: e.tensor_copy(out=am, in_=a))
                    s.chain("vector")
                    s.op("vector", lambda e, a=st_ap, amn=amn: e.tensor_copy(out=amn, in_=a))
                else:
                    s.op("vector", lambda e, a=st_ap: e.tensor_tensor(
                        out=am, in0=am, in1=a, op=ALU.max))
                    s.chain("vector")
                    s.op("vector", lambda e, a=st_ap, amn=amn: e.tensor_tensor(
                        out=amn, in0=amn, in1=a, op=ALU.min))
                s.chain("vector")
            amn = dt[:, 0:512]
            s.op("vector", lambda e, amn=amn: e.tensor_scalar(
                out=amn, in0=amn, scalar1=-1.0, scalar2=None, op0=ALU.mult))
            s.chain("vector")
            s.op("vector", lambda e, amn=amn: e.tensor_tensor(
                out=am, in0=am, in1=amn, op=ALU.max))
            amk = s.chain("vector")
            self.scales_from_colmax(am, scx, oscx, amk)
            s.op("vector", lambda e, tsl=tsl: e.tensor_tensor(
                out=oscx, in0=oscx, in1=rb[:, tsl], op=ALU.mult))
            s.chain("vector")
            s.op("vector", lambda e: e.tensor_scalar(
                out=oscx, in0=oscx, scalar1=1e-5, scalar2=1.0 / 127.0,
                op0=ALU.max, op1=ALU.mult))
            s.chain("vector")
            dAx = s.pool.ap().bitcast(BF16)[:, (R_D + 48 * KB) // 2:][:, :1024]
            dBx = s.pool.ap().bitcast(BF16)[:, (R_D + 50 * KB) // 2:][:, :1024]
            for pr in range(2):
                for c in range(4 * pr, 4 * pr + 4):
                    a = hb[:, c * 512:(c + 1) * 512]
                    s.op("vector", lambda e, c=c, a=a: e.tensor_tensor(
                        out=a, in0=a, in1=scx, op=ALU.mult))
                    s.chain("vector")
                g0, g1 = 2 * pr, 2 * pr + 1
                s.round_pair([
                    (hb[:, g0 * 1024:(g0 + 1) * 1024], vt, dAx,
                     ixs[:, g0 * 1024:(g0 + 1) * 1024]),
                    (hb[:, g1 * 1024:(g1 + 1) * 1024], dt, dBx,
                     ixs[:, g1 * 1024:(g1 + 1) * 1024])])
            ix_done = s.chain("vector")
            s.wait("tensor", ix_done[0], ix_done[1])
            s.wait("scalar", ix_done[0], ix_done[1])
            if s.dump == "ix0" and ti == 0:
                s.wait("sync", ix_done[0], ix_done[1])
                s.sth(s.dbg[:, 0:8 * 512], ixs)
                s.wait("vector", "st", s.val("st"))
                s.wait("tensor", "st", s.val("st"))

            s.op("vector", lambda e: e.memset(cm, 0.0))
            s.chain("vector")
            cmn = dt[:, 512:1024]
            s.op("vector", lambda e, cmn=cmn: e.memset(cmn, 0.0))
            s.chain("vector")
            ldmarks = {}
            ssm = {}
            for oc in range(ICH + 1):
                if oc < ICH:
                    gb, ub = gbuf[2 * (oc % 2)], gbuf[2 * (oc % 2) + 1]
                    if oc >= 2:
                        s.wait("sync", ldmarks[oc - 2][0], ldmarks[oc - 2][1])
                    else:
                        mk = s.chain("tensor")
                        s.wait("sync", mk[0], mk[1])
                    psem = "lha" if oc % 2 == 0 else "lhb"
                    s.ldh(gb, s.iwg_s.rearrange("(n p) o -> p n o", p=128)[:, :, oc * 128:(oc + 1) * 128], sem=psem)
                    s.ldh(ub, s.iwu_s.rearrange("(n p) o -> p n o", p=128)[:, :, oc * 128:(oc + 1) * 128], sem=psem)
                    s.wait("tensor", psem, s.val(psem))
                    if len(evac_marks) >= 2:
                        s.wait("tensor", "pst", evac_marks[-2])
                    bg, bu = s.psum[oc % 3], s.psum[3 + oc % 3]
                    for kc in range(NCH):
                        s.op("tensor", lambda e, kc=kc, b=bg, gb=gb: e.matmul(
                            out=b[:, 0:512], lhsT=gb[:, kc * 128:(kc + 1) * 128],
                            rhs=ixs[:, kc * 512:(kc + 1) * 512],
                            start=(kc == 0), stop=(kc == NCH - 1)))
                    for kc in range(NCH):
                        s.op("tensor", lambda e, kc=kc, b=bu, ub=ub: e.matmul(
                            out=b[:, 0:512], lhsT=ub[:, kc * 128:(kc + 1) * 128],
                            rhs=ixs[:, kc * 512:(kc + 1) * 512],
                            start=(kc == 0), stop=(kc == NCH - 1)),
                            inc="stt" if kc == NCH - 1 else None)
                    mk = s.chain("tensor")
                    ldmarks[oc] = mk
                    s.wait("vector", "stt", s.val("stt"))
                    gs = vt[:, (oc % 2) * 512:(oc % 2) * 512 + 512]
                    sil = scb if oc % 2 == 0 else oscb
                    s.op("vector", lambda e, b=bg, gs=gs: e.scalar_tensor_tensor(
                        out=gs, in0=b[:, 0:512], scalar=mw_cols[0], in1=oscx,
                        op0=ALU.mult, op1=ALU.mult), inc="pst")
                    mkv = s.chain("vector")
                    s.wait("scalar", mkv[0], mkv[1])
                    s.op("scalar", lambda e, gs=gs, sil=sil: e.activation(
                        out=sil, in_=gs, func=AF.Silu), inc="ss")
                    ssm[oc] = s.val("ss")
                if oc >= 1:
                    po = oc - 1
                    pbu = s.psum[3 + po % 3]
                    psil = scb if po % 2 == 0 else oscb
                    s.wait("vector", "ss", ssm[po])
                    s.op("vector", lambda e, po=po, b=pbu, sil=psil: e.tensor_tensor(
                        out=hb[:, po * 512:(po + 1) * 512], in0=b[:, 0:512],
                        in1=sil, op=ALU.mult), inc="pst")
                    evac_marks.append(s.val("pst"))
                    s.chain("vector")
                    s.op("vector", lambda e, po=po: e.tensor_tensor(
                        out=cm, in0=cm, in1=hb[:, po * 512:(po + 1) * 512], op=ALU.max))
                    s.chain("vector")
                    s.op("vector", lambda e, po=po, cmn=cmn: e.tensor_tensor(
                        out=cmn, in0=cmn, in1=hb[:, po * 512:(po + 1) * 512], op=ALU.min))
                    s.chain("vector")
            s.op("vector", lambda e, cmn=cmn: e.tensor_scalar(
                out=cmn, in0=cmn, scalar1=-1.0, scalar2=None, op0=ALU.mult))
            s.chain("vector")
            s.op("vector", lambda e, cmn=cmn: e.tensor_tensor(
                out=cm, in0=cm, in1=cmn, op=ALU.max))
            cmk = s.chain("vector")
            if s.dump == "h0" and ti == 0:
                s.wait("sync", cmk[0], cmk[1])
                s.op("vector", lambda e: e.tensor_copy(
                    out=s.dbg_sb_tmp, in_=hb[:, 0:16 * 512]), inc="sv")
                s.chain("vector")
                s.wait("sync", "sv", s.val("sv"))
                s.sth(s.dbg[:, 0:16 * 512], s.dbg_sb_tmp)
                s.wait("vector", "st", s.val("st"))
                s.wait("tensor", "st", s.val("st"))
            self.scales_from_colmax(cm, scb, oscb, cmk)
            s.op("vector", lambda e: e.tensor_tensor(
                out=oscb, in0=oscb, in1=oscx, op=ALU.mult))
            s.chain("vector")
            s.op("vector", lambda e: e.tensor_scalar(
                out=oscb, in0=oscb, scalar1=mw_cols[1], scalar2=1e-5,
                op0=ALU.mult, op1=ALU.max))
            s.chain("vector")
            s.op("vector", lambda e: e.tensor_scalar(
                out=oscb, in0=oscb, scalar1=mw_cols[2], scalar2=1.0 / 127.0,
                op0=ALU.mult, op1=ALU.mult))
            s.chain("vector")

            dstream_marks = {}
            pb_ = s.pool.ap().bitcast(BF16)
            dAh = pb_[:, (R_D + 20 * KB) // 2:][:, :1024]
            dBh = pb_[:, (R_D + 22 * KB) // 2:][:, :1024]
            ihgB = pb_[:, (R_D + 24 * KB) // 2:][:, :1024]
            for g in range(NCH):
                db = dbuf[g % 2]
                if g >= 2:
                    s.wait("sync", dstream_marks[g - 2][0], dstream_marks[g - 2][1])
                else:
                    mk = s.chain("tensor")
                    s.wait("sync", mk[0], mk[1])
                dsem = "lha" if g % 2 == 0 else "lhb"
                s.ldh(db, s.iwd_s.rearrange("(n p) o -> p n o", p=128)[:, 4 * g:4 * g + 4, :], sem=dsem)
                dlw = (dsem, s.val(dsem))
                for kc in range(4 * g, 4 * g + 4):
                    a = hb[:, kc * 512:(kc + 1) * 512]
                    s.op("vector", lambda e, a=a: e.tensor_tensor(
                        out=a, in0=a, in1=scb, op=ALU.mult))
                    s.chain("vector")
                mk = s.chain("tensor")
                s.wait("vector", mk[0], mk[1])
                s.round_pair([
                    (hb[:, (4 * g) * 512:(4 * g + 2) * 512], vt, dAh, ihg),
                    (hb[:, (4 * g + 2) * 512:(4 * g + 4) * 512], dt, dBh, ihgB)])
                rg = s.chain("vector")
                if s.dump == "ih0" and ti == 0 and g == 0:
                    s.wait("sync", rg[0], rg[1])
                    s.sth(s.dbg[:, 0:1024], ihg)
                    s.wait("vector", "st", s.val("st"))
                    s.wait("tensor", "st", s.val("st"))
                s.wait("tensor", rg[0], rg[1])
                s.wait("tensor", dlw[0], dlw[1])
                for oc8 in range(NCH):
                    for kci in range(4):
                        last = (g == NCH - 1 and kci == 3)
                        stt_ = (g == 0 and kci == 0)
                        rhs_ = ihg if kci < 2 else ihgB
                        c2 = kci % 2
                        s.op("tensor", lambda e, oc8=oc8, kci=kci, db=db, c2=c2,
                             rhs_=rhs_, st_=stt_, last=last: e.matmul(
                            out=s.psum[oc8][:, 0:512],
                            lhsT=db[:, kci * 1024 + oc8 * 128:kci * 1024 + (oc8 + 1) * 128],
                            rhs=rhs_[:, c2 * 512:(c2 + 1) * 512],
                            start=st_, stop=last),
                            inc="stt" if (oc8 == NCH - 1 and kci == 3) else None)
                mk = s.chain("tensor")
                dstream_marks[g] = mk
            s.wait("vector", "stt", s.val("stt"))
            if s.dump == "dps0" and ti == 0:
                s.op("vector", lambda e: e.tensor_copy(
                    out=s.dbg_sb_tmp[:, 0:512], in_=s.psum[0][:, 0:512]), inc="sv")
                s.chain("vector")
                s.wait("sync", "sv", s.val("sv"))
                s.sth(s.dbg[:, 0:512], s.dbg_sb_tmp[:, 0:512])
                s.wait("vector", "st", s.val("st"))
            for oc8 in range(NCH):
                s.op("vector", lambda e, oc8=oc8: e.tensor_tensor(
                    out=vt[:, 0:512], in0=s.psum[oc8][:, 0:512], in1=oscb, op=ALU.mult))
                s.chain("vector")
                s.op("vector", lambda e, oc8=oc8, tsl=tsl: e.tensor_tensor(
                    out=s.xc(oc8)[:, tsl], in0=s.xc(oc8)[:, tsl], in1=vt[:, 0:512],
                    op=ALU.add), inc="pst")
                s.chain("vector")
            evac_marks.append(s.val("pst"))
            dn = s.chain("vector")
            s.wait("tensor", dn[0], dn[1])

        mkv = s.chain("vector")
        s.wait("sync", mkv[0], mkv[1])
        s.ldh(hb[:, 0:NCH * T], s.attn_s)
        v2 = s.val("ldh")
        s.wait("vector", "ldh", v2)
        for c in range(NCH):
            s.op("vector", lambda e, c=c: e.tensor_tensor(
                out=s.xc(c), in0=s.xc(c), in1=hb[:, c * T:(c + 1) * T],
                op=ALU.add), inc="sv")
        s.chain("vector")
        s.pending_adds = []

    def scales_from_colmax(self, am_ap, sc_ap, osc_ap, amk):
        s = self
        s.wait("tensor", amk[0], amk[1])
        for t4 in range(4):
            bank = s.psum[4 + t4 % 2]
            s.op("tensor", lambda e, t4=t4, b=bank: e.transpose(
                out=b[:, 0:128], in_=am_ap[:, t4 * 128:(t4 + 1) * 128],
                identity=s.ident), inc="stt")
            s.wait("vector", "stt", s.val("stt"))
            s.op("vector", lambda e, b=bank: e.tensor_reduce(
                out=s.sccol, in_=b[:, 0:128], axis=AX.X, op=ALU.max))
            s.chain("vector")
            s.op("vector", lambda e: e.tensor_copy(out=s.sccol2, in_=s.sccol))
            s.chain("vector")
            s.op("vector", lambda e: e.tensor_scalar(
                out=s.sccol, in0=s.sccol, scalar1=1e-20, scalar2=None, op0=ALU.max))
            s.chain("vector")
            s.op("vector", lambda e: e.reciprocal(out=s.sccol, in_=s.sccol))
            s.chain("vector")
            s.op("vector", lambda e: e.tensor_scalar(
                out=s.sccol, in0=s.sccol, scalar1=127.0, scalar2=None, op0=ALU.mult))
            s.chain("vector")
            s.op("vector", lambda e: e.tensor_scalar(
                out=s.rep_aps[0], in0=s.ones128, scalar1=s.sccol, scalar2=None,
                op0=ALU.mult))
            s.chain("vector")
            s.op("vector", lambda e: e.tensor_scalar(
                out=s.rep_aps[1], in0=s.ones128, scalar1=s.sccol2, scalar2=None,
                op0=ALU.mult), inc="sv")
            rv = s.val("sv")
            s.wait("tensor", "sv", rv)
            b2 = s.psum[6]
            s.op("tensor", lambda e, b=b2: e.transpose(
                out=b[:, 0:128], in_=s.rep_aps[0], identity=s.ident))
            s.op("tensor", lambda e, b=b2: e.transpose(
                out=b[:, 128:256], in_=s.rep_aps[1], identity=s.ident), inc="stt")
            s.wait("vector", "stt", s.val("stt"))
            s.op("vector", lambda e, t4=t4, b=b2: e.tensor_copy(
                out=sc_ap[:, t4 * 128:(t4 + 1) * 128], in_=b[:, 0:128]))
            s.op("vector", lambda e, t4=t4, b=b2: e.tensor_copy(
                out=osc_ap[:, t4 * 128:(t4 + 1) * 128], in_=b[:, 128:256]), inc="pst")
            s.wait("tensor", "pst", s.val("pst"))
        s.chain("vector")


    def p_l1_attn(self):
        s = self
        s.phase_barrier()
        s.sth(s.xpark, s.f32("x"))
        s.x_parked = True
        pk = s.val("st")
        s.wait("scalar", "st", pk)
        s.wait("vector", "st", pk)
        s.alloc("qe", T * NCH * 2, at=R_X)
        s.alloc("ke", 16 * D * 2, at=R_X + 32 * KB)
        s.alloc("v1", 16 * D * 2, at=R_D)
        s.alloc("wstr", D * NCH * 2, at=R_E)
        s.alloc("sel8", 1024 * 4, at=R_E + 16 * KB)
        s.alloc("kvsb", 512 * 2, at=R_E + 20 * KB)
        s.alloc("ks", 8 * 16 * 2, at=R_E + 21 * KB)
        s.alloc("qks", T * 4, at=R_E + 22 * KB)
        s.alloc("rden", T * 4, at=R_E + 30 * KB)
        s.alloc("elu0", 512 * 4, at=R_E + 38 * KB)
        s.alloc("elu1", 512 * 4, at=R_E + 40 * KB)
        s.alloc("denb", 512 * 4, at=R_E + 42 * KB)
        s.alloc("ost2", 8 * 512 * 4, at=R_D)
        qe, ke, v1 = s.bf16("qe"), s.bf16("ke"), s.bf16("v1")
        wstr = s.bf16("wstr")
        sel8 = s.f32("sel8")[0:16, :]
        kvsb, ks = s.bf16("kvsb"), s.bf16("ks")
        qks = s.f32("qks")[0:16, :]
        rden = s.f32("rden")[0:16, :]
        m0, e1 = s.f32("elu0"), s.f32("elu1")
        denb = s.f32("denb")
        s.ldh(sel8, s.sel8d)
        s.wait("tensor", "ldh", s.val("ldh"))

        def elu_evac(bank, dst, w):
            s.op("vector", lambda e, b=bank, w=w: e.tensor_scalar(
                out=m0[:, 0:w], in0=b[:, 0:w], scalar1=0.0, scalar2=None,
                op0=ALU.min), inc="pst")
            mk = s.chain("vector")
            s.wait("scalar", mk[0], mk[1])
            s.op("scalar", lambda e, w=w: e.activation(
                out=e1[:, 0:w], in_=m0[:, 0:w], func=AF.Exp), inc="ss")
            s.wait("vector", "ss", s.val("ss"))
            s.op("vector", lambda e, b=bank, d=dst, w=w: e.scalar_tensor_tensor(
                out=d, in0=b[:, 0:w], scalar=0.0, in1=e1[:, 0:w],
                op0=ALU.max, op1=ALU.add), inc="pst")
            s.chain("vector")

        s.ldg(wstr, s.Wqkv.rearrange("(c p) o -> p c o", p=128)[:, :, 0:D], sem="ldgA")
        s.wait("tensor", "ldgA")
        kidx = 0
        for oc in range(NCH):
            for j in range(NT):
                bank = s.psum[kidx % 3]
                s.wait("tensor", "pst", s.val("pst") - 4)
                for kc in range(NCH):
                    s.op("tensor", lambda e, oc=oc, j=j, kc=kc, b=bank: e.matmul(
                        out=b[:, 0:512],
                        lhsT=wstr[:, kc * D + oc * 128:kc * D + (oc + 1) * 128],
                        rhs=s.xn_ap[:, kc * T + j * 512:kc * T + (j + 1) * 512],
                        start=(kc == 0), stop=(kc == NCH - 1)),
                        inc="stt" if kc == NCH - 1 else None)
                s.wait("vector", "stt", s.val("stt"))
                elu_evac(bank, qe[:, oc * T + j * 512:oc * T + (j + 1) * 512], 512)
                kidx += 1
        mk = s.chain("tensor")
        s.wait("gpsimd", mk[0], mk[1])
        s.ldg(wstr, s.Wqkv.rearrange("(c p) o -> p c o", p=128)[:, :, D:2 * D], sem="ldgB")
        s.wait("tensor", "ldgB")
        kidx = 0
        for a in range(16):
            for half in range(2):
                bank = s.psum[3 + kidx % 3]
                s.wait("tensor", "pst", s.val("pst") - 4)
                for kc in range(NCH):
                    s.op("tensor", lambda e, a=a, hf=half, kc=kc, b=bank: e.matmul(
                        out=b[:, 0:512],
                        lhsT=s.xn_ap[:, kc * T + a * 128:kc * T + (a + 1) * 128],
                        rhs=wstr[:, kc * D + hf * 512:kc * D + (hf + 1) * 512],
                        start=(kc == 0), stop=(kc == NCH - 1)),
                        inc="stt" if kc == NCH - 1 else None)
                s.wait("vector", "stt", s.val("stt"))
                elu_evac(bank, ke[:, a * D + half * 512:a * D + (half + 1) * 512], 512)
                kidx += 1
        mk = s.chain("tensor")
        s.wait("gpsimd", mk[0], mk[1])
        s.ldg(wstr, s.Wqkv.rearrange("(c p) o -> p c o", p=128)[:, :, 2 * D:3 * D], sem="ldgA")
        s.wait("tensor", "ldgA")
        kidx = 0
        ploop = s.val("pst")
        for a in range(16):
            for half in range(2):
                bank = s.psum[3 + kidx % 3]
                s.wait("tensor", "pst", max(s.val("pst") - 2, ploop))
                for kc in range(NCH):
                    s.op("tensor", lambda e, a=a, hf=half, kc=kc, b=bank: e.matmul(
                        out=b[:, 0:512],
                        lhsT=s.xn_ap[:, kc * T + a * 128:kc * T + (a + 1) * 128],
                        rhs=wstr[:, kc * D + hf * 512:kc * D + (hf + 1) * 512],
                        start=(kc == 0), stop=(kc == NCH - 1)),
                        inc="stt" if kc == NCH - 1 else None)
                s.wait("vector", "stt", s.val("stt"))
                s.op("vector", lambda e, a=a, hf=half, b=bank: e.tensor_copy(
                    out=v1[:, a * D + hf * 512:a * D + (hf + 1) * 512],
                    in_=b[:, 0:512]), inc="pst")
                kidx += 1
        prj = s.chain("vector")
        s.wait("tensor", prj[0], prj[1])

        s.op("vector", lambda e: e.memset(ks, 0.0), inc="pst")
        s.chain("vector")
        s.wait("tensor", "pst", s.val("pst"))
        for hp in range(8):
            bank = s.psum[4 + hp % 2]
            for parity in range(2):
                h = 2 * hp + parity
                base = 64 * parity
                for kt in range(16):
                    s.op("tensor", lambda e, h=h, kt=kt, b=bank, base=base: e.matmul(
                        out=b[base:base + 64, 0:64],
                        lhsT=ke[:, kt * D + h * 64:kt * D + h * 64 + 64],
                        rhs=v1[:, kt * D + h * 64:kt * D + h * 64 + 64],
                        start=(kt == 0), stop=(kt == 15)),
                        inc="stt" if kt == 15 else None)
            s.wait("vector", "stt", s.val("stt"))
            s.op("vector", lambda e, hp=hp, b=bank: e.tensor_copy(
                out=kvsb[:, hp * 64:(hp + 1) * 64], in_=b[:, 0:64]), inc="pst")
            if hp >= 1:
                s.wait("tensor", "pst", s.val("pst") - 1)
        for h in range(H):
            bank = s.psum[6]
            base = 64 * (h % 2)
            for kt in range(16):
                s.op("tensor", lambda e, h=h, kt=kt, b=bank, base=base: e.matmul(
                    out=b[base:base + 64, h:h + 1],
                    lhsT=ke[:, kt * D + h * 64:kt * D + h * 64 + 64],
                    rhs=s.onesb_col,
                    start=(kt == 0), stop=(kt == 15)),
                    inc="stt" if kt == 15 else None)
            s.wait("vector", "stt", s.val("stt"))
            s.op("vector", lambda e, h=h, b=bank, base=base: e.tensor_copy(
                out=ks[base:base + 64, (h // 2) * 16 + h:(h // 2) * 16 + h + 1],
                in_=b[base:base + 64, h:h + 1]), inc="pst")
            s.wait("tensor", "pst", s.val("pst"))
        for j in range(NT):
            bank = s.psum[7]
            for kc in range(NCH):
                s.op("tensor", lambda e, j=j, kc=kc, b=bank: e.matmul(
                    out=b[0:16, 0:512],
                    lhsT=ks[:, kc * 16:(kc + 1) * 16],
                    rhs=qe[:, kc * T + j * 512:kc * T + (j + 1) * 512],
                    start=(kc == 0), stop=(kc == NCH - 1)),
                    inc="stt" if kc == NCH - 1 else None)
            s.wait("vector", "stt", s.val("stt"))
            s.op("vector", lambda e, j=j, b=bank: e.tensor_scalar(
                out=qks[:, j * 512:(j + 1) * 512], in0=b[0:16, 0:512],
                scalar1=1e-6, scalar2=None, op0=ALU.add), inc="pst")
            s.wait("tensor", "pst", s.val("pst"))
        s.chain("vector")
        s.op("vector", lambda e: e.reciprocal(out=rden, in_=qks), inc="sv")
        s.chain("vector")
        rdv = s.val("sv")
        s.wait("tensor", "sv", rdv)

        af1 = s.bf16("ke")
        kidx = 0
        for hp in range(8):
            for j in range(NT):
                bank = s.psum[kidx % 2]
                bden = s.psum[2 + kidx % 2]
                s.wait("tensor", "pst", s.val("pst") - 2)
                for parity in range(2):
                    h = 2 * hp + parity
                    base = 64 * parity
                    s.op("tensor", lambda e, h=h, hp=hp, j=j, b=bank, base=base: e.matmul(
                        out=b[base:base + 64, 0:512],
                        lhsT=kvsb[base:base + 64, hp * 64:(hp + 1) * 64],
                        rhs=qe[base:base + 64, (h // 2) * T + j * 512:(h // 2) * T + (j + 1) * 512],
                        start=True, stop=True))
                s.op("tensor", lambda e, hp=hp, j=j, b=bden: e.matmul(
                    out=b[:, 0:512], lhsT=sel8[:, hp * 128:(hp + 1) * 128],
                    rhs=rden[:, j * 512:(j + 1) * 512],
                    start=True, stop=True), inc="stt")
                s.wait("vector", "stt", s.val("stt"))
                s.op("vector", lambda e, b=bden: e.tensor_copy(out=denb, in_=b[:, 0:512]), inc="pst")
                s.chain("vector")
                s.op("vector", lambda e, hp=hp, j=j, b=bank: e.tensor_tensor(
                    out=af1[:, hp * T + j * 512:hp * T + (j + 1) * 512],
                    in0=b[:, 0:512], in1=denb, op=ALU.mult), inc="pst")
                s.chain("vector")
                kidx += 1
        afd = s.chain("vector")
        s.wait("tensor", afd[0], afd[1])
        s.wait("gpsimd", afd[0], afd[1])

        s.ldg(wstr, s.Wout.rearrange("(c p) o -> p c o", p=128), sem="ldgB")
        s.wait("tensor", "ldgB")
        ost2 = s.f32("ost2")
        for j in range(NT):
            s.wait("vector", "st", s.val("st"))
            for oc in range(NCH):
                bank = s.psum[oc]
                for kc in range(NCH):
                    s.op("tensor", lambda e, oc=oc, j=j, kc=kc, b=bank: e.matmul(
                        out=b[:, 0:512],
                        lhsT=wstr[:, kc * D + oc * 128:kc * D + (oc + 1) * 128],
                        rhs=af1[:, kc * T + j * 512:kc * T + (j + 1) * 512],
                        start=(kc == 0), stop=(kc == NCH - 1)),
                        inc="stt" if kc == NCH - 1 else None)
                s.wait("vector", "stt", s.val("stt"))
                s.op("vector", lambda e, oc=oc, b=bank: e.tensor_scalar(
                    out=ost2[:, oc * 512:(oc + 1) * 512], in0=b[:, 0:512],
                    scalar1=s.bout_sb[:, oc:oc + 1], scalar2=None,
                    op0=ALU.add), inc="pst")
            s.wait("sync", "pst", s.val("pst"))
            s.sth(s.attn_s.rearrange("p (c t) -> p c t", c=NCH)[:, :, j * 512:(j + 1) * 512],
                  ost2.rearrange("p (c t) -> p c t", c=NCH))
            s.wait("tensor", "pst", s.val("pst"))
        s.pending_adds.append(s.attn_s)
        s.l1attn_st = s.val("st")


    def p_l1_ffn(self):
        s = self
        s.phase_barrier()

        def hchunk(kc):
            off = (R_X + kc * 4 * KB) if kc < 24 else (158 * KB + (kc - 24) * 4 * KB)
            return s.pool.ap().bitcast(BF16)[:, off // 2: off // 2 + T]

        bfspan = lambda off, n: s.pool.ap().bitcast(BF16)[:, off // 2: off // 2 + n]
        fspan = lambda off, n: s.pool.ap()[:, off // 4: off // 4 + n]
        wb1 = [bfspan(190 * KB + i * 2 * KB, 1024) for i in range(4)]
        sil = fspan(198 * KB, 512)
        us = fspan(200 * KB, 512)

        tmarks = {}
        emarks = []
        for oc in range(ICH):
            g0, u0 = wb1[2 * (oc % 2)], wb1[2 * (oc % 2) + 1]
            psem = "ldgA" if oc % 2 == 0 else "ldgB"
            if oc >= 2:
                s.wait("gpsimd", tmarks[oc - 2][0], tmarks[oc - 2][1])
            s.ldg(g0, s.w1T.rearrange("(c p) o -> p c o", p=128)[:, :, oc * 128:(oc + 1) * 128], sem=psem)
            s.ldg(u0, s.w2T.rearrange("(c p) o -> p c o", p=128)[:, :, oc * 128:(oc + 1) * 128], sem=psem)
            s.wait("tensor", psem, s.val(psem))
            for j in range(NT):
                n = oc * NT + j
                if n >= 2:
                    s.wait("tensor", "pss", emarks[n - 2][0])
                    s.wait("tensor", "pst", emarks[n - 2][1])
                bg, bu = s.psum[2 * (j % 2)], s.psum[2 * (j % 2) + 1]
                for kc in range(NCH):
                    s.op("tensor", lambda e, kc=kc, j=j, b=bg, g0=g0: e.matmul(
                        out=b[:, 0:512], lhsT=g0[:, kc * 128:(kc + 1) * 128],
                        rhs=s.xn_ap[:, kc * T + j * 512:kc * T + (j + 1) * 512],
                        start=(kc == 0), stop=(kc == NCH - 1)))
                for kc in range(NCH):
                    s.op("tensor", lambda e, kc=kc, j=j, b=bu, u0=u0: e.matmul(
                        out=b[:, 0:512], lhsT=u0[:, kc * 128:(kc + 1) * 128],
                        rhs=s.xn_ap[:, kc * T + j * 512:kc * T + (j + 1) * 512],
                        start=(kc == 0), stop=(kc == NCH - 1)),
                        inc="stt" if kc == NCH - 1 else None)
                s.wait("scalar", "stt", s.val("stt"))
                s.wait("scalar", "sv", s.val("sv"))
                s.op("scalar", lambda e, oc=oc, b=bg: e.activation(
                    out=sil, in_=b[:, 0:512], func=AF.Silu,
                    scale=s.s1_sb[:, oc:oc + 1]), inc="pss")
                pssv = s.val("pss")
                mks = s.chain("scalar")
                s.wait("vector", mks[0], mks[1])
                s.op("vector", lambda e, oc=oc, b=bu: e.tensor_scalar(
                    out=us, in0=b[:, 0:512], scalar1=s.s2_sb[:, oc:oc + 1],
                    scalar2=None, op0=ALU.mult), inc="pst")
                pstv = s.val("pst")
                emarks.append((pssv, pstv))
                s.chain("vector")
                s.op("vector", lambda e, oc=oc, j=j: e.tensor_tensor(
                    out=hchunk(oc)[:, j * 512:(j + 1) * 512], in0=sil, in1=us,
                    op=ALU.mult), inc="sv")
                s.chain("vector")
            mk = s.chain("tensor")
            tmarks[oc] = mk
        h1d = s.chain("vector")
        for eng in ("tensor", "sync", "gpsimd", "scalar"):
            s.wait(eng, h1d[0], h1d[1])

        def hidden_stage(wT, scol, dst_dram, buf_base):
            hb0 = bfspan(buf_base, 4096)
            hb1_ = bfspan(buf_base + 8 * KB, 4096)
            stg_ = bfspan(buf_base + 16 * KB, 2048)
            tm = {}
            gm = []
            for oc in range(ICH):
                hb_ = hb0 if oc % 2 == 0 else hb1_
                psem = "ldgA" if oc % 2 == 0 else "ldgB"
                if oc >= 2:
                    s.wait("gpsimd", tm[oc - 2][0], tm[oc - 2][1])
                s.ldg(hb_, wT.rearrange("(c p) o -> p c o", p=128)[:, :, oc * 128:(oc + 1) * 128], sem=psem)
                s.wait("tensor", psem, s.val(psem))
                for j in range(NT):
                    n = oc * NT + j
                    if n >= 2:
                        s.wait("tensor", "pss", gm[n - 2])
                    bank = s.psum[j % 2]
                    for kc in range(ICH):
                        s.op("tensor", lambda e, kc=kc, j=j, b=bank, hb_=hb_: e.matmul(
                            out=b[:, 0:512], lhsT=hb_[:, kc * 128:(kc + 1) * 128],
                            rhs=hchunk(kc)[:, j * 512:(j + 1) * 512],
                            start=(kc == 0), stop=(kc == ICH - 1)),
                            inc="stt" if kc == ICH - 1 else None)
                    s.wait("scalar", "stt", s.val("stt"))
                    s.op("scalar", lambda e, oc=oc, j=j, b=bank, sc=scol, stg_=stg_: e.activation(
                        out=stg_[:, j * 512:(j + 1) * 512], in_=b[:, 0:512],
                        func=AF.Gelu, scale=sc[:, oc:oc + 1]), inc="pss")
                    gm.append(s.val("pss"))
                mks = s.chain("scalar")
                s.wait("sync", mks[0], mks[1])
                s.sth(dst_dram.rearrange("p (c t) -> p c t", c=ICH)[:, oc, :], stg_)
                s.wait("scalar", "st", s.val("st"))
                mk = s.chain("tensor")
                tm[oc] = mk
            mkx = s.chain("scalar")
            for eng in ("sync", "vector", "tensor", "gpsimd"):
                s.wait(eng, mkx[0], mkx[1])

        hidden_stage(s.h1T, s.sh1_sb, s.h2_s, 126 * KB)
        s.wait("sync", "st", s.val("st"))
        s.ldh(bfspan(R_X, 24 * T), s.h2_s[:, 0:24 * T])
        s.ldh(bfspan(158 * KB, 8 * T), s.h2_s[:, 24 * T:32 * T])
        lv = s.val("ldh")
        s.wait("tensor", "ldh", lv)
        s.wait("sync", "ldh", lv)
        hidden_stage(s.h2T, s.sh2_sb, s.h3_s, 126 * KB)

        xh = fspan(R_D, NCH * 1024)
        ah = fspan(R_D + 32 * KB, NCH * 1024)

        def h3c(kc):
            return bfspan(R_X + kc * 2 * KB, 1024)

        for half in range(2):
            hsl = slice(half * 1024, (half + 1) * 1024)
            s.wait("sync", "st", s.val("st"))
            s.ldh(xh.rearrange("p (c t) -> p c t", c=NCH),
                  s.xpark.rearrange("p (c t) -> p c t", c=NCH)[:, :, hsl])
            s.ldh(ah.rearrange("p (c t) -> p c t", c=NCH),
                  s.attn_s.rearrange("p (c t) -> p c t", c=NCH)[:, :, hsl])
            for kc in range(ICH):
                s.ldh(h3c(kc), s.h3_s[:, kc * T + half * 1024:kc * T + half * 1024 + 1024],
                      sem="lha")
            lha_v = s.val("lha")
            s.wait("tensor", "lha", lha_v)
            ldv = s.val("ldh")
            s.wait("vector", "ldh", ldv)
            s.wait("tensor", "ldh", ldv)
            mkt = s.chain("tensor")
            s.wait("gpsimd", mkt[0], mkt[1])
            wmks = {}
            smarks = []
            for oc in range(NCH):
                hb_ = bfspan(156 * KB, 4096) if oc % 2 == 0 else bfspan(164 * KB, 4096)
                psem = "ldgA" if oc % 2 == 0 else "ldgB"
                if oc >= 2:
                    s.wait("gpsimd", wmks[oc - 2][0], wmks[oc - 2][1])
                s.ldg(hb_, s.w3T.rearrange("(c p) o -> p c o", p=128)[:, :, oc * 128:(oc + 1) * 128], sem=psem)
                s.wait("tensor", psem, s.val(psem))
                for j in range(2):
                    n = oc * 2 + j
                    if n >= 2:
                        s.wait("tensor", "pst", smarks[n - 2])
                    bank = s.psum[j % 2]
                    for kc in range(ICH):
                        s.op("tensor", lambda e, kc=kc, j=j, b=bank, hb_=hb_: e.matmul(
                            out=b[:, 0:512], lhsT=hb_[:, kc * 128:(kc + 1) * 128],
                            rhs=h3c(kc)[:, j * 512:(j + 1) * 512],
                            start=(kc == 0), stop=(kc == ICH - 1)),
                            inc="stt" if kc == ICH - 1 else None)
                    s.wait("vector", "stt", s.val("stt"))
                    s.op("vector", lambda e, oc=oc, j=j, b=bank, xh=xh: e.scalar_tensor_tensor(
                        out=xh[:, oc * 1024 + j * 512:oc * 1024 + (j + 1) * 512],
                        in0=b[:, 0:512], scalar=s.s3_sb[:, oc:oc + 1],
                        in1=xh[:, oc * 1024 + j * 512:oc * 1024 + (j + 1) * 512],
                        op0=ALU.mult, op1=ALU.add), inc="pst")
                    smarks.append(s.val("pst"))
                    s.chain("vector")
                    s.op("vector", lambda e, oc=oc, j=j, xh=xh, ah=ah: e.tensor_tensor(
                        out=xh[:, oc * 1024 + j * 512:oc * 1024 + (j + 1) * 512],
                        in0=xh[:, oc * 1024 + j * 512:oc * 1024 + (j + 1) * 512],
                        in1=ah[:, oc * 1024 + j * 512:oc * 1024 + (j + 1) * 512],
                        op=ALU.add), inc="sv")
                    s.chain("vector")
                mk = s.chain("tensor")
                wmks[oc] = mk
            fin = s.chain("vector")
            s.wait("sync", fin[0], fin[1])
            s.sth(s.xpark.rearrange("p (c t) -> p c t", c=NCH)[:, :, hsl],
                  xh.rearrange("p (c t) -> p c t", c=NCH))
            stv = s.val("st")
            s.wait("vector", "st", stv)
            s.wait("tensor", "st", stv)
            s.wait("sync", "st", stv)
        s.phase_barrier()
        s.ldh(s.f32("x"), s.xpark)
        lv2 = s.val("ldh")
        for eng in ("vector", "scalar", "tensor", "gpsimd"):
            s.wait(eng, "ldh", lv2)
        s.x_parked = False
        s.pending_adds = []

    def p_store(self):
        s = self
        s.phase_barrier()
        s.alloc("ostg", T * NCH * 4, at=R_D)
        o_tm = s.f32("ostg")
        if s.x_parked:
            s.ldh(s.f32("x"), s.xpark)
            for da in s.pending_adds:
                s.ldh(o_tm, da)
                v = s.val("ldh")
                s.wait("vector", "ldh", v)
                for c in range(NCH):
                    s.op("vector", lambda e, c=c: e.tensor_tensor(
                        out=s.xc(c), in0=s.xc(c), in1=o_tm[:, c * T:(c + 1) * T],
                        op=ALU.add), inc="sv")
                s.chain("vector")
            s.x_parked = False
            s.pending_adds = []
            s.wait("tensor", "sv", s.val("sv"))
            s.wait("tensor", "ldh", s.val("ldh"))
        base = s.val("pst")
        for a in range(16):
            for c in range(NCH):
                src = s.xc(c)[:, a * 128:(a + 1) * 128]
                dst = o_tm[:, a * D + c * 128: a * D + (c + 1) * 128]
                s.transpose_128(src, dst, base, a * NCH + c)
        s.wait("sync", "pst", s.val("pst"))
        s.sth(s.out.rearrange("(a p) d -> p a d", p=128), o_tm)
        s.wait("sync", "st", s.val("st"))


def host_inputs(ins, core):
    f = np.float32
    col = lambda v, nch: np.ascontiguousarray(np.asarray(v).reshape(nch, 128).T).astype(f)
    tr = lambda v: np.ascontiguousarray(np.asarray(v).T).astype(f)
    m = {
        "x_in": np.ascontiguousarray(np.asarray(ins["L0_attn_x"][core])).astype(f),
        "Wq": np.asarray(ins["L0_attn_Wq"], f), "Wk": np.asarray(ins["L0_attn_Wk"], f),
        "Wv": np.asarray(ins["L0_attn_Wv"], f), "Wm": np.asarray(ins["L0_attn_Wm"], f),
        "bq": col(ins["L0_attn_bq"], NCH), "bk": col(ins["L0_attn_bk"], NCH),
        "bv": np.asarray(ins["L0_attn_bv"], f).reshape(1, D), "bm": col(ins["L0_attn_bm"], NCH),
        "wgT": tr(ins["L0_ffn_w_gate"]), "wuT": tr(ins["L0_ffn_w_up"]),
        "wdT": tr(ins["L0_ffn_w_down"]),
        "Wqkv": np.asarray(ins["L1_attn_Wqkv"], f), "Wout": np.asarray(ins["L1_attn_Wout"], f),
        "bout": col(ins["L1_attn_bout"], NCH),
        "w1T": tr(ins["L1_ffn_w1_w"]), "w2T": tr(ins["L1_ffn_w2_w"]),
        "h1T": tr(ins["L1_ffn_h1_w"]), "h2T": tr(ins["L1_ffn_h2_w"]),
        "w3T": tr(ins["L1_ffn_w3_w"]),
        "s1c": col(ins["L1_ffn_w1_s"], ICH), "s2c": col(ins["L1_ffn_w2_s"], ICH),
        "sh1c": col(ins["L1_ffn_h1_s"], ICH), "sh2c": col(ins["L1_ffn_h2_s"], ICH),
        "s3c": col(ins["L1_ffn_w3_s"], NCH),
        "ident": np.eye(128, dtype=f),
        "identb": np.eye(128, dtype=ml_dtypes.bfloat16),
        "triu": np.triu(np.ones((128, 128), np.float32)).astype(ml_dtypes.bfloat16),
        "sel8": _sel8(),
        "ident4": np.tile(np.eye(128, dtype=f), (1, 4)),
    }
    return m


def np_softmax(x, axis=-1):
    m = np.max(x, axis=axis, keepdims=True)
    e = np.exp(x - m)
    return e / e.sum(axis=axis, keepdims=True)


def np_silu(x):
    return x / (1 + np.exp(-x))


def np_gelu_exact(x):
    from scipy.special import erf
    return 0.5 * x * (1 + erf(x / np.sqrt(2)))


def np_layernorm(x, eps=1e-5):
    m = x.mean(-1, keepdims=True)
    v = x.var(-1, keepdims=True)
    return (x - m) / np.sqrt(v + eps)


def ref_l0_attn(x, Wq, bq, Wk, bk, Wv, bv, Wm, bm):
    S = x.shape[0]
    def heads(w, b):
        return (x @ w + b).reshape(S, H, HS).transpose(1, 0, 2)
    q, k, v = heads(Wq, bq), heads(Wk, bk), heads(Wv, bv)
    sc = np.einsum("hqd,hkd->hqk", q, k) / np.sqrt(HS)
    mask = np.arange(S)[None, :] > np.arange(S)[:, None]
    sc = np.where(mask[None], -1e12, sc)
    p = np_softmax(sc, -1)
    a = np.einsum("hqk,hkd->hqd", p, v)
    a = a.transpose(1, 0, 2).reshape(S, D)
    return a @ Wm + bm


def ref_l0_ffn(x, w_gate, w_up, w_down):
    def wq_(w):
        sc = 1.0 / np.clip(np.mean(np.abs(w)), 1e-5, None)
        return np.clip(np.round(w * sc), -1, 1) / sc
    def aq_(t):
        sc = 127.0 / np.clip(np.max(np.abs(t), -1, keepdims=True), 1e-5, None)
        return np.clip(np.round(t * sc), -128, 127) / sc
    bl = lambda t, w: aq_(t) @ wq_(w).T
    g = np_silu(bl(x, w_gate))
    u = bl(x, w_up)
    return bl(g * u, w_down)


def ref_l1_attn(x, Wqkv, Wout, bout):
    S = x.shape[0]
    qkv = x @ Wqkv
    q, k, v = np.split(qkv, 3, -1)
    rs = lambda t: t.reshape(S, H, HS).transpose(1, 0, 2)
    q, k, v = rs(q), rs(k), rs(v)
    el = lambda t: np.where(t > 0, t + 1.0, np.exp(t))
    q, k = el(q), el(k)
    kv = np.einsum("hnd,hnf->hdf", k, v)
    ks = k.sum(1)
    qo = np.einsum("hnd,hdf->hnf", q, kv)
    qks = np.einsum("hnd,hd->hn", q, ks)
    o = qo / (qks[..., None] + 1e-6)
    o = o.transpose(1, 0, 2).reshape(S, D)
    return o @ Wout + bout


def ref_l1_ffn(x, ws):
    tl = lambda t, w, sc: (t @ w.T) * sc
    h = np_silu(tl(x, ws["w1_w"], ws["w1_s"])) * tl(x, ws["w2_w"], ws["w2_s"])
    h = np_gelu_exact(tl(h, ws["h1_w"], ws["h1_s"]))
    h = np_gelu_exact(tl(h, ws["h2_w"], ws["h2_s"]))
    return tl(h, ws["w3_w"], ws["w3_s"])


def _sel8():
    m = np.zeros((16, 1024), np.float32)
    for hp in range(8):
        for p in range(128):
            h = 2 * hp + (1 if p >= 64 else 0)
            m[h, 128 * hp + p] = 1.0
    return m


def kernel(**inputs):
    mk = MK(7, dump=None)
    nc = mk.build()
    shared = host_shared(inputs)
    in_maps = []
    for core in range(8):
        m = dict(shared)
        m["x_in"] = np.ascontiguousarray(np.asarray(inputs["L0_attn_x"][core])).astype(np.float32)
        in_maps.append(m)
    from concourse.bass_utils import run_bass_kernel_spmd
    res = run_bass_kernel_spmd(nc, in_maps, core_ids=list(range(8)))
    out = np.stack([np.asarray(r["out"], np.float32) for r in res.results], axis=0)
    return out


def host_shared(ins):
    f = np.float32
    col = lambda v, nch: np.ascontiguousarray(np.asarray(v).reshape(nch, 128).T).astype(f)
    tr = lambda v: np.ascontiguousarray(np.asarray(v).T).astype(f)
    return {
        "Wq": np.asarray(ins["L0_attn_Wq"], f), "Wk": np.asarray(ins["L0_attn_Wk"], f),
        "Wv": np.asarray(ins["L0_attn_Wv"], f), "Wm": np.asarray(ins["L0_attn_Wm"], f),
        "bq": col(ins["L0_attn_bq"], NCH), "bk": col(ins["L0_attn_bk"], NCH),
        "bv": np.asarray(ins["L0_attn_bv"], f).reshape(1, D), "bm": col(ins["L0_attn_bm"], NCH),
        "wgT": tr(ins["L0_ffn_w_gate"]), "wuT": tr(ins["L0_ffn_w_up"]),
        "wdT": tr(ins["L0_ffn_w_down"]),
        "Wqkv": np.asarray(ins["L1_attn_Wqkv"], f), "Wout": np.asarray(ins["L1_attn_Wout"], f),
        "bout": col(ins["L1_attn_bout"], NCH),
        "w1T": tr(ins["L1_ffn_w1_w"]), "w2T": tr(ins["L1_ffn_w2_w"]),
        "h1T": tr(ins["L1_ffn_h1_w"]), "h2T": tr(ins["L1_ffn_h2_w"]),
        "w3T": tr(ins["L1_ffn_w3_w"]),
        "s1c": col(ins["L1_ffn_w1_s"], ICH), "s2c": col(ins["L1_ffn_w2_s"], ICH),
        "sh1c": col(ins["L1_ffn_h1_s"], ICH), "sh2c": col(ins["L1_ffn_h2_s"], ICH),
        "s3c": col(ins["L1_ffn_w3_s"], NCH),
        "ident": np.eye(128, dtype=f),
        "identb": np.eye(128, dtype=ml_dtypes.bfloat16),
        "triu": np.triu(np.ones((128, 128), np.float32)).astype(ml_dtypes.bfloat16),
        "sel8": _sel8(),
        "ident4": np.tile(np.eye(128, dtype=f), (1, 4)),
    }


# version 13
# speedup vs baseline: 1.0452x; 1.0419x over previous
import sys
import numpy as np
import ml_dtypes
import concourse.bass as bass
import concourse.mybir as mybir
from contextlib import ExitStack

F32 = mybir.dt.float32
BF16 = mybir.dt.bfloat16
I32 = mybir.dt.int32
ALU = mybir.AluOpType
AF = mybir.ActivationFunctionType
AX = mybir.AxisListType

T = 2048
D = 1024
NCH = 8
H = 16
HS = 64
INNER = 4096
ICH = 32
NT = 4
LNEPS = 1e-5

ENGS = ("sync", "vector", "scalar", "tensor", "gpsimd")

KB = 1024
R_CONSTS = 0 * KB
R_ROWS = 4 * KB
R_X = 28 * KB
R_D = 92 * KB
R_E = 156 * KB
POOL_END = 204 * KB


class MK:
    def __init__(self, n_phases=99, dump=None):
        self.n_phases = n_phases
        self.dump = dump
        self.nc = bass.Bass("TRN2", target_bir_lowering=False)
        self.ops = {e: [] for e in ENGS}
        self.cnt = {}
        self.sems = {}
        self.x_parked = False
        self.pending_adds = []
        self.prepass_gen = None
        self.bg_pst_mark = None
        self.qkv_budget = 64

    def op(self, eng, fn, inc=None, inc_by=1):
        ent = {"fn": fn, "inc": None}
        if inc is not None:
            self.cnt[inc] += inc_by
            ent["inc"] = (inc, inc_by, self.cnt[inc])
        self.ops[eng].append(ent)
        return self.cnt.get(inc)

    def wait(self, eng, sem_name, val=None):
        v = self.cnt[sem_name] if val is None else val
        if v > 0:
            self.ops[eng].append({"wait": (sem_name, v)})
        return v

    def val(self, sem_name):
        return self.cnt[sem_name]

    CHAIN = {"vector": "vch", "scalar": "sch", "tensor": "tch", "gpsimd": "gch"}

    def phase_barrier(self, engs=("sync", "vector", "scalar", "tensor", "gpsimd")):
        for eng in ("vector", "scalar", "tensor", "gpsimd"):
            if self.ops[eng] and any("fn" in x for x in self.ops[eng]):
                self.chain(eng)
        snap = {nm: self.cnt[nm] for nm in
                ("ldh", "ldg", "ldgA", "ldgB", "ldgC", "st", "sv", "ss", "stt",
                 "sg", "pst", "pss", "psa", "lha", "lhb", "vch", "sch", "tch", "gch")}
        for eng in engs:
            for nm, v in snap.items():
                self.wait(eng, nm, v)

    def chain(self, eng):
        for ent in reversed(self.ops[eng]):
            if "fn" in ent:
                last = ent
                break
        else:
            return
        if last["inc"] is None:
            nm = self.CHAIN[eng]
            self.cnt[nm] += 1
            last["inc"] = (nm, 1, self.cnt[nm])
        nm, _, v = last["inc"]
        self.ops[eng].append({"wait": (nm, v)})
        return (nm, v)

    def ldh(self, out, in_, sem="ldh"):
        return self.op("sync", lambda e, o=out, i=in_: e.dma_start(out=o, in_=i),
                       inc=sem, inc_by=16)

    def ldg(self, out, in_, sem="ldgA"):
        self.wait("gpsimd", sem, self.cnt[sem])
        return self.op("gpsimd", lambda e, o=out, i=in_: e.dma_start(out=o, in_=i),
                       inc=sem, inc_by=16)

    def sth(self, out, in_):
        self.wait("sync", "st", self.cnt["st"])
        return self.op("sync", lambda e, o=out, i=in_: e.dma_start(out=o, in_=i),
                       inc="st", inc_by=16)

    def alloc(self, name, nbytes, at=None, align=64):
        off = at if at is not None else (self.cursor + align - 1) // align * align
        assert off + nbytes <= self.POOLB, f"pool overflow: {name} @{off}+{nbytes}"
        self.spans[name] = (off, nbytes)
        self.cursor = max(self.cursor, off + nbytes) if at is None else self.cursor
        self.watermark = max(self.watermark, off + nbytes)
        return off

    def f32(self, name, cols=None):
        off, size = self.spans[name]
        n = size // 4 if cols is None else cols
        return self.pool.ap()[:, off // 4: off // 4 + n]

    def bf16(self, name, cols=None):
        off, size = self.spans[name]
        n = size // 2 if cols is None else cols
        return self.pool.ap().bitcast(BF16)[:, off // 2: off // 2 + n]

    def build(self):
        nc = self.nc
        dram = lambda nm, sh, dt, kind="ExternalInput": nc.dram_tensor(nm, sh, dt, kind=kind).ap()
        s = self
        s.x_in = dram("x_in", [T, D], F32)
        s.Wq = dram("Wq", [D, D], F32); s.Wk = dram("Wk", [D, D], F32)
        s.Wv = dram("Wv", [D, D], F32); s.Wm = dram("Wm", [D, D], F32)
        s.bq = dram("bq", [128, NCH], F32); s.bk = dram("bk", [128, NCH], F32)
        s.bv = dram("bv", [1, D], F32); s.bm = dram("bm", [128, NCH], F32)
        s.wgT = dram("wgT", [D, INNER], F32); s.wuT = dram("wuT", [D, INNER], F32)
        s.wdT = dram("wdT", [INNER, D], F32)
        s.Wqkv = dram("Wqkv", [D, 3 * D], F32)
        s.Wout = dram("Wout", [D, D], F32); s.bout = dram("bout", [128, NCH], F32)
        s.w1T = dram("w1T", [D, INNER], F32); s.w2T = dram("w2T", [D, INNER], F32)
        s.h1T = dram("h1T", [INNER, INNER], F32); s.h2T = dram("h2T", [INNER, INNER], F32)
        s.w3T = dram("w3T", [INNER, D], F32)
        s.s1c = dram("s1c", [128, ICH], F32); s.s2c = dram("s2c", [128, ICH], F32)
        s.sh1c = dram("sh1c", [128, ICH], F32); s.sh2c = dram("sh2c", [128, ICH], F32)
        s.s3c = dram("s3c", [128, NCH], F32)
        s.identd = dram("ident", [128, 128], F32)
        s.identbd = dram("identb", [128, 128], BF16)
        s.triud = dram("triu", [128, 128], BF16)
        s.sel8d = dram("sel8", [16, 1024], F32)
        s.ident4d = dram("ident4", [128, 512], F32)
        s.out = dram("out", [T, D], F32, "ExternalOutput")
        s.dbg = dram("dbg", [128, 17408], BF16, "ExternalOutput") if s.dump else None
        s.iwg_s = dram("iwg_s", [D, INNER], BF16, "Internal")
        s.iwu_s = dram("iwu_s", [D, INNER], BF16, "Internal")
        s.iwd_s = dram("iwd_s", [INNER, D], BF16, "Internal")
        s.xpark = dram("xpark", [128, NCH * T], F32, "Internal")
        s.attn_s = dram("attn_s", [128, NCH * T], F32, "Internal")
        s.h2_s = dram("h2_s", [128, ICH * T], BF16, "Internal")
        s.h3_s = dram("h3_s", [128, ICH * T], BF16, "Internal")

        ctx = ExitStack()
        with ctx:
            self.POOLB = 204 * 1024
            self.pool = ctx.enter_context(nc.sbuf_tensor("pool", [128, self.POOLB // 4], F32))
            self.spans, self.cursor, self.watermark = {}, 0, 0
            self.psumbig = ctx.enter_context(nc.psum_tensor("pb", [128, 4096], F32))
            self.psum = [self.psumbig.ap()[:, i * 512:(i + 1) * 512]
                         for i in range(8)]
            for nm in ("ldh", "ldg", "ldgA", "ldgB", "ldgC", "st", "sv", "ss",
                       "stt", "sg", "pst", "pss", "psa", "wb", "lha", "lhb",
                       "vch", "sch", "tch", "gch"):
                self.sems[nm] = ctx.enter_context(nc.semaphore(nm))
                self.cnt[nm] = 0
            blk = ctx.enter_context(nc.Block())

            self.alloc("consts", 4096, at=R_CONSTS)
            self.alloc("rows", 24576, at=R_ROWS)
            self.alloc("x", T * NCH * 4, at=R_X)

            self.setup_consts()
            self.p0_load_x()
            if self.n_phases >= 1:
                self.p_ln(0)
            if self.n_phases >= 2:
                if self.n_phases >= 4:
                    self.prepass_gen = self.prepass_generator()
                self.p_l0_qkv()
            if self.n_phases >= 3:
                self.p_l0_attn()
            if self.n_phases >= 4:
                self.p_l0_ffn()
            if self.n_phases >= 5:
                self.p_ln(1)
            if self.n_phases >= 6:
                self.p_l1_attn()
            if self.n_phases >= 7:
                self.p_l1_ffn()
            self.p_store()

            for eng in ENGS:
                def make(eng):
                    def body(e):
                        for ent in self.ops[eng]:
                            if "wait" in ent:
                                nm, v = ent["wait"]
                                e.wait_ge(self.sems[nm], v)
                            else:
                                i = ent["fn"](e)
                                if ent["inc"] is not None:
                                    nm, n, _ = ent["inc"]
                                    i.then_inc(self.sems[nm], n)
                    return body
                getattr(blk, eng)(make(eng))
        return nc

    def xc(self, c):
        return self.f32("x")[:, c * T:(c + 1) * T]

    def setup_consts(self):
        s = self
        co = s.spans["consts"][0]
        p, pb = s.pool.ap(), s.pool.ap().bitcast(BF16)
        s.ident = p[:, co // 4: co // 4 + 128]
        s.identb = pb[:, (co + 512) // 2:][:, :128]
        s.triu = pb[:, (co + 768) // 2:][:, :128]
        s.ones_col = p[:, (co + 1024) // 4:][:, :1]
        s.ones_row = p[0:1, (co + 1028) // 4:][:, :128]
        s.onesb_col = pb[:, (co + 1540) // 2:][:, :1]
        s.ones128 = p[:, (co + 2048) // 4:][:, :128]
        s.halfcol = p[:, (co + 2616) // 4:][:, :1]
        ro = s.spans["rows"][0]

        def colf32(n):
            nonlocal ro
            ap = p[:, ro // 4: ro // 4 + n]
            ro += n * 4
            return ap
        s.bq_sb = colf32(NCH); s.bk_sb = colf32(NCH); s.bm_sb = colf32(NCH)
        s.bout_sb = colf32(NCH)
        s.s1_sb = colf32(ICH); s.s2_sb = colf32(ICH)
        s.sh1_sb = colf32(ICH); s.sh2_sb = colf32(ICH); s.s3_sb = colf32(NCH)
        s.bqs_sb = colf32(NCH)
        s.bv_row = p[0:1, ro // 4: ro // 4 + D]; ro += D * 4
        s.mrow_off = ro
        s.mrow = p[0:1, ro // 4: ro // 4 + T]; ro += T * 4
        s.rrow = p[0:1, ro // 4: ro // 4 + T]; ro += T * 4
        assert ro <= s.spans["rows"][0] + 24576

        s.ldh(s.ident, s.identd)
        s.ldh(s.identb, s.identbd)
        s.ldh(s.triu, s.triud)
        for dst, src in ((s.bq_sb, s.bq), (s.bk_sb, s.bk), (s.bm_sb, s.bm),
                         (s.bout_sb, s.bout), (s.s1_sb, s.s1c), (s.s2_sb, s.s2c),
                         (s.sh1_sb, s.sh1c), (s.sh2_sb, s.sh2c), (s.s3_sb, s.s3c),
                         (s.bv_row, s.bv)):
            s.ldh(dst, src)

    def transpose_128(self, src_f32, dst, kidx_sem_base, k, evac_eng="vector", bf=False):
        s = self
        bank = s.psum[k % 2]
        s.wait("tensor", "pst", kidx_sem_base + k - 1)
        ident = s.identb if bf else s.ident
        s.op("tensor", lambda e, b=bank, sr=src_f32, idn=ident:
             e.transpose(out=b[:, 0:128], in_=sr, identity=idn), inc="stt")
        s.wait(evac_eng, "stt", s.val("stt"))
        s.op(evac_eng, lambda e, b=bank, d=dst: e.tensor_copy(out=d, in_=b[:, 0:128]),
             inc="pst")

    def p0_load_x(self):
        s = self
        s.alloc("stg", T * NCH * 4, at=R_D)
        x_tm = s.f32("stg")
        pre = s.val("ldh")
        xsrc = s.x_in.rearrange("(a p) d -> p a d", p=128)
        wave_marks = []
        for wv in range(4):
            s.wait("sync", "ldh", s.val("ldh"))
            s.ldh(x_tm[:, wv * 4 * D:(wv + 1) * 4 * D], xsrc[:, wv * 4:(wv + 1) * 4, :])
            wave_marks.append(s.val("ldh"))
        s.wait("vector", "ldh", pre)
        s.op("vector", lambda e: e.memset(s.ones_col, 1.0))
        s.op("vector", lambda e: e.memset(s.ones_row, 1.0))
        s.op("vector", lambda e: e.memset(s.onesb_col, 1.0))
        s.op("vector", lambda e: e.memset(s.ones128, 1.0))
        s.op("vector", lambda e: e.memset(s.halfcol, 0.5))
        s.op("vector", lambda e: e.tensor_scalar(out=s.bqs_sb, in0=s.bq_sb,
             scalar1=0.125, scalar2=None, op0=ALU.mult), inc="sv")
        base = s.val("pst")
        for a in range(16):
            if a % 4 == 0:
                s.wait("tensor", "ldh", wave_marks[a // 4])
            for c in range(NCH):
                src = x_tm[:, a * D + c * 128: a * D + (c + 1) * 128]
                dst = s.xc(c)[:, a * 128:(a + 1) * 128]
                s.transpose_128(src, dst, base, a * NCH + c)
        self.x_loaded_pst = s.val("pst")

    def stats_mb_rb(self, mb_ap, rb_ap, sq0, sq1):
        s = self
        sq = [sq0, sq1]
        chunk_mark = []
        for c in range(NCH):
            buf = sq[c % 2]
            if c >= 2:
                s.wait("scalar", "stt", chunk_mark[c - 2])
            s.op("scalar", lambda e, c=c, b=buf: e.activation(out=b, in_=s.xc(c), func=AF.Square), inc="ss")
            s.wait("tensor", "ss", s.val("ss"))
            for j in range(NT):
                s.op("tensor", lambda e, c=c, j=j: e.matmul(
                    out=s.psum[j][0:1, 0:512],
                    lhsT=s.ones_col, rhs=s.xc(c)[:, j * 512:(j + 1) * 512],
                    start=(c == 0), stop=(c == NCH - 1)))
            for j in range(NT):
                s.op("tensor", lambda e, c=c, j=j, b=buf: e.matmul(
                    out=s.psum[4 + j][0:1, 0:512],
                    lhsT=s.ones_col, rhs=b[:, j * 512:(j + 1) * 512],
                    start=(c == 0), stop=(c == NCH - 1)),
                    inc="stt" if j == NT - 1 else None)
            chunk_mark.append(s.val("stt"))
        tstats = s.val("stt")
        s.wait("vector", "stt", tstats)
        for j in range(NT):
            s.op("vector", lambda e, j=j: e.tensor_scalar(
                out=s.mrow[:, j * 512:(j + 1) * 512], in0=s.psum[j][0:1, 0:512],
                scalar1=1.0 / D, scalar2=None, op0=ALU.mult))
            s.op("vector", lambda e, j=j: e.tensor_scalar(
                out=s.rrow[:, j * 512:(j + 1) * 512], in0=s.psum[4 + j][0:1, 0:512],
                scalar1=1.0 / D, scalar2=None, op0=ALU.mult))
        s.chain("vector")
        s.op("vector", lambda e: e.tensor_tensor(
            out=sq[0][0:1, 0:T], in0=s.mrow, in1=s.mrow, op=ALU.mult))
        s.chain("vector")
        s.op("vector", lambda e: e.tensor_tensor(
            out=s.rrow, in0=s.rrow, in1=sq[0][0:1, 0:T], op=ALU.subtract), inc="sv")
        s.chain("vector")
        s.op("vector", lambda e: e.tensor_scalar(out=s.rrow, in0=s.rrow,
             scalar1=float(LNEPS), scalar2=None, op0=ALU.add), inc="sv")
        s.wait("scalar", "sv", s.val("sv"))
        s.op("scalar", lambda e: e.activation(out=s.rrow, in_=s.rrow, func=AF.Sqrt), inc="ss")
        s.wait("vector", "ss", s.val("ss"))
        s.op("vector", lambda e: e.reciprocal(out=s.rrow, in_=s.rrow), inc="sv")
        svr = s.val("sv")
        s.wait("tensor", "sv", svr)
        for j in range(NT):
            bm_, br_ = s.psum[j % 2], s.psum[2 + j % 2]
            s.op("tensor", lambda e, j=j, b=bm_: e.matmul(
                out=b[:, 0:512], lhsT=s.ones_row,
                rhs=s.mrow[:, j * 512:(j + 1) * 512], start=True, stop=True))
            s.op("tensor", lambda e, j=j, b=br_: e.matmul(
                out=b[:, 0:512], lhsT=s.ones_row,
                rhs=s.rrow[:, j * 512:(j + 1) * 512], start=True, stop=True), inc="stt")
            s.wait("vector", "stt", s.val("stt"))
            s.op("vector", lambda e, j=j, b=bm_: e.tensor_copy(
                out=mb_ap[:, j * 512:(j + 1) * 512], in_=b[:, 0:512]))
            s.op("vector", lambda e, j=j, b=br_: e.tensor_copy(
                out=rb_ap[:, j * 512:(j + 1) * 512], in_=b[:, 0:512]), inc="pst")
            s.wait("tensor", "pst", s.val("pst"))

    def round_pair(self, lanes):
        s = self
        for li, (t, v, d, o) in enumerate(lanes):
            ii = v.bitcast(I32)
            s.op("vector", lambda e, t=t, ii=ii: e.tensor_copy(out=ii, in_=t))
            s.chain("vector")
            s.op("vector", lambda e, d=d, ii=ii: e.tensor_copy(out=d, in_=ii))
            s.chain("vector")
            s.op("vector", lambda e, t=t, d=d: e.tensor_tensor(
                out=t, in0=t, in1=d, op=ALU.subtract))
            s.chain("vector")
            s.op("vector", lambda e, t=t, v=v, d=d: e.scalar_tensor_tensor(
                out=v, in0=t, scalar=-0.5, in1=d, op0=ALU.is_le,
                op1=ALU.subtract))
            s.chain("vector")
            s.op("vector", lambda e, t=t, v=v, o=o: e.scalar_tensor_tensor(
                out=o, in0=t, scalar=0.5, in1=v, op0=ALU.is_ge,
                op1=ALU.subtract), inc="sv")
        s.chain("vector")

    def p_ln(self, layer):
        s = self
        s.phase_barrier()
        s.alloc("mb", T * 4, at=R_D)
        s.alloc("rb", T * 4, at=R_D + 8 * KB)
        s.alloc("sq0", T * 4, at=R_D + 16 * KB)
        s.alloc("sq1", T * 4, at=R_D + 24 * KB)
        s.alloc("xn", T * NCH * 2, at=R_D + 34 * KB)
        s.xn_ap, s.mb_ap, s.rb_ap = s.bf16("xn"), s.f32("mb"), s.f32("rb")
        sq = [s.f32("sq0"), s.f32("sq1")]


        chunk_mark = []
        for c in range(NCH):
            buf = sq[c % 2]
            if c >= 2:
                s.wait("scalar", "stt", chunk_mark[c - 2])
            s.op("scalar", lambda e, c=c, b=buf: e.activation(out=b, in_=s.xc(c), func=AF.Square), inc="ss")
            s.wait("tensor", "ss", s.val("ss"))
            for j in range(NT):
                s.op("tensor", lambda e, c=c, j=j: e.matmul(
                    out=s.psum[j][0:1, 0:512],
                    lhsT=s.ones_col, rhs=s.xc(c)[:, j * 512:(j + 1) * 512],
                    start=(c == 0), stop=(c == NCH - 1)))
            for j in range(NT):
                s.op("tensor", lambda e, c=c, j=j, b=buf: e.matmul(
                    out=s.psum[4 + j][0:1, 0:512],
                    lhsT=s.ones_col, rhs=b[:, j * 512:(j + 1) * 512],
                    start=(c == 0), stop=(c == NCH - 1)),
                    inc="stt" if j == NT - 1 else None)
            chunk_mark.append(s.val("stt"))
        tstats = s.val("stt")

        s.wait("vector", "stt", tstats)
        for j in range(NT):
            s.op("vector", lambda e, j=j: e.tensor_scalar(
                out=s.mrow[:, j * 512:(j + 1) * 512], in0=s.psum[j][0:1, 0:512],
                scalar1=1.0 / D, scalar2=None, op0=ALU.mult))
            s.op("vector", lambda e, j=j: e.tensor_scalar(
                out=s.rrow[:, j * 512:(j + 1) * 512], in0=s.psum[4 + j][0:1, 0:512],
                scalar1=1.0 / D, scalar2=None, op0=ALU.mult))
        s.chain("vector")
        s.op("vector", lambda e: e.tensor_tensor(
            out=sq[0][0:1, 0:T], in0=s.mrow, in1=s.mrow, op=ALU.mult))
        s.chain("vector")
        s.op("vector", lambda e: e.tensor_tensor(
            out=s.rrow, in0=s.rrow, in1=sq[0][0:1, 0:T], op=ALU.subtract), inc="sv")
        s.chain("vector")
        s.op("vector", lambda e: e.tensor_scalar(out=s.rrow, in0=s.rrow,
             scalar1=float(LNEPS), scalar2=None, op0=ALU.add), inc="sv")
        s.wait("scalar", "sv", s.val("sv"))
        s.op("scalar", lambda e: e.activation(out=s.rrow, in_=s.rrow, func=AF.Sqrt), inc="ss")
        s.wait("vector", "ss", s.val("ss"))
        s.op("vector", lambda e: e.reciprocal(out=s.rrow, in_=s.rrow), inc="sv")
        svr = s.val("sv")
        s.wait("tensor", "sv", svr)
        for j in range(NT):
            bm_, br_ = s.psum[j % 2], s.psum[2 + j % 2]
            s.op("tensor", lambda e, j=j, b=bm_: e.matmul(
                out=b[:, 0:512], lhsT=s.ones_row,
                rhs=s.mrow[:, j * 512:(j + 1) * 512], start=True, stop=True))
            s.op("tensor", lambda e, j=j, b=br_: e.matmul(
                out=b[:, 0:512], lhsT=s.ones_row,
                rhs=s.rrow[:, j * 512:(j + 1) * 512], start=True, stop=True), inc="stt")
            s.wait("vector", "stt", s.val("stt"))
            s.op("vector", lambda e, j=j, b=bm_: e.tensor_copy(
                out=s.mb_ap[:, j * 512:(j + 1) * 512], in_=b[:, 0:512]))
            s.op("vector", lambda e, j=j, b=br_: e.tensor_copy(
                out=s.rb_ap[:, j * 512:(j + 1) * 512], in_=b[:, 0:512]), inc="pst")
            if j >= 1:
                s.wait("tensor", "pst", s.val("pst") - 1)
        s.chain("vector")
        for c in range(NCH):
            buf = sq[c % 2]
            s.op("vector", lambda e, c=c, b=buf: e.tensor_tensor(
                out=b, in0=s.xc(c), in1=s.mb_ap, op=ALU.subtract))
            s.chain("vector")
            s.op("vector", lambda e, c=c, b=buf: e.tensor_tensor(
                out=s.xn_ap[:, c * T:(c + 1) * T], in0=b, in1=s.rb_ap,
                op=ALU.mult), inc="sv")
            s.chain("vector")
        s.ln_done_sv = s.val("sv")
        if s.dump == f"xn{layer}":
            s.wait("sync", "sv", s.ln_done_sv)
            s.sth(s.dbg, s.xn_ap)

    def p_l0_qkv(self):
        s = self
        s.phase_barrier()
        s.sth(s.xpark, s.f32("x"))
        s.x_parked = True
        park_v = s.val("st")
        s.wait("scalar", "st", park_v)
        s.wait("vector", "st", park_v)
        s.alloc("q", T * NCH * 2, at=R_X)
        s.alloc("k", T * NCH * 2, at=R_X + 32 * KB)
        s.alloc("v", 16 * 1040 * 2, at=R_D)
        s.alloc("wbufA", D * NCH * 2, at=R_E + 2 * KB)
        s.alloc("wbufB", D * NCH * 2, at=R_E + 18 * KB)
        s.alloc("bvb", D * 4, at=R_E + 34 * KB)
        q, k, v = s.bf16("q"), s.bf16("k"), s.bf16("v")
        s.q_ap, s.k_ap, s.v_ap = q, k, v
        wA, wB = s.bf16("wbufA"), s.bf16("wbufB")
        bvb = s.f32("bvb")

        s.ldg(wA, s.Wq.rearrange("(c p) o -> p c o", p=128), sem="ldgA")
        ldq = ("ldgA", s.val("ldgA"))
        s.ldg(wB, s.Wk.rearrange("(c p) o -> p c o", p=128), sem="ldgB")
        ldk = ("ldgB", s.val("ldgB"))

        for half in range(2):
            b = s.psum[6 + half]
            s.op("tensor", lambda e, h=half, b=b: e.matmul(
                out=b[:, 0:512], lhsT=s.ones_row, rhs=s.bv_row[:, h * 512:(h + 1) * 512],
                start=True, stop=True), inc="stt")
        s.wait("vector", "stt", s.val("stt"))
        for half in range(2):
            s.op("vector", lambda e, h=half: e.tensor_copy(
                out=bvb[:, h * 512:(h + 1) * 512], in_=s.psum[6 + h][:, 0:512]), inc="pst")
        s.op("vector", lambda e: e.memset(v[:, 0:16 * 1040], 1.0))
        s.chain("vector")
        bvb_pst = s.val("pst")

        def proj_fm(dst, wbuf, bias_col, scale, ld_mark, eng):
            s.wait("tensor", ld_mark[0], ld_mark[1])
            pvbase = s.val("pss")
            kidx = 0
            for oc in range(NCH):
                for j in range(NT):
                    bank = s.psum[kidx % 2]
                    s.wait("tensor", "pss", pvbase + kidx - 1)
                    for kc in range(NCH):
                        s.op("tensor", lambda e, oc=oc, j=j, kc=kc, b=bank, w=wbuf: e.matmul(
                            out=b[:, 0:512],
                            lhsT=w[:, kc * D + oc * 128:kc * D + (oc + 1) * 128],
                            rhs=s.xn_ap[:, kc * T + j * 512:kc * T + (j + 1) * 512],
                            start=(kc == 0), stop=(kc == NCH - 1)),
                            inc="stt" if kc == NCH - 1 else None)
                    s.wait(eng, "stt", s.val("stt"))
                    if eng == "scalar":
                        s.op("scalar", lambda e, oc=oc, j=j, b=bank, bc=bias_col, sc=scale: e.activation(
                            out=dst[:, oc * T + j * 512:oc * T + (j + 1) * 512],
                            in_=b[:, 0:512], func=AF.Identity,
                            bias=bc[:, oc:oc + 1], scale=sc), inc="pss")
                    for _ in range(2):
                        if s.prepass_gen is None or s.qkv_budget <= 0:
                            break
                        s.qkv_budget -= 1
                        if next(s.prepass_gen, "END") == "END":
                            s.prepass_gen = None
                    kidx += 1
            return s.val("pss")

        proj_fm(q, wA, s.bqs_sb, 0.125, ldq, "scalar")
        mark = s.chain("tensor")
        s.wait("gpsimd", mark[0], mark[1])
        s.ldg(wA, s.Wv.rearrange("(c p) o -> p c o", p=128), sem="ldgA")
        ldv = ("ldgA", s.val("ldgA"))
        proj_fm(k, wB, s.bk_sb, 1.0, ldk, "scalar")

        s.wait("tensor", ldv[0], ldv[1])
        s.wait("vector", "pst", bvb_pst)
        pvbase = s.val("pst")
        kidx = 0
        for a in range(16):
            for half in range(2):
                bank = s.psum[2 + kidx % 2]
                s.wait("tensor", "pst", pvbase + kidx - 1)
                for kc in range(NCH):
                    s.op("tensor", lambda e, a=a, hf=half, kc=kc, b=bank: e.matmul(
                        out=b[:, 0:512],
                        lhsT=s.xn_ap[:, kc * T + a * 128:kc * T + (a + 1) * 128],
                        rhs=wA[:, kc * D + hf * 512:kc * D + (hf + 1) * 512],
                        start=(kc == 0), stop=(kc == NCH - 1)),
                        inc="stt" if kc == NCH - 1 else None)
                s.wait("vector", "stt", s.val("stt"))
                for hh in range(8):
                    head = half * 8 + hh
                    s.op("vector", lambda e, a=a, hf=half, hh=hh, head=head, b=bank: e.tensor_tensor(
                        out=v[:, a * 1040 + 65 * head: a * 1040 + 65 * head + 64],
                        in0=b[:, hh * 64:(hh + 1) * 64],
                        in1=bvb[:, hf * 512 + hh * 64: hf * 512 + (hh + 1) * 64],
                        op=ALU.add), inc="pst" if hh == 7 else None)
                kidx += 1
        s.qkv_done = s.val("pst")
        if s.dump in ("q0", "k0", "v0"):
            tgt = {"q0": q, "k0": k, "v0": v}[s.dump]
            s.wait("sync", "pst", s.val("pst"))
            s.wait("sync", "pss", s.val("pss"))
            s.sth(s.dbg.rearrange("p n -> p n")[:, 0:tgt.shape[1]], tgt)


    def p_l0_attn(self):
        s = self
        s.phase_barrier()
        s.alloc("wm", D * NCH * 2, at=R_D + 33 * KB)
        s.alloc("probsA", T * 2, at=R_D + 49 * KB)
        s.alloc("probsB", T * 2, at=R_D + 56 * KB)
        s.alloc("rc", 4, at=R_D + 53 * KB)
        s.alloc("atm", 1024 * 2, at=R_D + 54 * KB)
        s.alloc("afm", 512 * NCH * 2, at=R_E + 2 * KB)
        s.alloc("ost", 8 * 512 * 4, at=R_E + 10 * KB)
        q, k, v = s.q_ap, s.k_ap, s.v_ap
        wm = s.bf16("wm")
        probs2 = [s.bf16("probsA"), s.bf16("probsB")]
        rc = s.f32("rc")
        atm = s.bf16("atm")
        afm = s.bf16("afm")
        ost = s.f32("ost")

        s.ldg(wm, s.Wm.rearrange("(c p) o -> p c o", p=128), sem="ldgB")
        s.wait("tensor", "ldgB")

        def qh(h, cols):
            return q[:, (h // 2) * T:][:, cols][64 * (h % 2):64 * (h % 2) + 64, :]

        def kh(h, cols):
            return k[:, (h // 2) * T:][:, cols][64 * (h % 2):64 * (h % 2) + 64, :]

        exp_marks = {}
        pv_marks = []
        pv_stt = {}
        sv_marks = {}
        for qt in range(16):
            nblk = qt + 1
            s.wait("tensor", "pst", s.val("pst"))
            pend = None
            for h in range(H + 1):
                if h < H:
                    probsT = probs2[h % 2]
                    half1 = min(nblk, 8)
                    for hf in range(2 if nblk > 8 else 1):
                        b0, b1 = (0, half1) if hf == 0 else (8, nblk)
                        if exp_marks.get(hf) is not None:
                            s.wait("tensor", exp_marks[hf][0], exp_marks[hf][1])
                        for kt in range(b0, b1):
                            bank = s.psum[2 + kt // 4]
                            st_ = (kt % 4 == 0)
                            sp_ = (kt % 4 == 3 or kt == nblk - 1)
                            s.op("tensor", lambda e, h=h, qt=qt, kt=kt, b=bank,
                                 st_=st_, sp_=sp_: e.matmul(
                                out=b[:, (kt % 4) * 128:(kt % 4) * 128 + 128],
                                lhsT=kh(h, slice(kt * 128, (kt + 1) * 128)),
                                rhs=qh(h, slice(qt * 128, (qt + 1) * 128)),
                                start=st_, stop=sp_),
                                inc="stt" if kt == b1 - 1 else None)
                        s.wait("scalar", "stt", s.val("stt"))
                        if hf == 0 and pv_stt.get(h % 2) is not None:
                            s.wait("scalar", "stt", pv_stt[h % 2])
                        w0, wq = b0 * 128, (b1 - b0) * 128
                        s.op("scalar", lambda e, w0=w0, wq=wq, pt=probsT: e.activation(
                            out=pt[:, w0:w0 + wq],
                            in_=s.psumbig.ap()[:, 1024 + w0:1024 + w0 + wq],
                            func=AF.Exp), inc="pss")
                        exp_marks[hf] = ("pss", s.val("pss"))
                    s.wait("vector", "pss", s.val("pss"))
                    s.op("vector", lambda e, qt=qt, pt=probsT: e.tensor_tensor(
                        out=pt[:, qt * 128:(qt + 1) * 128],
                        in0=pt[:, qt * 128:(qt + 1) * 128], in1=s.triu,
                        op=ALU.mult), inc="sv")
                    sv_marks[h] = ("sv", s.val("sv"))
                if pend is not None:
                    ph, pprobs = pend
                    s.wait("tensor", sv_marks[ph][0], sv_marks[ph][1])
                    pvb = s.psum[6 + ph % 2]
                    if len(pv_marks) >= 2:
                        s.wait("tensor", "pst", pv_marks[-2])
                    for kt in range(nblk):
                        sp_ = (kt == nblk - 1)
                        s.op("tensor", lambda e, ph=ph, kt=kt, b=pvb, sp_=sp_,
                             pt=pprobs: e.matmul(
                            out=b[:, 0:65],
                            lhsT=pt[:, kt * 128:(kt + 1) * 128],
                            rhs=v[:, kt * 1040 + 65 * ph: kt * 1040 + 65 * ph + 65],
                            start=(kt == 0), stop=sp_),
                            inc="stt" if sp_ else None)
                    pv_stt[ph % 2] = s.val("stt")
                    s.wait("vector", "stt", s.val("stt"))
                    s.wait("vector", "pst", s.val("pst"))
                    s.op("vector", lambda e, b=pvb: e.reciprocal(out=rc, in_=b[:, 64:65]))
                    s.chain("vector")
                    s.op("vector", lambda e, ph=ph, b=pvb: e.tensor_scalar(
                        out=atm[:, ph * 64:(ph + 1) * 64], in0=b[:, 0:64],
                        scalar1=rc, scalar2=None, op0=ALU.mult), inc="pst")
                    pv_marks.append(s.val("pst"))
                    if s.prepass_gen is not None:
                        if next(s.prepass_gen, "END") == "END":
                            s.prepass_gen = None
                        if s.bg_pst_mark is not None:
                            pv_marks += [s.bg_pst_mark, s.bg_pst_mark]
                            s.bg_pst_mark = None
                pend = (h, probs2[h % 2]) if h < H else None

            s.chain("vector")
            for pc in range(NCH):
                bank = s.psum[pc % 2]
                bb = bank.bitcast(BF16)
                if pc >= 2:
                    s.wait("tensor", "pst", s.val("pst") - 1)
                s.op("tensor", lambda e, pc=pc, b=bb: e.transpose(
                    out=b[:, 0:128], in_=atm[:, pc * 128:(pc + 1) * 128],
                    identity=s.identb), inc="stt")
                s.wait("vector", "stt", s.val("stt"))
                s.op("vector", lambda e, pc=pc, qt=qt, b=bb: e.tensor_copy(
                    out=afm[:, pc * 512 + (qt % 4) * 128:
                            pc * 512 + (qt % 4) * 128 + 128],
                    in_=b[:, 0:128]), inc="pst")

            if qt % 4 == 3:
                ntg = qt // 4
                s.chain("vector")
                s.wait("vector", "st", s.val("st"))
                s.wait("tensor", "pst", s.val("pst"))
                for oc in range(NCH):
                    bank = s.psum[oc]
                    for kc in range(NCH):
                        s.op("tensor", lambda e, oc=oc, kc=kc, b=bank: e.matmul(
                            out=b[:, 0:512],
                            lhsT=wm[:, kc * D + oc * 128:kc * D + (oc + 1) * 128],
                            rhs=afm[:, kc * 512:(kc + 1) * 512],
                            start=(kc == 0), stop=(kc == NCH - 1)),
                            inc="stt" if kc == NCH - 1 else None)
                    s.wait("vector", "stt", s.val("stt"))
                    s.op("vector", lambda e, oc=oc, b=bank: e.tensor_scalar(
                        out=ost[:, oc * 512:(oc + 1) * 512], in0=b[:, 0:512],
                        scalar1=s.bm_sb[:, oc:oc + 1], scalar2=None,
                        op0=ALU.add), inc="pst")
                s.wait("sync", "pst", s.val("pst"))
                s.sth(s.attn_s.rearrange("p (c t) -> p c t", c=NCH)[:, :, ntg * 512:(ntg + 1) * 512],
                      ost.rearrange("p (c t) -> p c t", c=NCH))
                s.wait("tensor", "pst", s.val("pst"))
                exp_marks = {}
                s.wait("tensor", "pss", s.val("pss"))
        s.attn_stores = s.val("st")
        s.pending_adds.append(s.attn_s)

    def prepass_generator(self):
        s = self
        p = s.pool.ap()
        co = s.spans["consts"][0]
        base = co + 2560
        thr_cols, nthr_cols, mw_cols = [], [], []
        for i in range(3):
            thr_cols.append(p[:, (base + i * 12) // 4:][:, :1])
            nthr_cols.append(p[:, (base + i * 12 + 4) // 4:][:, :1])
            mw_cols.append(p[:, (base + i * 12 + 8) // 4:][:, :1])
        accc = p[:, (base + 40) // 4:][:, :1]
        cell = p[0:1, (base + 44) // 4:][:, :1]
        cell2 = p[0:1, (base + 48) // 4:][:, :1]
        redc = p[:, (base + 52) // 4:][:, :1]
        s.thr_cols, s.nthr_cols, s.mw_cols = thr_cols, nthr_cols, mw_cols
        mroff = s.mrow_off
        ws = [p[:, (mroff + i * 4 * KB) // 4:][:, :1024] for i in range(3)]
        gA = p[:, (198 * KB) // 4:][:, :1024]
        pb = s.pool.ap().bitcast(BF16)
        iqb = [pb[:, (194 * KB) // 2:][:, :1024], pb[:, (202 * KB) // 2:][:, :1024]]
        sems3 = ("ldgA", "ldgB", "ldgC")

        accc_w = [accc, p[:, (base + 60) // 4:][:, :1], p[:, (base + 64) // 4:][:, :1]]
        weights = [(s.wgT, s.iwg_s, 8, 4, D * INNER),
                   (s.wuT, s.iwu_s, 8, 4, D * INNER),
                   (s.wdT, s.iwd_s, 32, 1, D * INNER)]
        chunks = []
        for ps in (1, 2):
            for wi in range(3):
                nn, no = weights[wi][2], weights[wi][3]
                for n in range(nn):
                    for o in range(no):
                        chunks.append((wi, n, o))
        NLC = len(chunks)

        def chunk_ap(t, wi, n, o):
            r = t.rearrange("(n p) o -> p n o", p=128)
            return r[:, n, :][:, o * 1024:(o + 1) * 1024]

        state = {"issued": 0}
        load_marks = [None] * NLC
        cons_marks = [None, None, None]
        st_marks = [None, None]

        def issue_one():
            i = state["issued"]
            if i >= NLC:
                return
            b = i % 3
            wi, n, o = chunks[i]
            if cons_marks[b] is not None:
                s.wait("gpsimd", cons_marks[b][0], cons_marks[b][1])
            pv = s.val(sems3[b])
            if pv > 0:
                s.wait("gpsimd", sems3[b], pv)
            s.ldg(ws[b], chunk_ap(weights[wi][0], wi, n, o), sem=sems3[b])
            load_marks[i] = (sems3[b], s.val(sems3[b]))
            state["issued"] = i + 1

        consumed = 0
        for wi in range(3):
            wsrc, wdst, nn, no, nel = weights[wi]
            acw = accc_w[wi]
            for ci in range(nn * no):
                while state["issued"] - consumed < 3 and state["issued"] < NLC:
                    issue_one()
                b = consumed % 3
                m = load_marks[consumed]
                s.wait("vector", m[0], m[1])
                if ci == 0:
                    s.op("vector", lambda e, bb=ws[b], acw=acw: e.tensor_reduce(
                        out=acw, in_=bb, axis=AX.X, op=ALU.add,
                        apply_absolute_value=True))
                else:
                    s.op("vector", lambda e, bb=ws[b]: e.tensor_reduce(
                        out=redc, in_=bb, axis=AX.X, op=ALU.add,
                        apply_absolute_value=True))
                    s.chain("vector")
                    s.op("vector", lambda e, acw=acw: e.tensor_tensor(
                        out=acw, in0=acw, in1=redc, op=ALU.add))
                cons_marks[b] = s.chain("vector")
                consumed += 1
                yield
        for wi in range(3):
            wsrc, wdst, nn, no, nel = weights[wi]
            acw = accc_w[wi]
            s.wait("tensor", "vch", s.val("vch"))
            s.wait("tensor", "pst", s.val("pst"))
            s.op("tensor", lambda e, acw=acw: e.matmul(
                out=s.psum[7][0:1, 0:1], lhsT=acw, rhs=s.ones_col,
                start=True, stop=True), inc="stt")
            s.wait("vector", "stt", s.val("stt"))
            s.op("vector", lambda e, n=nel: e.tensor_scalar(
                out=cell, in0=s.psum[7][0:1, 0:1], scalar1=1.0 / n,
                scalar2=1e-5, op0=ALU.mult, op1=ALU.max))
            s.chain("vector")
            s.op("vector", lambda e: e.tensor_scalar(
                out=cell2, in0=cell, scalar1=0.5, scalar2=None,
                op0=ALU.mult), inc="sv")
            s.wait("tensor", "sv", s.val("sv"))
            s.op("tensor", lambda e: e.matmul(
                out=s.psum[7][:, 1:2], lhsT=s.ones_row, rhs=cell2,
                start=True, stop=True))
            s.op("tensor", lambda e: e.matmul(
                out=s.psum[7][:, 2:3], lhsT=s.ones_row, rhs=cell,
                start=True, stop=True), inc="stt")
            s.wait("vector", "stt", s.val("stt"))
            s.op("vector", lambda e, wi=wi: e.tensor_copy(
                out=thr_cols[wi], in_=s.psum[7][:, 1:2]))
            s.op("vector", lambda e, wi=wi: e.tensor_copy(
                out=mw_cols[wi], in_=s.psum[7][:, 2:3]))
            s.chain("vector")
            s.op("vector", lambda e, wi=wi: e.tensor_scalar(
                out=nthr_cols[wi], in0=thr_cols[wi], scalar1=-1.0,
                scalar2=None, op0=ALU.mult), inc="pst")
            s.bg_pst_mark = s.val("pst")
            nthr_mark = ("pst", s.val("pst"))
            yield
            for ci in range(nn * no):
                while state["issued"] - consumed < 3 and state["issued"] < NLC:
                    issue_one()
                b = consumed % 3
                qb = consumed % 2
                wi2, n, o = chunks[consumed]
                m = load_marks[consumed]
                for hf in range(2):
                    cs = slice(hf * 512, hf * 512 + 512)
                    s.wait("vector", m[0], m[1])
                    if ci == 0 and hf == 0:
                        s.wait("vector", nthr_mark[0], nthr_mark[1])
                    if hf == 0 and st_marks[qb] is not None:
                        s.wait("vector", "st", st_marks[qb])
                    s.op("vector", lambda e, bb=ws[b], wi=wi, cs=cs: e.tensor_scalar(
                        out=gA[:, cs], in0=bb[:, cs], scalar1=nthr_cols[wi],
                        scalar2=None, op0=ALU.is_lt))
                    s.chain("vector")
                    s.op("vector", lambda e, bb=ws[b], wi=wi, q=qb, cs=cs:
                         e.scalar_tensor_tensor(
                            out=iqb[q][:, cs], in0=bb[:, cs], scalar=thr_cols[wi],
                            in1=gA[:, cs], op0=ALU.is_gt, op1=ALU.subtract))
                    cmk = s.chain("vector")
                    if hf == 1:
                        cons_marks[b] = cmk
                        s.wait("sync", cmk[0], cmk[1])
                        s.sth(chunk_ap(wdst, wi, n, o), iqb[qb])
                        st_marks[qb] = s.val("st")
                        consumed += 1
                    yield

    def p_l0_ffn(self):
        s = self
        s.phase_barrier()
        s.alloc("mb2", T * 4, at=R_D)
        s.alloc("rb2", T * 4, at=R_D + 8 * KB)
        s.alloc("oscx", 512 * 4, at=R_D + 16 * KB)
        s.alloc("scx", 512 * 4, at=R_D + 18 * KB)
        s.alloc("ixs", 8 * 512 * 2, at=R_D + 20 * KB)
        s.alloc("am", 512 * 4, at=R_D + 28 * KB)
        s.alloc("cm", 512 * 4, at=R_D + 30 * KB)
        s.alloc("hbuf", 32 * 512 * 4, at=R_D + 32 * KB)
        s.alloc("scb", 512 * 4, at=188 * KB)
        s.alloc("oscb", 512 * 4, at=190 * KB)
        s.alloc("vt", 1024 * 4, at=192 * KB)
        s.alloc("dt", 1024 * 4, at=196 * KB)
        s.alloc("ihg", 1024 * 2, at=200 * KB)
        s.alloc("rep0", 512, at=202 * KB)
        s.alloc("rep1", 512, at=202 * KB + 512)
        s.alloc("sccols", 64, at=203 * KB)
        mb, rb = s.f32("mb2"), s.f32("rb2")
        oscx, scx = s.f32("oscx"), s.f32("scx")
        ixs = s.bf16("ixs")
        am, cm = s.f32("am"), s.f32("cm")
        hb = s.f32("hbuf")
        scb, oscb = s.f32("scb"), s.f32("oscb")
        vt, dt = s.f32("vt"), s.f32("dt")
        ihg = s.bf16("ihg")
        s.rep_aps = [s.f32("rep0"), s.f32("rep1")]
        s.dbg_sb_tmp = s.bf16("ixs") if False else s.pool.ap().bitcast(BF16)[:, (R_D + 20 * KB) // 2:][:, :16 * 512]
        sc_span = s.f32("sccols")
        s.sccol = sc_span[:, 0:1]
        s.sccol2 = sc_span[:, 1:2]
        p = s.pool.ap()
        mroff = s.mrow_off
        gbuf = [s.pool.ap().bitcast(BF16)[:, (mroff + i * 2048) // 2:][:, :1024]
                for i in range(4)]
        dbuf = [s.pool.ap().bitcast(BF16)[:, (mroff + i * 8192) // 2:][:, :4096]
                for i in range(2)]

        s.ldh(s.f32("x"), s.xpark)
        v = s.val("ldh")
        s.wait("vector", "ldh", v)
        s.wait("scalar", "ldh", v)
        s.wait("tensor", "ldh", v)
        s.x_parked = False

        s.stats_mb_rb(mb, rb, hb[:, 0:T], hb[:, T:2 * T])

        if s.prepass_gen is not None:
            for _ in s.prepass_gen:
                pass
            s.prepass_gen = None
        for sm in ("ldgA", "ldgB", "ldgC", "st"):
            v = s.val(sm)
            s.wait("vector", sm, v)
            s.wait("scalar", sm, v)
            s.wait("sync", sm, v)
            s.wait("tensor", sm, v)
        s.wait("tensor", "pst", s.val("pst"))
        mw_cols = s.mw_cols

        evac_marks = []
        for ti in range(NT):
            tsl = slice(ti * 512, (ti + 1) * 512)
            s.chain("vector")
            for c in range(NCH):
                st_ap = hb[:, c * 512:(c + 1) * 512]
                s.op("vector", lambda e, c=c, a=st_ap, tsl=tsl: e.tensor_tensor(
                    out=a, in0=s.xc(c)[:, tsl], in1=mb[:, tsl], op=ALU.subtract))
                s.chain("vector")
                amn = dt[:, 0:512]
                if c == 0:
                    s.op("vector", lambda e, a=st_ap: e.tensor_copy(out=am, in_=a))
                    s.chain("vector")
                    s.op("vector", lambda e, a=st_ap, amn=amn: e.tensor_copy(out=amn, in_=a))
                else:
                    s.op("vector", lambda e, a=st_ap: e.tensor_tensor(
                        out=am, in0=am, in1=a, op=ALU.max))
                    s.chain("vector")
                    s.op("vector", lambda e, a=st_ap, amn=amn: e.tensor_tensor(
                        out=amn, in0=amn, in1=a, op=ALU.min))
                s.chain("vector")
            amn = dt[:, 0:512]
            s.op("vector", lambda e, amn=amn: e.tensor_scalar(
                out=amn, in0=amn, scalar1=-1.0, scalar2=None, op0=ALU.mult))
            s.chain("vector")
            s.op("vector", lambda e, amn=amn: e.tensor_tensor(
                out=am, in0=am, in1=amn, op=ALU.max))
            amk = s.chain("vector")
            self.scales_from_colmax(am, scx, oscx, amk)
            s.op("vector", lambda e, tsl=tsl: e.tensor_tensor(
                out=oscx, in0=oscx, in1=rb[:, tsl], op=ALU.mult))
            s.chain("vector")
            s.op("vector", lambda e: e.tensor_scalar(
                out=oscx, in0=oscx, scalar1=1e-5, scalar2=1.0 / 127.0,
                op0=ALU.max, op1=ALU.mult))
            s.chain("vector")
            dAx = s.pool.ap().bitcast(BF16)[:, (R_D + 48 * KB) // 2:][:, :1024]
            dBx = s.pool.ap().bitcast(BF16)[:, (R_D + 50 * KB) // 2:][:, :1024]
            for pr in range(2):
                for c in range(4 * pr, 4 * pr + 4):
                    a = hb[:, c * 512:(c + 1) * 512]
                    s.op("vector", lambda e, c=c, a=a: e.tensor_tensor(
                        out=a, in0=a, in1=scx, op=ALU.mult))
                    s.chain("vector")
                g0, g1 = 2 * pr, 2 * pr + 1
                s.round_pair([
                    (hb[:, g0 * 1024:(g0 + 1) * 1024], vt, dAx,
                     ixs[:, g0 * 1024:(g0 + 1) * 1024]),
                    (hb[:, g1 * 1024:(g1 + 1) * 1024], dt, dBx,
                     ixs[:, g1 * 1024:(g1 + 1) * 1024])])
            ix_done = s.chain("vector")
            s.wait("tensor", ix_done[0], ix_done[1])
            s.wait("scalar", ix_done[0], ix_done[1])
            if s.dump == "ix0" and ti == 0:
                s.wait("sync", ix_done[0], ix_done[1])
                s.sth(s.dbg[:, 0:8 * 512], ixs)
                s.wait("vector", "st", s.val("st"))
                s.wait("tensor", "st", s.val("st"))

            s.op("vector", lambda e: e.memset(cm, 0.0))
            s.chain("vector")
            cmn = dt[:, 512:1024]
            s.op("vector", lambda e, cmn=cmn: e.memset(cmn, 0.0))
            s.chain("vector")
            ldmarks = {}
            ssm = {}
            for oc in range(ICH + 1):
                if oc < ICH:
                    gb, ub = gbuf[2 * (oc % 2)], gbuf[2 * (oc % 2) + 1]
                    if oc >= 2:
                        s.wait("sync", ldmarks[oc - 2][0], ldmarks[oc - 2][1])
                    else:
                        mk = s.chain("tensor")
                        s.wait("sync", mk[0], mk[1])
                    psem = "lha" if oc % 2 == 0 else "lhb"
                    s.ldh(gb, s.iwg_s.rearrange("(n p) o -> p n o", p=128)[:, :, oc * 128:(oc + 1) * 128], sem=psem)
                    s.ldh(ub, s.iwu_s.rearrange("(n p) o -> p n o", p=128)[:, :, oc * 128:(oc + 1) * 128], sem=psem)
                    s.wait("tensor", psem, s.val(psem))
                    if len(evac_marks) >= 2:
                        s.wait("tensor", "pst", evac_marks[-2])
                    bg, bu = s.psum[oc % 3], s.psum[3 + oc % 3]
                    for kc in range(NCH):
                        s.op("tensor", lambda e, kc=kc, b=bg, gb=gb: e.matmul(
                            out=b[:, 0:512], lhsT=gb[:, kc * 128:(kc + 1) * 128],
                            rhs=ixs[:, kc * 512:(kc + 1) * 512],
                            start=(kc == 0), stop=(kc == NCH - 1)))
                    for kc in range(NCH):
                        s.op("tensor", lambda e, kc=kc, b=bu, ub=ub: e.matmul(
                            out=b[:, 0:512], lhsT=ub[:, kc * 128:(kc + 1) * 128],
                            rhs=ixs[:, kc * 512:(kc + 1) * 512],
                            start=(kc == 0), stop=(kc == NCH - 1)),
                            inc="stt" if kc == NCH - 1 else None)
                    mk = s.chain("tensor")
                    ldmarks[oc] = mk
                    s.wait("vector", "stt", s.val("stt"))
                    gs = vt[:, (oc % 2) * 512:(oc % 2) * 512 + 512]
                    sil = scb if oc % 2 == 0 else oscb
                    s.op("vector", lambda e, b=bg, gs=gs: e.scalar_tensor_tensor(
                        out=gs, in0=b[:, 0:512], scalar=mw_cols[0], in1=oscx,
                        op0=ALU.mult, op1=ALU.mult), inc="pst")
                    mkv = s.chain("vector")
                    s.wait("scalar", mkv[0], mkv[1])
                    s.op("scalar", lambda e, gs=gs, sil=sil: e.activation(
                        out=sil, in_=gs, func=AF.Silu), inc="ss")
                    ssm[oc] = s.val("ss")
                if oc >= 1:
                    po = oc - 1
                    pbu = s.psum[3 + po % 3]
                    psil = scb if po % 2 == 0 else oscb
                    s.wait("vector", "ss", ssm[po])
                    s.op("vector", lambda e, po=po, b=pbu, sil=psil: e.tensor_tensor(
                        out=hb[:, po * 512:(po + 1) * 512], in0=b[:, 0:512],
                        in1=sil, op=ALU.mult), inc="pst")
                    evac_marks.append(s.val("pst"))
                    s.chain("vector")
                    s.op("vector", lambda e, po=po: e.tensor_tensor(
                        out=cm, in0=cm, in1=hb[:, po * 512:(po + 1) * 512], op=ALU.max))
                    s.chain("vector")
                    s.op("vector", lambda e, po=po, cmn=cmn: e.tensor_tensor(
                        out=cmn, in0=cmn, in1=hb[:, po * 512:(po + 1) * 512], op=ALU.min))
                    s.chain("vector")
            s.op("vector", lambda e, cmn=cmn: e.tensor_scalar(
                out=cmn, in0=cmn, scalar1=-1.0, scalar2=None, op0=ALU.mult))
            s.chain("vector")
            s.op("vector", lambda e, cmn=cmn: e.tensor_tensor(
                out=cm, in0=cm, in1=cmn, op=ALU.max))
            cmk = s.chain("vector")
            if s.dump == "h0" and ti == 0:
                s.wait("sync", cmk[0], cmk[1])
                s.op("vector", lambda e: e.tensor_copy(
                    out=s.dbg_sb_tmp, in_=hb[:, 0:16 * 512]), inc="sv")
                s.chain("vector")
                s.wait("sync", "sv", s.val("sv"))
                s.sth(s.dbg[:, 0:16 * 512], s.dbg_sb_tmp)
                s.wait("vector", "st", s.val("st"))
                s.wait("tensor", "st", s.val("st"))
            self.scales_from_colmax(cm, scb, oscb, cmk)
            s.op("vector", lambda e: e.tensor_tensor(
                out=oscb, in0=oscb, in1=oscx, op=ALU.mult))
            s.chain("vector")
            s.op("vector", lambda e: e.tensor_scalar(
                out=oscb, in0=oscb, scalar1=mw_cols[1], scalar2=1e-5,
                op0=ALU.mult, op1=ALU.max))
            s.chain("vector")
            s.op("vector", lambda e: e.tensor_scalar(
                out=oscb, in0=oscb, scalar1=mw_cols[2], scalar2=1.0 / 127.0,
                op0=ALU.mult, op1=ALU.mult))
            s.chain("vector")

            dstream_marks = {}
            pb_ = s.pool.ap().bitcast(BF16)
            dAh = pb_[:, (R_D + 20 * KB) // 2:][:, :1024]
            dBh = pb_[:, (R_D + 22 * KB) // 2:][:, :1024]
            ihgB = pb_[:, (R_D + 24 * KB) // 2:][:, :1024]
            for g in range(NCH):
                db = dbuf[g % 2]
                if g >= 2:
                    s.wait("sync", dstream_marks[g - 2][0], dstream_marks[g - 2][1])
                else:
                    mk = s.chain("tensor")
                    s.wait("sync", mk[0], mk[1])
                dsem = "lha" if g % 2 == 0 else "lhb"
                s.ldh(db, s.iwd_s.rearrange("(n p) o -> p n o", p=128)[:, 4 * g:4 * g + 4, :], sem=dsem)
                dlw = (dsem, s.val(dsem))
                for kc in range(4 * g, 4 * g + 4):
                    a = hb[:, kc * 512:(kc + 1) * 512]
                    s.op("vector", lambda e, a=a: e.tensor_tensor(
                        out=a, in0=a, in1=scb, op=ALU.mult))
                    s.chain("vector")
                mk = s.chain("tensor")
                s.wait("vector", mk[0], mk[1])
                s.round_pair([
                    (hb[:, (4 * g) * 512:(4 * g + 2) * 512], vt, dAh, ihg),
                    (hb[:, (4 * g + 2) * 512:(4 * g + 4) * 512], dt, dBh, ihgB)])
                rg = s.chain("vector")
                if s.dump == "ih0" and ti == 0 and g == 0:
                    s.wait("sync", rg[0], rg[1])
                    s.sth(s.dbg[:, 0:1024], ihg)
                    s.wait("vector", "st", s.val("st"))
                    s.wait("tensor", "st", s.val("st"))
                s.wait("tensor", rg[0], rg[1])
                s.wait("tensor", dlw[0], dlw[1])
                for oc8 in range(NCH):
                    for kci in range(4):
                        last = (g == NCH - 1 and kci == 3)
                        stt_ = (g == 0 and kci == 0)
                        rhs_ = ihg if kci < 2 else ihgB
                        c2 = kci % 2
                        s.op("tensor", lambda e, oc8=oc8, kci=kci, db=db, c2=c2,
                             rhs_=rhs_, st_=stt_, last=last: e.matmul(
                            out=s.psum[oc8][:, 0:512],
                            lhsT=db[:, kci * 1024 + oc8 * 128:kci * 1024 + (oc8 + 1) * 128],
                            rhs=rhs_[:, c2 * 512:(c2 + 1) * 512],
                            start=st_, stop=last),
                            inc="stt" if (oc8 == NCH - 1 and kci == 3) else None)
                mk = s.chain("tensor")
                dstream_marks[g] = mk
            s.wait("vector", "stt", s.val("stt"))
            if s.dump == "dps0" and ti == 0:
                s.op("vector", lambda e: e.tensor_copy(
                    out=s.dbg_sb_tmp[:, 0:512], in_=s.psum[0][:, 0:512]), inc="sv")
                s.chain("vector")
                s.wait("sync", "sv", s.val("sv"))
                s.sth(s.dbg[:, 0:512], s.dbg_sb_tmp[:, 0:512])
                s.wait("vector", "st", s.val("st"))
            for oc8 in range(NCH):
                s.op("vector", lambda e, oc8=oc8: e.tensor_tensor(
                    out=vt[:, 0:512], in0=s.psum[oc8][:, 0:512], in1=oscb, op=ALU.mult))
                s.chain("vector")
                s.op("vector", lambda e, oc8=oc8, tsl=tsl: e.tensor_tensor(
                    out=s.xc(oc8)[:, tsl], in0=s.xc(oc8)[:, tsl], in1=vt[:, 0:512],
                    op=ALU.add), inc="pst")
                s.chain("vector")
            evac_marks.append(s.val("pst"))
            dn = s.chain("vector")
            s.wait("tensor", dn[0], dn[1])

        mkv = s.chain("vector")
        s.wait("sync", mkv[0], mkv[1])
        s.ldh(hb[:, 0:NCH * T], s.attn_s)
        v2 = s.val("ldh")
        s.wait("vector", "ldh", v2)
        for c in range(NCH):
            s.op("vector", lambda e, c=c: e.tensor_tensor(
                out=s.xc(c), in0=s.xc(c), in1=hb[:, c * T:(c + 1) * T],
                op=ALU.add), inc="sv")
        s.chain("vector")
        s.pending_adds = []

    def scales_from_colmax(self, am_ap, sc_ap, osc_ap, amk):
        s = self
        s.wait("tensor", amk[0], amk[1])
        for t4 in range(4):
            bank = s.psum[4 + t4 % 2]
            s.op("tensor", lambda e, t4=t4, b=bank: e.transpose(
                out=b[:, 0:128], in_=am_ap[:, t4 * 128:(t4 + 1) * 128],
                identity=s.ident), inc="stt")
            s.wait("vector", "stt", s.val("stt"))
            s.op("vector", lambda e, b=bank: e.tensor_reduce(
                out=s.sccol, in_=b[:, 0:128], axis=AX.X, op=ALU.max))
            s.chain("vector")
            s.op("vector", lambda e: e.tensor_copy(out=s.sccol2, in_=s.sccol))
            s.chain("vector")
            s.op("vector", lambda e: e.tensor_scalar(
                out=s.sccol, in0=s.sccol, scalar1=1e-20, scalar2=None, op0=ALU.max))
            s.chain("vector")
            s.op("vector", lambda e: e.reciprocal(out=s.sccol, in_=s.sccol))
            s.chain("vector")
            s.op("vector", lambda e: e.tensor_scalar(
                out=s.sccol, in0=s.sccol, scalar1=127.0, scalar2=None, op0=ALU.mult))
            s.chain("vector")
            s.op("vector", lambda e: e.tensor_scalar(
                out=s.rep_aps[0], in0=s.ones128, scalar1=s.sccol, scalar2=None,
                op0=ALU.mult))
            s.chain("vector")
            s.op("vector", lambda e: e.tensor_scalar(
                out=s.rep_aps[1], in0=s.ones128, scalar1=s.sccol2, scalar2=None,
                op0=ALU.mult), inc="sv")
            rv = s.val("sv")
            s.wait("tensor", "sv", rv)
            b2 = s.psum[6]
            s.op("tensor", lambda e, b=b2: e.transpose(
                out=b[:, 0:128], in_=s.rep_aps[0], identity=s.ident))
            s.op("tensor", lambda e, b=b2: e.transpose(
                out=b[:, 128:256], in_=s.rep_aps[1], identity=s.ident), inc="stt")
            s.wait("vector", "stt", s.val("stt"))
            s.op("vector", lambda e, t4=t4, b=b2: e.tensor_copy(
                out=sc_ap[:, t4 * 128:(t4 + 1) * 128], in_=b[:, 0:128]))
            s.op("vector", lambda e, t4=t4, b=b2: e.tensor_copy(
                out=osc_ap[:, t4 * 128:(t4 + 1) * 128], in_=b[:, 128:256]), inc="pst")
            s.wait("tensor", "pst", s.val("pst"))
        s.chain("vector")


    def p_l1_attn(self):
        s = self
        s.phase_barrier()
        s.sth(s.xpark, s.f32("x"))
        s.x_parked = True
        pk = s.val("st")
        s.wait("scalar", "st", pk)
        s.wait("vector", "st", pk)
        s.alloc("qe", T * NCH * 2, at=R_X)
        s.alloc("ke", 16 * D * 2, at=R_X + 32 * KB)
        s.alloc("v1", 16 * D * 2, at=R_D)
        s.alloc("wstr", D * NCH * 2, at=R_E)
        s.alloc("sel8", 1024 * 4, at=R_E + 16 * KB)
        s.alloc("kvsb", 512 * 2, at=R_E + 20 * KB)
        s.alloc("ks", 8 * 16 * 2, at=R_E + 21 * KB)
        s.alloc("qks", T * 4, at=R_E + 22 * KB)
        s.alloc("rden", T * 4, at=R_E + 30 * KB)
        s.alloc("elu0", 512 * 4, at=R_E + 38 * KB)
        s.alloc("elu1", 512 * 4, at=R_E + 40 * KB)
        s.alloc("denb", 512 * 4, at=R_E + 42 * KB)
        s.alloc("ost2", 8 * 512 * 4, at=R_D)
        qe, ke, v1 = s.bf16("qe"), s.bf16("ke"), s.bf16("v1")
        wstr = s.bf16("wstr")
        sel8 = s.f32("sel8")[0:16, :]
        kvsb, ks = s.bf16("kvsb"), s.bf16("ks")
        qks = s.f32("qks")[0:16, :]
        rden = s.f32("rden")[0:16, :]
        m0, e1 = s.f32("elu0"), s.f32("elu1")
        denb = s.f32("denb")
        s.ldh(sel8, s.sel8d)
        s.wait("tensor", "ldh", s.val("ldh"))

        def elu_evac(bank, dst, w):
            s.op("vector", lambda e, b=bank, w=w: e.tensor_scalar(
                out=m0[:, 0:w], in0=b[:, 0:w], scalar1=0.0, scalar2=None,
                op0=ALU.min), inc="pst")
            mk = s.chain("vector")
            s.wait("scalar", mk[0], mk[1])
            s.op("scalar", lambda e, w=w: e.activation(
                out=e1[:, 0:w], in_=m0[:, 0:w], func=AF.Exp), inc="ss")
            s.wait("vector", "ss", s.val("ss"))
            s.op("vector", lambda e, b=bank, d=dst, w=w: e.scalar_tensor_tensor(
                out=d, in0=b[:, 0:w], scalar=0.0, in1=e1[:, 0:w],
                op0=ALU.max, op1=ALU.add), inc="pst")
            s.chain("vector")

        s.ldg(wstr, s.Wqkv.rearrange("(c p) o -> p c o", p=128)[:, :, 0:D], sem="ldgA")
        s.wait("tensor", "ldgA")
        kidx = 0
        for oc in range(NCH):
            for j in range(NT):
                bank = s.psum[kidx % 3]
                s.wait("tensor", "pst", s.val("pst") - 4)
                for kc in range(NCH):
                    s.op("tensor", lambda e, oc=oc, j=j, kc=kc, b=bank: e.matmul(
                        out=b[:, 0:512],
                        lhsT=wstr[:, kc * D + oc * 128:kc * D + (oc + 1) * 128],
                        rhs=s.xn_ap[:, kc * T + j * 512:kc * T + (j + 1) * 512],
                        start=(kc == 0), stop=(kc == NCH - 1)),
                        inc="stt" if kc == NCH - 1 else None)
                s.wait("vector", "stt", s.val("stt"))
                elu_evac(bank, qe[:, oc * T + j * 512:oc * T + (j + 1) * 512], 512)
                kidx += 1
        mk = s.chain("tensor")
        s.wait("gpsimd", mk[0], mk[1])
        s.ldg(wstr, s.Wqkv.rearrange("(c p) o -> p c o", p=128)[:, :, D:2 * D], sem="ldgB")
        s.wait("tensor", "ldgB")
        kidx = 0
        for a in range(16):
            for half in range(2):
                bank = s.psum[3 + kidx % 3]
                s.wait("tensor", "pst", s.val("pst") - 4)
                for kc in range(NCH):
                    s.op("tensor", lambda e, a=a, hf=half, kc=kc, b=bank: e.matmul(
                        out=b[:, 0:512],
                        lhsT=s.xn_ap[:, kc * T + a * 128:kc * T + (a + 1) * 128],
                        rhs=wstr[:, kc * D + hf * 512:kc * D + (hf + 1) * 512],
                        start=(kc == 0), stop=(kc == NCH - 1)),
                        inc="stt" if kc == NCH - 1 else None)
                s.wait("vector", "stt", s.val("stt"))
                elu_evac(bank, ke[:, a * D + half * 512:a * D + (half + 1) * 512], 512)
                kidx += 1
        mk = s.chain("tensor")
        s.wait("gpsimd", mk[0], mk[1])
        s.ldg(wstr, s.Wqkv.rearrange("(c p) o -> p c o", p=128)[:, :, 2 * D:3 * D], sem="ldgA")
        s.wait("tensor", "ldgA")
        kidx = 0
        ploop = s.val("pst")
        for a in range(16):
            for half in range(2):
                bank = s.psum[3 + kidx % 3]
                s.wait("tensor", "pst", max(s.val("pst") - 2, ploop))
                for kc in range(NCH):
                    s.op("tensor", lambda e, a=a, hf=half, kc=kc, b=bank: e.matmul(
                        out=b[:, 0:512],
                        lhsT=s.xn_ap[:, kc * T + a * 128:kc * T + (a + 1) * 128],
                        rhs=wstr[:, kc * D + hf * 512:kc * D + (hf + 1) * 512],
                        start=(kc == 0), stop=(kc == NCH - 1)),
                        inc="stt" if kc == NCH - 1 else None)
                s.wait("vector", "stt", s.val("stt"))
                s.op("vector", lambda e, a=a, hf=half, b=bank: e.tensor_copy(
                    out=v1[:, a * D + hf * 512:a * D + (hf + 1) * 512],
                    in_=b[:, 0:512]), inc="pst")
                kidx += 1
        prj = s.chain("vector")
        s.wait("tensor", prj[0], prj[1])

        s.op("vector", lambda e: e.memset(ks, 0.0), inc="pst")
        s.chain("vector")
        s.wait("tensor", "pst", s.val("pst"))
        for hp in range(8):
            bank = s.psum[4 + hp % 2]
            for parity in range(2):
                h = 2 * hp + parity
                base = 64 * parity
                for kt in range(16):
                    s.op("tensor", lambda e, h=h, kt=kt, b=bank, base=base: e.matmul(
                        out=b[base:base + 64, 0:64],
                        lhsT=ke[:, kt * D + h * 64:kt * D + h * 64 + 64],
                        rhs=v1[:, kt * D + h * 64:kt * D + h * 64 + 64],
                        start=(kt == 0), stop=(kt == 15)),
                        inc="stt" if kt == 15 else None)
            s.wait("vector", "stt", s.val("stt"))
            s.op("vector", lambda e, hp=hp, b=bank: e.tensor_copy(
                out=kvsb[:, hp * 64:(hp + 1) * 64], in_=b[:, 0:64]), inc="pst")
            if hp >= 1:
                s.wait("tensor", "pst", s.val("pst") - 1)
        for h in range(H):
            bank = s.psum[6]
            base = 64 * (h % 2)
            for kt in range(16):
                s.op("tensor", lambda e, h=h, kt=kt, b=bank, base=base: e.matmul(
                    out=b[base:base + 64, h:h + 1],
                    lhsT=ke[:, kt * D + h * 64:kt * D + h * 64 + 64],
                    rhs=s.onesb_col,
                    start=(kt == 0), stop=(kt == 15)),
                    inc="stt" if kt == 15 else None)
            s.wait("vector", "stt", s.val("stt"))
            s.op("vector", lambda e, h=h, b=bank, base=base: e.tensor_copy(
                out=ks[base:base + 64, (h // 2) * 16 + h:(h // 2) * 16 + h + 1],
                in_=b[base:base + 64, h:h + 1]), inc="pst")
            s.wait("tensor", "pst", s.val("pst"))
        for j in range(NT):
            bank = s.psum[7]
            for kc in range(NCH):
                s.op("tensor", lambda e, j=j, kc=kc, b=bank: e.matmul(
                    out=b[0:16, 0:512],
                    lhsT=ks[:, kc * 16:(kc + 1) * 16],
                    rhs=qe[:, kc * T + j * 512:kc * T + (j + 1) * 512],
                    start=(kc == 0), stop=(kc == NCH - 1)),
                    inc="stt" if kc == NCH - 1 else None)
            s.wait("vector", "stt", s.val("stt"))
            s.op("vector", lambda e, j=j, b=bank: e.tensor_scalar(
                out=qks[:, j * 512:(j + 1) * 512], in0=b[0:16, 0:512],
                scalar1=1e-6, scalar2=None, op0=ALU.add), inc="pst")
            s.wait("tensor", "pst", s.val("pst"))
        s.chain("vector")
        s.op("vector", lambda e: e.reciprocal(out=rden, in_=qks), inc="sv")
        s.chain("vector")
        rdv = s.val("sv")
        s.wait("tensor", "sv", rdv)

        af1 = s.bf16("ke")
        kidx = 0
        for hp in range(8):
            for j in range(NT):
                bank = s.psum[kidx % 2]
                bden = s.psum[2 + kidx % 2]
                s.wait("tensor", "pst", s.val("pst") - 2)
                for parity in range(2):
                    h = 2 * hp + parity
                    base = 64 * parity
                    s.op("tensor", lambda e, h=h, hp=hp, j=j, b=bank, base=base: e.matmul(
                        out=b[base:base + 64, 0:512],
                        lhsT=kvsb[base:base + 64, hp * 64:(hp + 1) * 64],
                        rhs=qe[base:base + 64, (h // 2) * T + j * 512:(h // 2) * T + (j + 1) * 512],
                        start=True, stop=True))
                s.op("tensor", lambda e, hp=hp, j=j, b=bden: e.matmul(
                    out=b[:, 0:512], lhsT=sel8[:, hp * 128:(hp + 1) * 128],
                    rhs=rden[:, j * 512:(j + 1) * 512],
                    start=True, stop=True), inc="stt")
                s.wait("vector", "stt", s.val("stt"))
                s.op("vector", lambda e, b=bden: e.tensor_copy(out=denb, in_=b[:, 0:512]), inc="pst")
                s.chain("vector")
                s.op("vector", lambda e, hp=hp, j=j, b=bank: e.tensor_tensor(
                    out=af1[:, hp * T + j * 512:hp * T + (j + 1) * 512],
                    in0=b[:, 0:512], in1=denb, op=ALU.mult), inc="pst")
                s.chain("vector")
                kidx += 1
        afd = s.chain("vector")
        s.wait("tensor", afd[0], afd[1])
        s.wait("gpsimd", afd[0], afd[1])

        s.ldg(wstr, s.Wout.rearrange("(c p) o -> p c o", p=128), sem="ldgB")
        s.wait("tensor", "ldgB")
        ost2 = s.f32("ost2")
        for j in range(NT):
            s.wait("vector", "st", s.val("st"))
            for oc in range(NCH):
                bank = s.psum[oc]
                for kc in range(NCH):
                    s.op("tensor", lambda e, oc=oc, j=j, kc=kc, b=bank: e.matmul(
                        out=b[:, 0:512],
                        lhsT=wstr[:, kc * D + oc * 128:kc * D + (oc + 1) * 128],
                        rhs=af1[:, kc * T + j * 512:kc * T + (j + 1) * 512],
                        start=(kc == 0), stop=(kc == NCH - 1)),
                        inc="stt" if kc == NCH - 1 else None)
                s.wait("vector", "stt", s.val("stt"))
                s.op("vector", lambda e, oc=oc, b=bank: e.tensor_scalar(
                    out=ost2[:, oc * 512:(oc + 1) * 512], in0=b[:, 0:512],
                    scalar1=s.bout_sb[:, oc:oc + 1], scalar2=None,
                    op0=ALU.add), inc="pst")
            s.wait("sync", "pst", s.val("pst"))
            s.sth(s.attn_s.rearrange("p (c t) -> p c t", c=NCH)[:, :, j * 512:(j + 1) * 512],
                  ost2.rearrange("p (c t) -> p c t", c=NCH))
            s.wait("tensor", "pst", s.val("pst"))
        s.pending_adds.append(s.attn_s)
        s.l1attn_st = s.val("st")


    def p_l1_ffn(self):
        s = self
        s.phase_barrier()

        def hchunk(kc):
            off = (R_X + kc * 4 * KB) if kc < 24 else (158 * KB + (kc - 24) * 4 * KB)
            return s.pool.ap().bitcast(BF16)[:, off // 2: off // 2 + T]

        bfspan = lambda off, n: s.pool.ap().bitcast(BF16)[:, off // 2: off // 2 + n]
        fspan = lambda off, n: s.pool.ap()[:, off // 4: off // 4 + n]
        wb1 = [bfspan(190 * KB + i * 2 * KB, 1024) for i in range(4)]
        sil = fspan(198 * KB, 512)
        us = fspan(200 * KB, 512)

        tmarks = {}
        emarks = []
        for oc in range(ICH):
            g0, u0 = wb1[2 * (oc % 2)], wb1[2 * (oc % 2) + 1]
            psem = "ldgA" if oc % 2 == 0 else "ldgB"
            if oc >= 2:
                s.wait("gpsimd", tmarks[oc - 2][0], tmarks[oc - 2][1])
            s.ldg(g0, s.w1T.rearrange("(c p) o -> p c o", p=128)[:, :, oc * 128:(oc + 1) * 128], sem=psem)
            s.ldg(u0, s.w2T.rearrange("(c p) o -> p c o", p=128)[:, :, oc * 128:(oc + 1) * 128], sem=psem)
            s.wait("tensor", psem, s.val(psem))
            for j in range(NT):
                n = oc * NT + j
                if n >= 2:
                    s.wait("tensor", "pss", emarks[n - 2][0])
                    s.wait("tensor", "pst", emarks[n - 2][1])
                bg, bu = s.psum[2 * (j % 2)], s.psum[2 * (j % 2) + 1]
                for kc in range(NCH):
                    s.op("tensor", lambda e, kc=kc, j=j, b=bg, g0=g0: e.matmul(
                        out=b[:, 0:512], lhsT=g0[:, kc * 128:(kc + 1) * 128],
                        rhs=s.xn_ap[:, kc * T + j * 512:kc * T + (j + 1) * 512],
                        start=(kc == 0), stop=(kc == NCH - 1)))
                for kc in range(NCH):
                    s.op("tensor", lambda e, kc=kc, j=j, b=bu, u0=u0: e.matmul(
                        out=b[:, 0:512], lhsT=u0[:, kc * 128:(kc + 1) * 128],
                        rhs=s.xn_ap[:, kc * T + j * 512:kc * T + (j + 1) * 512],
                        start=(kc == 0), stop=(kc == NCH - 1)),
                        inc="stt" if kc == NCH - 1 else None)
                s.wait("scalar", "stt", s.val("stt"))
                s.wait("scalar", "sv", s.val("sv"))
                s.op("scalar", lambda e, oc=oc, b=bg: e.activation(
                    out=sil, in_=b[:, 0:512], func=AF.Silu,
                    scale=s.s1_sb[:, oc:oc + 1]), inc="pss")
                pssv = s.val("pss")
                mks = s.chain("scalar")
                s.wait("vector", mks[0], mks[1])
                s.op("vector", lambda e, oc=oc, b=bu: e.tensor_scalar(
                    out=us, in0=b[:, 0:512], scalar1=s.s2_sb[:, oc:oc + 1],
                    scalar2=None, op0=ALU.mult), inc="pst")
                pstv = s.val("pst")
                emarks.append((pssv, pstv))
                s.chain("vector")
                s.op("vector", lambda e, oc=oc, j=j: e.tensor_tensor(
                    out=hchunk(oc)[:, j * 512:(j + 1) * 512], in0=sil, in1=us,
                    op=ALU.mult), inc="sv")
                s.chain("vector")
            mk = s.chain("tensor")
            tmarks[oc] = mk
        h1d = s.chain("vector")
        for eng in ("tensor", "sync", "gpsimd", "scalar"):
            s.wait(eng, h1d[0], h1d[1])

        def hidden_stage(wT, scol, dst_dram, buf_base):
            hb0 = bfspan(buf_base, 4096)
            hb1_ = bfspan(buf_base + 8 * KB, 4096)
            stg_ = bfspan(buf_base + 16 * KB, 2048)
            tm = {}
            gm = []
            for oc in range(ICH):
                hb_ = hb0 if oc % 2 == 0 else hb1_
                psem = "ldgA" if oc % 2 == 0 else "ldgB"
                if oc >= 2:
                    s.wait("gpsimd", tm[oc - 2][0], tm[oc - 2][1])
                s.ldg(hb_, wT.rearrange("(c p) o -> p c o", p=128)[:, :, oc * 128:(oc + 1) * 128], sem=psem)
                s.wait("tensor", psem, s.val(psem))
                for j in range(NT):
                    n = oc * NT + j
                    if n >= 2:
                        s.wait("tensor", "pss", gm[n - 2])
                    bank = s.psum[j % 2]
                    for kc in range(ICH):
                        s.op("tensor", lambda e, kc=kc, j=j, b=bank, hb_=hb_: e.matmul(
                            out=b[:, 0:512], lhsT=hb_[:, kc * 128:(kc + 1) * 128],
                            rhs=hchunk(kc)[:, j * 512:(j + 1) * 512],
                            start=(kc == 0), stop=(kc == ICH - 1)),
                            inc="stt" if kc == ICH - 1 else None)
                    s.wait("scalar", "stt", s.val("stt"))
                    s.op("scalar", lambda e, oc=oc, j=j, b=bank, sc=scol, stg_=stg_: e.activation(
                        out=stg_[:, j * 512:(j + 1) * 512], in_=b[:, 0:512],
                        func=AF.Gelu, scale=sc[:, oc:oc + 1]), inc="pss")
                    gm.append(s.val("pss"))
                mks = s.chain("scalar")
                s.wait("sync", mks[0], mks[1])
                s.sth(dst_dram.rearrange("p (c t) -> p c t", c=ICH)[:, oc, :], stg_)
                s.wait("scalar", "st", s.val("st"))
                mk = s.chain("tensor")
                tm[oc] = mk
            mkx = s.chain("scalar")
            for eng in ("sync", "vector", "tensor", "gpsimd"):
                s.wait(eng, mkx[0], mkx[1])

        hidden_stage(s.h1T, s.sh1_sb, s.h2_s, 126 * KB)
        s.wait("sync", "st", s.val("st"))
        s.ldh(bfspan(R_X, 24 * T), s.h2_s[:, 0:24 * T])
        s.ldh(bfspan(158 * KB, 8 * T), s.h2_s[:, 24 * T:32 * T])
        lv = s.val("ldh")
        s.wait("tensor", "ldh", lv)
        s.wait("sync", "ldh", lv)
        hidden_stage(s.h2T, s.sh2_sb, s.h3_s, 126 * KB)

        xh = fspan(R_D, NCH * 1024)
        ah = fspan(R_D + 32 * KB, NCH * 1024)

        def h3c(kc):
            return bfspan(R_X + kc * 2 * KB, 1024)

        for half in range(2):
            hsl = slice(half * 1024, (half + 1) * 1024)
            s.wait("sync", "st", s.val("st"))
            s.ldh(xh.rearrange("p (c t) -> p c t", c=NCH),
                  s.xpark.rearrange("p (c t) -> p c t", c=NCH)[:, :, hsl])
            s.ldh(ah.rearrange("p (c t) -> p c t", c=NCH),
                  s.attn_s.rearrange("p (c t) -> p c t", c=NCH)[:, :, hsl])
            for kc in range(ICH):
                s.ldh(h3c(kc), s.h3_s[:, kc * T + half * 1024:kc * T + half * 1024 + 1024],
                      sem="lha")
            lha_v = s.val("lha")
            s.wait("tensor", "lha", lha_v)
            ldv = s.val("ldh")
            s.wait("vector", "ldh", ldv)
            s.wait("tensor", "ldh", ldv)
            mkt = s.chain("tensor")
            s.wait("gpsimd", mkt[0], mkt[1])
            wmks = {}
            smarks = []
            for oc in range(NCH):
                hb_ = bfspan(156 * KB, 4096) if oc % 2 == 0 else bfspan(164 * KB, 4096)
                psem = "ldgA" if oc % 2 == 0 else "ldgB"
                if oc >= 2:
                    s.wait("gpsimd", wmks[oc - 2][0], wmks[oc - 2][1])
                s.ldg(hb_, s.w3T.rearrange("(c p) o -> p c o", p=128)[:, :, oc * 128:(oc + 1) * 128], sem=psem)
                s.wait("tensor", psem, s.val(psem))
                for j in range(2):
                    n = oc * 2 + j
                    if n >= 2:
                        s.wait("tensor", "pst", smarks[n - 2])
                    bank = s.psum[j % 2]
                    for kc in range(ICH):
                        s.op("tensor", lambda e, kc=kc, j=j, b=bank, hb_=hb_: e.matmul(
                            out=b[:, 0:512], lhsT=hb_[:, kc * 128:(kc + 1) * 128],
                            rhs=h3c(kc)[:, j * 512:(j + 1) * 512],
                            start=(kc == 0), stop=(kc == ICH - 1)),
                            inc="stt" if kc == ICH - 1 else None)
                    s.wait("vector", "stt", s.val("stt"))
                    s.op("vector", lambda e, oc=oc, j=j, b=bank, xh=xh: e.scalar_tensor_tensor(
                        out=xh[:, oc * 1024 + j * 512:oc * 1024 + (j + 1) * 512],
                        in0=b[:, 0:512], scalar=s.s3_sb[:, oc:oc + 1],
                        in1=xh[:, oc * 1024 + j * 512:oc * 1024 + (j + 1) * 512],
                        op0=ALU.mult, op1=ALU.add), inc="pst")
                    smarks.append(s.val("pst"))
                    s.chain("vector")
                    s.op("vector", lambda e, oc=oc, j=j, xh=xh, ah=ah: e.tensor_tensor(
                        out=xh[:, oc * 1024 + j * 512:oc * 1024 + (j + 1) * 512],
                        in0=xh[:, oc * 1024 + j * 512:oc * 1024 + (j + 1) * 512],
                        in1=ah[:, oc * 1024 + j * 512:oc * 1024 + (j + 1) * 512],
                        op=ALU.add), inc="sv")
                    s.chain("vector")
                mk = s.chain("tensor")
                wmks[oc] = mk
            fin = s.chain("vector")
            s.wait("sync", fin[0], fin[1])
            s.sth(s.xpark.rearrange("p (c t) -> p c t", c=NCH)[:, :, hsl],
                  xh.rearrange("p (c t) -> p c t", c=NCH))
            stv = s.val("st")
            s.wait("vector", "st", stv)
            s.wait("tensor", "st", stv)
            s.wait("sync", "st", stv)
        s.phase_barrier()
        s.ldh(s.f32("x"), s.xpark)
        lv2 = s.val("ldh")
        for eng in ("vector", "scalar", "tensor", "gpsimd"):
            s.wait(eng, "ldh", lv2)
        s.x_parked = False
        s.pending_adds = []

    def p_store(self):
        s = self
        s.phase_barrier()
        s.alloc("ostg", T * NCH * 4, at=R_D)
        o_tm = s.f32("ostg")
        if s.x_parked:
            s.ldh(s.f32("x"), s.xpark)
            for da in s.pending_adds:
                s.ldh(o_tm, da)
                v = s.val("ldh")
                s.wait("vector", "ldh", v)
                for c in range(NCH):
                    s.op("vector", lambda e, c=c: e.tensor_tensor(
                        out=s.xc(c), in0=s.xc(c), in1=o_tm[:, c * T:(c + 1) * T],
                        op=ALU.add), inc="sv")
                s.chain("vector")
            s.x_parked = False
            s.pending_adds = []
            s.wait("tensor", "sv", s.val("sv"))
            s.wait("tensor", "ldh", s.val("ldh"))
        base = s.val("pst")
        for a in range(16):
            for c in range(NCH):
                src = s.xc(c)[:, a * 128:(a + 1) * 128]
                dst = o_tm[:, a * D + c * 128: a * D + (c + 1) * 128]
                s.transpose_128(src, dst, base, a * NCH + c)
        s.wait("sync", "pst", s.val("pst"))
        s.sth(s.out.rearrange("(a p) d -> p a d", p=128), o_tm)
        s.wait("sync", "st", s.val("st"))


def host_inputs(ins, core):
    f = np.float32
    col = lambda v, nch: np.ascontiguousarray(np.asarray(v).reshape(nch, 128).T).astype(f)
    tr = lambda v: np.ascontiguousarray(np.asarray(v).T).astype(f)
    m = {
        "x_in": np.ascontiguousarray(np.asarray(ins["L0_attn_x"][core])).astype(f),
        "Wq": np.asarray(ins["L0_attn_Wq"], f), "Wk": np.asarray(ins["L0_attn_Wk"], f),
        "Wv": np.asarray(ins["L0_attn_Wv"], f), "Wm": np.asarray(ins["L0_attn_Wm"], f),
        "bq": col(ins["L0_attn_bq"], NCH), "bk": col(ins["L0_attn_bk"], NCH),
        "bv": np.asarray(ins["L0_attn_bv"], f).reshape(1, D), "bm": col(ins["L0_attn_bm"], NCH),
        "wgT": tr(ins["L0_ffn_w_gate"]), "wuT": tr(ins["L0_ffn_w_up"]),
        "wdT": tr(ins["L0_ffn_w_down"]),
        "Wqkv": np.asarray(ins["L1_attn_Wqkv"], f), "Wout": np.asarray(ins["L1_attn_Wout"], f),
        "bout": col(ins["L1_attn_bout"], NCH),
        "w1T": tr(ins["L1_ffn_w1_w"]), "w2T": tr(ins["L1_ffn_w2_w"]),
        "h1T": tr(ins["L1_ffn_h1_w"]), "h2T": tr(ins["L1_ffn_h2_w"]),
        "w3T": tr(ins["L1_ffn_w3_w"]),
        "s1c": col(ins["L1_ffn_w1_s"], ICH), "s2c": col(ins["L1_ffn_w2_s"], ICH),
        "sh1c": col(ins["L1_ffn_h1_s"], ICH), "sh2c": col(ins["L1_ffn_h2_s"], ICH),
        "s3c": col(ins["L1_ffn_w3_s"], NCH),
        "ident": np.eye(128, dtype=f),
        "identb": np.eye(128, dtype=ml_dtypes.bfloat16),
        "triu": np.triu(np.ones((128, 128), np.float32)).astype(ml_dtypes.bfloat16),
        "sel8": _sel8(),
        "ident4": np.tile(np.eye(128, dtype=f), (1, 4)),
    }
    return m


def np_softmax(x, axis=-1):
    m = np.max(x, axis=axis, keepdims=True)
    e = np.exp(x - m)
    return e / e.sum(axis=axis, keepdims=True)


def np_silu(x):
    return x / (1 + np.exp(-x))


def np_gelu_exact(x):
    from scipy.special import erf
    return 0.5 * x * (1 + erf(x / np.sqrt(2)))


def np_layernorm(x, eps=1e-5):
    m = x.mean(-1, keepdims=True)
    v = x.var(-1, keepdims=True)
    return (x - m) / np.sqrt(v + eps)


def ref_l0_attn(x, Wq, bq, Wk, bk, Wv, bv, Wm, bm):
    S = x.shape[0]
    def heads(w, b):
        return (x @ w + b).reshape(S, H, HS).transpose(1, 0, 2)
    q, k, v = heads(Wq, bq), heads(Wk, bk), heads(Wv, bv)
    sc = np.einsum("hqd,hkd->hqk", q, k) / np.sqrt(HS)
    mask = np.arange(S)[None, :] > np.arange(S)[:, None]
    sc = np.where(mask[None], -1e12, sc)
    p = np_softmax(sc, -1)
    a = np.einsum("hqk,hkd->hqd", p, v)
    a = a.transpose(1, 0, 2).reshape(S, D)
    return a @ Wm + bm


def ref_l0_ffn(x, w_gate, w_up, w_down):
    def wq_(w):
        sc = 1.0 / np.clip(np.mean(np.abs(w)), 1e-5, None)
        return np.clip(np.round(w * sc), -1, 1) / sc
    def aq_(t):
        sc = 127.0 / np.clip(np.max(np.abs(t), -1, keepdims=True), 1e-5, None)
        return np.clip(np.round(t * sc), -128, 127) / sc
    bl = lambda t, w: aq_(t) @ wq_(w).T
    g = np_silu(bl(x, w_gate))
    u = bl(x, w_up)
    return bl(g * u, w_down)


def ref_l1_attn(x, Wqkv, Wout, bout):
    S = x.shape[0]
    qkv = x @ Wqkv
    q, k, v = np.split(qkv, 3, -1)
    rs = lambda t: t.reshape(S, H, HS).transpose(1, 0, 2)
    q, k, v = rs(q), rs(k), rs(v)
    el = lambda t: np.where(t > 0, t + 1.0, np.exp(t))
    q, k = el(q), el(k)
    kv = np.einsum("hnd,hnf->hdf", k, v)
    ks = k.sum(1)
    qo = np.einsum("hnd,hdf->hnf", q, kv)
    qks = np.einsum("hnd,hd->hn", q, ks)
    o = qo / (qks[..., None] + 1e-6)
    o = o.transpose(1, 0, 2).reshape(S, D)
    return o @ Wout + bout


def ref_l1_ffn(x, ws):
    tl = lambda t, w, sc: (t @ w.T) * sc
    h = np_silu(tl(x, ws["w1_w"], ws["w1_s"])) * tl(x, ws["w2_w"], ws["w2_s"])
    h = np_gelu_exact(tl(h, ws["h1_w"], ws["h1_s"]))
    h = np_gelu_exact(tl(h, ws["h2_w"], ws["h2_s"]))
    return tl(h, ws["w3_w"], ws["w3_s"])


def _sel8():
    m = np.zeros((16, 1024), np.float32)
    for hp in range(8):
        for p in range(128):
            h = 2 * hp + (1 if p >= 64 else 0)
            m[h, 128 * hp + p] = 1.0
    return m


def kernel(**inputs):
    mk = MK(7, dump=None)
    nc = mk.build()
    shared = host_shared(inputs)
    in_maps = []
    for core in range(8):
        m = dict(shared)
        m["x_in"] = np.ascontiguousarray(np.asarray(inputs["L0_attn_x"][core])).astype(np.float32)
        in_maps.append(m)
    from concourse.bass_utils import run_bass_kernel_spmd
    res = run_bass_kernel_spmd(nc, in_maps, core_ids=list(range(8)))
    out = np.stack([np.asarray(r["out"], np.float32) for r in res.results], axis=0)
    return out


def host_shared(ins):
    f = np.float32
    col = lambda v, nch: np.ascontiguousarray(np.asarray(v).reshape(nch, 128).T).astype(f)
    tr = lambda v: np.ascontiguousarray(np.asarray(v).T).astype(f)
    return {
        "Wq": np.asarray(ins["L0_attn_Wq"], f), "Wk": np.asarray(ins["L0_attn_Wk"], f),
        "Wv": np.asarray(ins["L0_attn_Wv"], f), "Wm": np.asarray(ins["L0_attn_Wm"], f),
        "bq": col(ins["L0_attn_bq"], NCH), "bk": col(ins["L0_attn_bk"], NCH),
        "bv": np.asarray(ins["L0_attn_bv"], f).reshape(1, D), "bm": col(ins["L0_attn_bm"], NCH),
        "wgT": tr(ins["L0_ffn_w_gate"]), "wuT": tr(ins["L0_ffn_w_up"]),
        "wdT": tr(ins["L0_ffn_w_down"]),
        "Wqkv": np.asarray(ins["L1_attn_Wqkv"], f), "Wout": np.asarray(ins["L1_attn_Wout"], f),
        "bout": col(ins["L1_attn_bout"], NCH),
        "w1T": tr(ins["L1_ffn_w1_w"]), "w2T": tr(ins["L1_ffn_w2_w"]),
        "h1T": tr(ins["L1_ffn_h1_w"]), "h2T": tr(ins["L1_ffn_h2_w"]),
        "w3T": tr(ins["L1_ffn_w3_w"]),
        "s1c": col(ins["L1_ffn_w1_s"], ICH), "s2c": col(ins["L1_ffn_w2_s"], ICH),
        "sh1c": col(ins["L1_ffn_h1_s"], ICH), "sh2c": col(ins["L1_ffn_h2_s"], ICH),
        "s3c": col(ins["L1_ffn_w3_s"], NCH),
        "ident": np.eye(128, dtype=f),
        "identb": np.eye(128, dtype=ml_dtypes.bfloat16),
        "triu": np.triu(np.ones((128, 128), np.float32)).astype(ml_dtypes.bfloat16),
        "sel8": _sel8(),
        "ident4": np.tile(np.eye(128, dtype=f), (1, 4)),
    }


# version 14 (final)
# speedup vs baseline: 1.0881x; 1.0411x over previous
import sys
import numpy as np
import ml_dtypes
import concourse.bass as bass
import concourse.mybir as mybir
from contextlib import ExitStack

F32 = mybir.dt.float32
BF16 = mybir.dt.bfloat16
I32 = mybir.dt.int32
ALU = mybir.AluOpType
AF = mybir.ActivationFunctionType
AX = mybir.AxisListType

T = 2048
D = 1024
NCH = 8
H = 16
HS = 64
INNER = 4096
ICH = 32
NT = 4
LNEPS = 1e-5

ENGS = ("sync", "vector", "scalar", "tensor", "gpsimd")

KB = 1024
R_CONSTS = 0 * KB
R_ROWS = 4 * KB
R_X = 28 * KB
R_D = 92 * KB
R_E = 156 * KB
POOL_END = 204 * KB


class MK:
    def __init__(self, n_phases=99, dump=None):
        self.n_phases = n_phases
        self.dump = dump
        self.nc = bass.Bass("TRN2", target_bir_lowering=False)
        self.ops = {e: [] for e in ENGS}
        self.cnt = {}
        self.sems = {}
        self.x_parked = False
        self.pending_adds = []
        self.prepass_gen = None
        self.bg_pst_mark = None
        self.qkv_budget = 64

    def op(self, eng, fn, inc=None, inc_by=1):
        ent = {"fn": fn, "inc": None}
        if inc is not None:
            self.cnt[inc] += inc_by
            ent["inc"] = (inc, inc_by, self.cnt[inc])
        self.ops[eng].append(ent)
        return self.cnt.get(inc)

    def wait(self, eng, sem_name, val=None):
        v = self.cnt[sem_name] if val is None else val
        if v > 0:
            self.ops[eng].append({"wait": (sem_name, v)})
        return v

    def val(self, sem_name):
        return self.cnt[sem_name]

    CHAIN = {"vector": "vch", "scalar": "sch", "tensor": "tch", "gpsimd": "gch"}

    def phase_barrier(self, engs=("sync", "vector", "scalar", "tensor", "gpsimd")):
        for eng in ("vector", "scalar", "tensor", "gpsimd"):
            if self.ops[eng] and any("fn" in x for x in self.ops[eng]):
                self.chain(eng)
        snap = {nm: self.cnt[nm] for nm in
                ("ldh", "ldg", "ldgA", "ldgB", "ldgC", "st", "sv", "ss", "stt",
                 "sg", "pst", "pss", "psa", "lha", "lhb", "vch", "sch", "tch", "gch")}
        for eng in engs:
            for nm, v in snap.items():
                self.wait(eng, nm, v)

    def chain(self, eng):
        for ent in reversed(self.ops[eng]):
            if "fn" in ent:
                last = ent
                break
        else:
            return
        if last["inc"] is None:
            nm = self.CHAIN[eng]
            self.cnt[nm] += 1
            last["inc"] = (nm, 1, self.cnt[nm])
        nm, _, v = last["inc"]
        self.ops[eng].append({"wait": (nm, v)})
        return (nm, v)

    def ldh(self, out, in_, sem="ldh"):
        return self.op("sync", lambda e, o=out, i=in_: e.dma_start(out=o, in_=i),
                       inc=sem, inc_by=16)

    def ldg(self, out, in_, sem="ldgA"):
        self.wait("gpsimd", sem, self.cnt[sem])
        return self.op("gpsimd", lambda e, o=out, i=in_: e.dma_start(out=o, in_=i),
                       inc=sem, inc_by=16)

    def sth(self, out, in_):
        self.wait("sync", "st", self.cnt["st"])
        return self.op("sync", lambda e, o=out, i=in_: e.dma_start(out=o, in_=i),
                       inc="st", inc_by=16)

    def alloc(self, name, nbytes, at=None, align=64):
        off = at if at is not None else (self.cursor + align - 1) // align * align
        assert off + nbytes <= self.POOLB, f"pool overflow: {name} @{off}+{nbytes}"
        self.spans[name] = (off, nbytes)
        self.cursor = max(self.cursor, off + nbytes) if at is None else self.cursor
        self.watermark = max(self.watermark, off + nbytes)
        return off

    def f32(self, name, cols=None):
        off, size = self.spans[name]
        n = size // 4 if cols is None else cols
        return self.pool.ap()[:, off // 4: off // 4 + n]

    def bf16(self, name, cols=None):
        off, size = self.spans[name]
        n = size // 2 if cols is None else cols
        return self.pool.ap().bitcast(BF16)[:, off // 2: off // 2 + n]

    def build(self):
        nc = self.nc
        dram = lambda nm, sh, dt, kind="ExternalInput": nc.dram_tensor(nm, sh, dt, kind=kind).ap()
        s = self
        s.x_in = dram("x_in", [T, D], F32)
        s.Wq = dram("Wq", [D, D], F32); s.Wk = dram("Wk", [D, D], F32)
        s.Wv = dram("Wv", [D, D], F32); s.Wm = dram("Wm", [D, D], F32)
        s.bq = dram("bq", [128, NCH], F32); s.bk = dram("bk", [128, NCH], F32)
        s.bv = dram("bv", [1, D], F32); s.bm = dram("bm", [128, NCH], F32)
        s.wgT = dram("wgT", [D, INNER], F32); s.wuT = dram("wuT", [D, INNER], F32)
        s.wdT = dram("wdT", [INNER, D], F32)
        s.Wqkv = dram("Wqkv", [D, 3 * D], F32)
        s.Wout = dram("Wout", [D, D], F32); s.bout = dram("bout", [128, NCH], F32)
        s.w1T = dram("w1T", [D, INNER], F32); s.w2T = dram("w2T", [D, INNER], F32)
        s.h1T = dram("h1T", [INNER, INNER], F32); s.h2T = dram("h2T", [INNER, INNER], F32)
        s.w3T = dram("w3T", [INNER, D], F32)
        s.s1c = dram("s1c", [128, ICH], F32); s.s2c = dram("s2c", [128, ICH], F32)
        s.sh1c = dram("sh1c", [128, ICH], F32); s.sh2c = dram("sh2c", [128, ICH], F32)
        s.s3c = dram("s3c", [128, NCH], F32)
        s.identd = dram("ident", [128, 128], F32)
        s.identbd = dram("identb", [128, 128], BF16)
        s.triud = dram("triu", [128, 128], BF16)
        s.sel8d = dram("sel8", [16, 1024], F32)
        s.ident4d = dram("ident4", [128, 512], F32)
        s.out = dram("out", [T, D], F32, "ExternalOutput")
        s.dbg = dram("dbg", [128, 17408], BF16, "ExternalOutput") if s.dump else None
        s.iwg_s = dram("iwg_s", [D, INNER], BF16, "Internal")
        s.iwu_s = dram("iwu_s", [D, INNER], BF16, "Internal")
        s.iwd_s = dram("iwd_s", [INNER, D], BF16, "Internal")
        s.xpark = dram("xpark", [128, NCH * T], F32, "Internal")
        s.attn_s = dram("attn_s", [128, NCH * T], F32, "Internal")
        s.h2_s = dram("h2_s", [128, ICH * T], BF16, "Internal")
        s.h3_s = dram("h3_s", [128, ICH * T], BF16, "Internal")

        ctx = ExitStack()
        with ctx:
            self.POOLB = 204 * 1024
            self.pool = ctx.enter_context(nc.sbuf_tensor("pool", [128, self.POOLB // 4], F32))
            self.spans, self.cursor, self.watermark = {}, 0, 0
            self.psumbig = ctx.enter_context(nc.psum_tensor("pb", [128, 4096], F32))
            self.psum = [self.psumbig.ap()[:, i * 512:(i + 1) * 512]
                         for i in range(8)]
            for nm in ("ldh", "ldg", "ldgA", "ldgB", "ldgC", "st", "sv", "ss",
                       "stt", "sg", "pst", "pss", "psa", "wb", "lha", "lhb",
                       "vch", "sch", "tch", "gch"):
                self.sems[nm] = ctx.enter_context(nc.semaphore(nm))
                self.cnt[nm] = 0
            blk = ctx.enter_context(nc.Block())

            self.alloc("consts", 4096, at=R_CONSTS)
            self.alloc("rows", 24576, at=R_ROWS)
            self.alloc("x", T * NCH * 4, at=R_X)

            self.setup_consts()
            self.p0_load_x()
            if self.n_phases >= 1:
                self.p_ln(0)
            if self.n_phases >= 2:
                if self.n_phases >= 4:
                    self.prepass_gen = self.prepass_generator()
                self.p_l0_qkv()
            if self.n_phases >= 3:
                self.p_l0_attn()
            if self.n_phases >= 4:
                self.p_l0_ffn()
            if self.n_phases >= 5:
                self.p_ln(1)
            if self.n_phases >= 6:
                self.p_l1_attn()
            if self.n_phases >= 7:
                self.p_l1_ffn()
            self.p_store()

            for eng in ENGS:
                def make(eng):
                    def body(e):
                        for ent in self.ops[eng]:
                            if "wait" in ent:
                                nm, v = ent["wait"]
                                e.wait_ge(self.sems[nm], v)
                            else:
                                i = ent["fn"](e)
                                if ent["inc"] is not None:
                                    nm, n, _ = ent["inc"]
                                    i.then_inc(self.sems[nm], n)
                    return body
                getattr(blk, eng)(make(eng))
        return nc

    def xc(self, c):
        return self.f32("x")[:, c * T:(c + 1) * T]

    def setup_consts(self):
        s = self
        co = s.spans["consts"][0]
        p, pb = s.pool.ap(), s.pool.ap().bitcast(BF16)
        s.ident = p[:, co // 4: co // 4 + 128]
        s.identb = pb[:, (co + 512) // 2:][:, :128]
        s.triu = pb[:, (co + 768) // 2:][:, :128]
        s.ones_col = p[:, (co + 1024) // 4:][:, :1]
        s.ones_row = p[0:1, (co + 1028) // 4:][:, :128]
        s.onesb_col = pb[:, (co + 1540) // 2:][:, :1]
        s.ones128 = p[:, (co + 2048) // 4:][:, :128]
        s.halfcol = p[:, (co + 2616) // 4:][:, :1]
        ro = s.spans["rows"][0]

        def colf32(n):
            nonlocal ro
            ap = p[:, ro // 4: ro // 4 + n]
            ro += n * 4
            return ap
        s.bq_sb = colf32(NCH); s.bk_sb = colf32(NCH); s.bm_sb = colf32(NCH)
        s.bout_sb = colf32(NCH)
        s.s1_sb = colf32(ICH); s.s2_sb = colf32(ICH)
        s.sh1_sb = colf32(ICH); s.sh2_sb = colf32(ICH); s.s3_sb = colf32(NCH)
        s.bqs_sb = colf32(NCH)
        s.bv_row = p[0:1, ro // 4: ro // 4 + D]; ro += D * 4
        s.mrow_off = ro
        s.mrow = p[0:1, ro // 4: ro // 4 + T]; ro += T * 4
        s.rrow = p[0:1, ro // 4: ro // 4 + T]; ro += T * 4
        assert ro <= s.spans["rows"][0] + 24576

        s.ldh(s.ident, s.identd)
        s.ldh(s.identb, s.identbd)
        s.ldh(s.triu, s.triud)
        for dst, src in ((s.bq_sb, s.bq), (s.bk_sb, s.bk), (s.bm_sb, s.bm),
                         (s.bout_sb, s.bout), (s.s1_sb, s.s1c), (s.s2_sb, s.s2c),
                         (s.sh1_sb, s.sh1c), (s.sh2_sb, s.sh2c), (s.s3_sb, s.s3c),
                         (s.bv_row, s.bv)):
            s.ldh(dst, src)

    def transpose_128(self, src_f32, dst, kidx_sem_base, k, evac_eng="vector", bf=False):
        s = self
        bank = s.psum[k % 2]
        s.wait("tensor", "pst", kidx_sem_base + k - 1)
        ident = s.identb if bf else s.ident
        s.op("tensor", lambda e, b=bank, sr=src_f32, idn=ident:
             e.transpose(out=b[:, 0:128], in_=sr, identity=idn), inc="stt")
        s.wait(evac_eng, "stt", s.val("stt"))
        s.op(evac_eng, lambda e, b=bank, d=dst: e.tensor_copy(out=d, in_=b[:, 0:128]),
             inc="pst")

    def p0_load_x(self):
        s = self
        s.alloc("stg", T * NCH * 4, at=R_D)
        x_tm = s.f32("stg")
        pre = s.val("ldh")
        xsrc = s.x_in.rearrange("(a p) d -> p a d", p=128)
        wave_marks = []
        for wv in range(4):
            s.wait("sync", "ldh", s.val("ldh"))
            s.ldh(x_tm[:, wv * 4 * D:(wv + 1) * 4 * D], xsrc[:, wv * 4:(wv + 1) * 4, :])
            wave_marks.append(s.val("ldh"))
        s.wait("vector", "ldh", pre)
        s.op("vector", lambda e: e.memset(s.ones_col, 1.0))
        s.op("vector", lambda e: e.memset(s.ones_row, 1.0))
        s.op("vector", lambda e: e.memset(s.onesb_col, 1.0))
        s.op("vector", lambda e: e.memset(s.ones128, 1.0))
        s.op("vector", lambda e: e.memset(s.halfcol, 0.5))
        s.op("vector", lambda e: e.tensor_scalar(out=s.bqs_sb, in0=s.bq_sb,
             scalar1=0.125, scalar2=None, op0=ALU.mult), inc="sv")
        base = s.val("pst")
        for a in range(16):
            if a % 4 == 0:
                s.wait("tensor", "ldh", wave_marks[a // 4])
            for c in range(NCH):
                src = x_tm[:, a * D + c * 128: a * D + (c + 1) * 128]
                dst = s.xc(c)[:, a * 128:(a + 1) * 128]
                s.transpose_128(src, dst, base, a * NCH + c)
        self.x_loaded_pst = s.val("pst")

    def stats_mb_rb(self, mb_ap, rb_ap, sq0, sq1):
        s = self
        sq = [sq0, sq1]
        chunk_mark = []
        for c in range(NCH):
            buf = sq[c % 2]
            if c >= 2:
                s.wait("scalar", "stt", chunk_mark[c - 2])
            s.op("scalar", lambda e, c=c, b=buf: e.activation(out=b, in_=s.xc(c), func=AF.Square), inc="ss")
            s.wait("tensor", "ss", s.val("ss"))
            for j in range(NT):
                s.op("tensor", lambda e, c=c, j=j: e.matmul(
                    out=s.psum[j][0:1, 0:512],
                    lhsT=s.ones_col, rhs=s.xc(c)[:, j * 512:(j + 1) * 512],
                    start=(c == 0), stop=(c == NCH - 1)))
            for j in range(NT):
                s.op("tensor", lambda e, c=c, j=j, b=buf: e.matmul(
                    out=s.psum[4 + j][0:1, 0:512],
                    lhsT=s.ones_col, rhs=b[:, j * 512:(j + 1) * 512],
                    start=(c == 0), stop=(c == NCH - 1)),
                    inc="stt" if j == NT - 1 else None)
            chunk_mark.append(s.val("stt"))
        tstats = s.val("stt")
        s.wait("vector", "stt", tstats)
        for j in range(NT):
            s.op("vector", lambda e, j=j: e.tensor_scalar(
                out=s.mrow[:, j * 512:(j + 1) * 512], in0=s.psum[j][0:1, 0:512],
                scalar1=1.0 / D, scalar2=None, op0=ALU.mult))
            s.op("vector", lambda e, j=j: e.tensor_scalar(
                out=s.rrow[:, j * 512:(j + 1) * 512], in0=s.psum[4 + j][0:1, 0:512],
                scalar1=1.0 / D, scalar2=None, op0=ALU.mult))
        s.chain("vector")
        s.op("vector", lambda e: e.tensor_tensor(
            out=sq[0][0:1, 0:T], in0=s.mrow, in1=s.mrow, op=ALU.mult))
        s.chain("vector")
        s.op("vector", lambda e: e.tensor_tensor(
            out=s.rrow, in0=s.rrow, in1=sq[0][0:1, 0:T], op=ALU.subtract), inc="sv")
        s.chain("vector")
        s.op("vector", lambda e: e.tensor_scalar(out=s.rrow, in0=s.rrow,
             scalar1=float(LNEPS), scalar2=None, op0=ALU.add), inc="sv")
        s.wait("scalar", "sv", s.val("sv"))
        s.op("scalar", lambda e: e.activation(out=s.rrow, in_=s.rrow, func=AF.Sqrt), inc="ss")
        s.wait("vector", "ss", s.val("ss"))
        s.op("vector", lambda e: e.reciprocal(out=s.rrow, in_=s.rrow), inc="sv")
        svr = s.val("sv")
        s.wait("tensor", "sv", svr)
        for j in range(NT):
            bm_, br_ = s.psum[j % 2], s.psum[2 + j % 2]
            s.op("tensor", lambda e, j=j, b=bm_: e.matmul(
                out=b[:, 0:512], lhsT=s.ones_row,
                rhs=s.mrow[:, j * 512:(j + 1) * 512], start=True, stop=True))
            s.op("tensor", lambda e, j=j, b=br_: e.matmul(
                out=b[:, 0:512], lhsT=s.ones_row,
                rhs=s.rrow[:, j * 512:(j + 1) * 512], start=True, stop=True), inc="stt")
            s.wait("vector", "stt", s.val("stt"))
            s.op("vector", lambda e, j=j, b=bm_: e.tensor_copy(
                out=mb_ap[:, j * 512:(j + 1) * 512], in_=b[:, 0:512]))
            s.op("vector", lambda e, j=j, b=br_: e.tensor_copy(
                out=rb_ap[:, j * 512:(j + 1) * 512], in_=b[:, 0:512]), inc="pst")
            s.wait("tensor", "pst", s.val("pst"))

    def round_pair(self, lanes):
        s = self
        tm = s.chain("vector")
        s.wait("scalar", tm[0], tm[1])
        smarks = {}
        for li, (t, v, d, o) in enumerate(lanes):
            ii = v.bitcast(I32)
            s.op("scalar", lambda e, t=t, ii=ii: e.activation(
                out=ii, in_=t, func=AF.Copy), inc="ss")
            smarks[li] = s.val("ss")
        for li, (t, v, d, o) in enumerate(lanes):
            ii = v.bitcast(I32)
            s.wait("vector", "ss", smarks[li])
            s.op("vector", lambda e, t=t, ii=ii: e.tensor_tensor(
                out=t, in0=t, in1=ii, op=ALU.subtract))
            s.chain("vector")
            s.op("vector", lambda e, t=t, d=d, ii=ii: e.scalar_tensor_tensor(
                out=d, in0=t, scalar=-0.5, in1=ii, op0=ALU.is_le,
                op1=ALU.subtract))
            s.chain("vector")
            s.op("vector", lambda e, t=t, d=d, o=o: e.scalar_tensor_tensor(
                out=o, in0=t, scalar=0.5, in1=d, op0=ALU.is_ge,
                op1=ALU.subtract), inc="sv")
        s.chain("vector")

    def p_ln(self, layer):
        s = self
        s.phase_barrier()
        s.alloc("mb", T * 4, at=R_D)
        s.alloc("rb", T * 4, at=R_D + 8 * KB)
        s.alloc("sq0", T * 4, at=R_D + 16 * KB)
        s.alloc("sq1", T * 4, at=R_D + 24 * KB)
        s.alloc("xn", T * NCH * 2, at=R_D + 34 * KB)
        s.xn_ap, s.mb_ap, s.rb_ap = s.bf16("xn"), s.f32("mb"), s.f32("rb")
        sq = [s.f32("sq0"), s.f32("sq1")]


        chunk_mark = []
        for c in range(NCH):
            buf = sq[c % 2]
            if c >= 2:
                s.wait("scalar", "stt", chunk_mark[c - 2])
            s.op("scalar", lambda e, c=c, b=buf: e.activation(out=b, in_=s.xc(c), func=AF.Square), inc="ss")
            s.wait("tensor", "ss", s.val("ss"))
            for j in range(NT):
                s.op("tensor", lambda e, c=c, j=j: e.matmul(
                    out=s.psum[j][0:1, 0:512],
                    lhsT=s.ones_col, rhs=s.xc(c)[:, j * 512:(j + 1) * 512],
                    start=(c == 0), stop=(c == NCH - 1)))
            for j in range(NT):
                s.op("tensor", lambda e, c=c, j=j, b=buf: e.matmul(
                    out=s.psum[4 + j][0:1, 0:512],
                    lhsT=s.ones_col, rhs=b[:, j * 512:(j + 1) * 512],
                    start=(c == 0), stop=(c == NCH - 1)),
                    inc="stt" if j == NT - 1 else None)
            chunk_mark.append(s.val("stt"))
        tstats = s.val("stt")

        s.wait("vector", "stt", tstats)
        for j in range(NT):
            s.op("vector", lambda e, j=j: e.tensor_scalar(
                out=s.mrow[:, j * 512:(j + 1) * 512], in0=s.psum[j][0:1, 0:512],
                scalar1=1.0 / D, scalar2=None, op0=ALU.mult))
            s.op("vector", lambda e, j=j: e.tensor_scalar(
                out=s.rrow[:, j * 512:(j + 1) * 512], in0=s.psum[4 + j][0:1, 0:512],
                scalar1=1.0 / D, scalar2=None, op0=ALU.mult))
        s.chain("vector")
        s.op("vector", lambda e: e.tensor_tensor(
            out=sq[0][0:1, 0:T], in0=s.mrow, in1=s.mrow, op=ALU.mult))
        s.chain("vector")
        s.op("vector", lambda e: e.tensor_tensor(
            out=s.rrow, in0=s.rrow, in1=sq[0][0:1, 0:T], op=ALU.subtract), inc="sv")
        s.chain("vector")
        s.op("vector", lambda e: e.tensor_scalar(out=s.rrow, in0=s.rrow,
             scalar1=float(LNEPS), scalar2=None, op0=ALU.add), inc="sv")
        s.wait("scalar", "sv", s.val("sv"))
        s.op("scalar", lambda e: e.activation(out=s.rrow, in_=s.rrow, func=AF.Sqrt), inc="ss")
        s.wait("vector", "ss", s.val("ss"))
        s.op("vector", lambda e: e.reciprocal(out=s.rrow, in_=s.rrow), inc="sv")
        svr = s.val("sv")
        s.wait("tensor", "sv", svr)
        for j in range(NT):
            bm_, br_ = s.psum[j % 2], s.psum[2 + j % 2]
            s.op("tensor", lambda e, j=j, b=bm_: e.matmul(
                out=b[:, 0:512], lhsT=s.ones_row,
                rhs=s.mrow[:, j * 512:(j + 1) * 512], start=True, stop=True))
            s.op("tensor", lambda e, j=j, b=br_: e.matmul(
                out=b[:, 0:512], lhsT=s.ones_row,
                rhs=s.rrow[:, j * 512:(j + 1) * 512], start=True, stop=True), inc="stt")
            s.wait("vector", "stt", s.val("stt"))
            s.op("vector", lambda e, j=j, b=bm_: e.tensor_copy(
                out=s.mb_ap[:, j * 512:(j + 1) * 512], in_=b[:, 0:512]))
            s.op("vector", lambda e, j=j, b=br_: e.tensor_copy(
                out=s.rb_ap[:, j * 512:(j + 1) * 512], in_=b[:, 0:512]), inc="pst")
            if j >= 1:
                s.wait("tensor", "pst", s.val("pst") - 1)
        s.chain("vector")
        for c in range(NCH):
            buf = sq[c % 2]
            s.op("vector", lambda e, c=c, b=buf: e.tensor_tensor(
                out=b, in0=s.xc(c), in1=s.mb_ap, op=ALU.subtract))
            s.chain("vector")
            s.op("vector", lambda e, c=c, b=buf: e.tensor_tensor(
                out=s.xn_ap[:, c * T:(c + 1) * T], in0=b, in1=s.rb_ap,
                op=ALU.mult), inc="sv")
            s.chain("vector")
        s.ln_done_sv = s.val("sv")
        if s.dump == f"xn{layer}":
            s.wait("sync", "sv", s.ln_done_sv)
            s.sth(s.dbg, s.xn_ap)

    def p_l0_qkv(self):
        s = self
        s.phase_barrier()
        s.sth(s.xpark, s.f32("x"))
        s.x_parked = True
        park_v = s.val("st")
        s.wait("scalar", "st", park_v)
        s.wait("vector", "st", park_v)
        s.alloc("q", T * NCH * 2, at=R_X)
        s.alloc("k", T * NCH * 2, at=R_X + 32 * KB)
        s.alloc("v", 16 * 1040 * 2, at=R_D)
        s.alloc("wbufA", D * NCH * 2, at=R_E + 2 * KB)
        s.alloc("wbufB", D * NCH * 2, at=R_E + 18 * KB)
        s.alloc("bvb", D * 4, at=R_E + 34 * KB)
        q, k, v = s.bf16("q"), s.bf16("k"), s.bf16("v")
        s.q_ap, s.k_ap, s.v_ap = q, k, v
        wA, wB = s.bf16("wbufA"), s.bf16("wbufB")
        bvb = s.f32("bvb")

        s.ldg(wA, s.Wq.rearrange("(c p) o -> p c o", p=128), sem="ldgA")
        ldq = ("ldgA", s.val("ldgA"))
        s.ldg(wB, s.Wk.rearrange("(c p) o -> p c o", p=128), sem="ldgB")
        ldk = ("ldgB", s.val("ldgB"))

        for half in range(2):
            b = s.psum[6 + half]
            s.op("tensor", lambda e, h=half, b=b: e.matmul(
                out=b[:, 0:512], lhsT=s.ones_row, rhs=s.bv_row[:, h * 512:(h + 1) * 512],
                start=True, stop=True), inc="stt")
        s.wait("vector", "stt", s.val("stt"))
        for half in range(2):
            s.op("vector", lambda e, h=half: e.tensor_copy(
                out=bvb[:, h * 512:(h + 1) * 512], in_=s.psum[6 + h][:, 0:512]), inc="pst")
        s.op("vector", lambda e: e.memset(v[:, 0:16 * 1040], 1.0))
        s.chain("vector")
        bvb_pst = s.val("pst")

        def proj_fm(dst, wbuf, bias_col, scale, ld_mark, eng):
            s.wait("tensor", ld_mark[0], ld_mark[1])
            pvbase = s.val("pss")
            kidx = 0
            for oc in range(NCH):
                for j in range(NT):
                    bank = s.psum[kidx % 2]
                    s.wait("tensor", "pss", pvbase + kidx - 1)
                    for kc in range(NCH):
                        s.op("tensor", lambda e, oc=oc, j=j, kc=kc, b=bank, w=wbuf: e.matmul(
                            out=b[:, 0:512],
                            lhsT=w[:, kc * D + oc * 128:kc * D + (oc + 1) * 128],
                            rhs=s.xn_ap[:, kc * T + j * 512:kc * T + (j + 1) * 512],
                            start=(kc == 0), stop=(kc == NCH - 1)),
                            inc="stt" if kc == NCH - 1 else None)
                    s.wait(eng, "stt", s.val("stt"))
                    if eng == "scalar":
                        s.op("scalar", lambda e, oc=oc, j=j, b=bank, bc=bias_col, sc=scale: e.activation(
                            out=dst[:, oc * T + j * 512:oc * T + (j + 1) * 512],
                            in_=b[:, 0:512], func=AF.Identity,
                            bias=bc[:, oc:oc + 1], scale=sc), inc="pss")
                    for _ in range(2):
                        if s.prepass_gen is None or s.qkv_budget <= 0:
                            break
                        s.qkv_budget -= 1
                        if next(s.prepass_gen, "END") == "END":
                            s.prepass_gen = None
                    kidx += 1
            return s.val("pss")

        proj_fm(q, wA, s.bqs_sb, 0.125, ldq, "scalar")
        mark = s.chain("tensor")
        s.wait("gpsimd", mark[0], mark[1])
        s.ldg(wA, s.Wv.rearrange("(c p) o -> p c o", p=128), sem="ldgA")
        ldv = ("ldgA", s.val("ldgA"))
        proj_fm(k, wB, s.bk_sb, 1.0, ldk, "scalar")

        s.wait("tensor", ldv[0], ldv[1])
        s.wait("vector", "pst", bvb_pst)
        pvbase = s.val("pst")
        kidx = 0
        for a in range(16):
            for half in range(2):
                bank = s.psum[2 + kidx % 2]
                s.wait("tensor", "pst", pvbase + kidx - 1)
                for kc in range(NCH):
                    s.op("tensor", lambda e, a=a, hf=half, kc=kc, b=bank: e.matmul(
                        out=b[:, 0:512],
                        lhsT=s.xn_ap[:, kc * T + a * 128:kc * T + (a + 1) * 128],
                        rhs=wA[:, kc * D + hf * 512:kc * D + (hf + 1) * 512],
                        start=(kc == 0), stop=(kc == NCH - 1)),
                        inc="stt" if kc == NCH - 1 else None)
                s.wait("vector", "stt", s.val("stt"))
                for hh in range(8):
                    head = half * 8 + hh
                    s.op("vector", lambda e, a=a, hf=half, hh=hh, head=head, b=bank: e.tensor_tensor(
                        out=v[:, a * 1040 + 65 * head: a * 1040 + 65 * head + 64],
                        in0=b[:, hh * 64:(hh + 1) * 64],
                        in1=bvb[:, hf * 512 + hh * 64: hf * 512 + (hh + 1) * 64],
                        op=ALU.add), inc="pst" if hh == 7 else None)
                kidx += 1
        s.qkv_done = s.val("pst")
        if s.dump in ("q0", "k0", "v0"):
            tgt = {"q0": q, "k0": k, "v0": v}[s.dump]
            s.wait("sync", "pst", s.val("pst"))
            s.wait("sync", "pss", s.val("pss"))
            s.sth(s.dbg.rearrange("p n -> p n")[:, 0:tgt.shape[1]], tgt)


    def p_l0_attn(self):
        s = self
        s.phase_barrier()
        s.alloc("wm", D * NCH * 2, at=R_D + 33 * KB)
        s.alloc("probsA", T * 2, at=R_D + 49 * KB)
        s.alloc("probsB", T * 2, at=R_D + 56 * KB)
        s.alloc("rc", 4, at=R_D + 53 * KB)
        s.alloc("atm", 1024 * 2, at=R_D + 54 * KB)
        s.alloc("afm", 512 * NCH * 2, at=R_E + 2 * KB)
        s.alloc("ost", 8 * 512 * 4, at=R_E + 10 * KB)
        q, k, v = s.q_ap, s.k_ap, s.v_ap
        wm = s.bf16("wm")
        probs2 = [s.bf16("probsA"), s.bf16("probsB")]
        rc = s.f32("rc")
        atm = s.bf16("atm")
        afm = s.bf16("afm")
        ost = s.f32("ost")

        s.ldg(wm, s.Wm.rearrange("(c p) o -> p c o", p=128), sem="ldgB")
        s.wait("tensor", "ldgB")

        def qh(h, cols):
            return q[:, (h // 2) * T:][:, cols][64 * (h % 2):64 * (h % 2) + 64, :]

        def kh(h, cols):
            return k[:, (h // 2) * T:][:, cols][64 * (h % 2):64 * (h % 2) + 64, :]

        exp_marks = {}
        pv_marks = []
        pv_stt = {}
        sv_marks = {}
        for qt in range(16):
            nblk = qt + 1
            s.wait("tensor", "pst", s.val("pst"))
            pend = None
            for h in range(H + 1):
                if h < H:
                    probsT = probs2[h % 2]
                    half1 = min(nblk, 8)
                    for hf in range(2 if nblk > 8 else 1):
                        b0, b1 = (0, half1) if hf == 0 else (8, nblk)
                        if exp_marks.get(hf) is not None:
                            s.wait("tensor", exp_marks[hf][0], exp_marks[hf][1])
                        for kt in range(b0, b1):
                            bank = s.psum[2 + kt // 4]
                            st_ = (kt % 4 == 0)
                            sp_ = (kt % 4 == 3 or kt == nblk - 1)
                            s.op("tensor", lambda e, h=h, qt=qt, kt=kt, b=bank,
                                 st_=st_, sp_=sp_: e.matmul(
                                out=b[:, (kt % 4) * 128:(kt % 4) * 128 + 128],
                                lhsT=kh(h, slice(kt * 128, (kt + 1) * 128)),
                                rhs=qh(h, slice(qt * 128, (qt + 1) * 128)),
                                start=st_, stop=sp_),
                                inc="stt" if kt == b1 - 1 else None)
                        s.wait("scalar", "stt", s.val("stt"))
                        if hf == 0 and pv_stt.get(h % 2) is not None:
                            s.wait("scalar", "stt", pv_stt[h % 2])
                        w0, wq = b0 * 128, (b1 - b0) * 128
                        s.op("scalar", lambda e, w0=w0, wq=wq, pt=probsT: e.activation(
                            out=pt[:, w0:w0 + wq],
                            in_=s.psumbig.ap()[:, 1024 + w0:1024 + w0 + wq],
                            func=AF.Exp), inc="pss")
                        exp_marks[hf] = ("pss", s.val("pss"))
                    s.wait("vector", "pss", s.val("pss"))
                    s.op("vector", lambda e, qt=qt, pt=probsT: e.tensor_tensor(
                        out=pt[:, qt * 128:(qt + 1) * 128],
                        in0=pt[:, qt * 128:(qt + 1) * 128], in1=s.triu,
                        op=ALU.mult), inc="sv")
                    sv_marks[h] = ("sv", s.val("sv"))
                if pend is not None:
                    ph, pprobs = pend
                    s.wait("tensor", sv_marks[ph][0], sv_marks[ph][1])
                    pvb = s.psum[6 + ph % 2]
                    if len(pv_marks) >= 2:
                        s.wait("tensor", "pst", pv_marks[-2])
                    for kt in range(nblk):
                        sp_ = (kt == nblk - 1)
                        s.op("tensor", lambda e, ph=ph, kt=kt, b=pvb, sp_=sp_,
                             pt=pprobs: e.matmul(
                            out=b[:, 0:65],
                            lhsT=pt[:, kt * 128:(kt + 1) * 128],
                            rhs=v[:, kt * 1040 + 65 * ph: kt * 1040 + 65 * ph + 65],
                            start=(kt == 0), stop=sp_),
                            inc="stt" if sp_ else None)
                    pv_stt[ph % 2] = s.val("stt")
                    s.wait("vector", "stt", s.val("stt"))
                    s.wait("vector", "pst", s.val("pst"))
                    s.op("vector", lambda e, b=pvb: e.reciprocal(out=rc, in_=b[:, 64:65]))
                    s.chain("vector")
                    s.op("vector", lambda e, ph=ph, b=pvb: e.tensor_scalar(
                        out=atm[:, ph * 64:(ph + 1) * 64], in0=b[:, 0:64],
                        scalar1=rc, scalar2=None, op0=ALU.mult), inc="pst")
                    pv_marks.append(s.val("pst"))
                    if s.prepass_gen is not None:
                        if next(s.prepass_gen, "END") == "END":
                            s.prepass_gen = None
                        if s.bg_pst_mark is not None:
                            pv_marks += [s.bg_pst_mark, s.bg_pst_mark]
                            s.bg_pst_mark = None
                pend = (h, probs2[h % 2]) if h < H else None

            s.chain("vector")
            for pc in range(NCH):
                bank = s.psum[pc % 2]
                bb = bank.bitcast(BF16)
                if pc >= 2:
                    s.wait("tensor", "pst", s.val("pst") - 1)
                s.op("tensor", lambda e, pc=pc, b=bb: e.transpose(
                    out=b[:, 0:128], in_=atm[:, pc * 128:(pc + 1) * 128],
                    identity=s.identb), inc="stt")
                s.wait("vector", "stt", s.val("stt"))
                s.op("vector", lambda e, pc=pc, qt=qt, b=bb: e.tensor_copy(
                    out=afm[:, pc * 512 + (qt % 4) * 128:
                            pc * 512 + (qt % 4) * 128 + 128],
                    in_=b[:, 0:128]), inc="pst")

            if qt % 4 == 3:
                ntg = qt // 4
                s.chain("vector")
                s.wait("vector", "st", s.val("st"))
                s.wait("tensor", "pst", s.val("pst"))
                for oc in range(NCH):
                    bank = s.psum[oc]
                    for kc in range(NCH):
                        s.op("tensor", lambda e, oc=oc, kc=kc, b=bank: e.matmul(
                            out=b[:, 0:512],
                            lhsT=wm[:, kc * D + oc * 128:kc * D + (oc + 1) * 128],
                            rhs=afm[:, kc * 512:(kc + 1) * 512],
                            start=(kc == 0), stop=(kc == NCH - 1)),
                            inc="stt" if kc == NCH - 1 else None)
                    s.wait("vector", "stt", s.val("stt"))
                    s.op("vector", lambda e, oc=oc, b=bank: e.tensor_scalar(
                        out=ost[:, oc * 512:(oc + 1) * 512], in0=b[:, 0:512],
                        scalar1=s.bm_sb[:, oc:oc + 1], scalar2=None,
                        op0=ALU.add), inc="pst")
                s.wait("sync", "pst", s.val("pst"))
                s.sth(s.attn_s.rearrange("p (c t) -> p c t", c=NCH)[:, :, ntg * 512:(ntg + 1) * 512],
                      ost.rearrange("p (c t) -> p c t", c=NCH))
                s.wait("tensor", "pst", s.val("pst"))
                exp_marks = {}
                s.wait("tensor", "pss", s.val("pss"))
        s.attn_stores = s.val("st")
        s.pending_adds.append(s.attn_s)

    def prepass_generator(self):
        s = self
        p = s.pool.ap()
        co = s.spans["consts"][0]
        base = co + 2560
        thr_cols, nthr_cols, mw_cols = [], [], []
        for i in range(3):
            thr_cols.append(p[:, (base + i * 12) // 4:][:, :1])
            nthr_cols.append(p[:, (base + i * 12 + 4) // 4:][:, :1])
            mw_cols.append(p[:, (base + i * 12 + 8) // 4:][:, :1])
        accc = p[:, (base + 40) // 4:][:, :1]
        cell = p[0:1, (base + 44) // 4:][:, :1]
        cell2 = p[0:1, (base + 48) // 4:][:, :1]
        redc = p[:, (base + 52) // 4:][:, :1]
        s.thr_cols, s.nthr_cols, s.mw_cols = thr_cols, nthr_cols, mw_cols
        mroff = s.mrow_off
        ws = [p[:, (mroff + i * 4 * KB) // 4:][:, :1024] for i in range(3)]
        gA = p[:, (198 * KB) // 4:][:, :1024]
        pb = s.pool.ap().bitcast(BF16)
        iqb = [pb[:, (194 * KB) // 2:][:, :1024], pb[:, (202 * KB) // 2:][:, :1024]]
        sems3 = ("ldgA", "ldgB", "ldgC")

        accc_w = [accc, p[:, (base + 60) // 4:][:, :1], p[:, (base + 64) // 4:][:, :1]]
        weights = [(s.wgT, s.iwg_s, 8, 4, D * INNER),
                   (s.wuT, s.iwu_s, 8, 4, D * INNER),
                   (s.wdT, s.iwd_s, 32, 1, D * INNER)]
        chunks = []
        for ps in (1, 2):
            for wi in range(3):
                nn, no = weights[wi][2], weights[wi][3]
                for n in range(nn):
                    for o in range(no):
                        chunks.append((wi, n, o))
        NLC = len(chunks)

        def chunk_ap(t, wi, n, o):
            r = t.rearrange("(n p) o -> p n o", p=128)
            return r[:, n, :][:, o * 1024:(o + 1) * 1024]

        state = {"issued": 0}
        load_marks = [None] * NLC
        cons_marks = [None, None, None]
        st_marks = [None, None]

        def issue_one():
            i = state["issued"]
            if i >= NLC:
                return
            b = i % 3
            wi, n, o = chunks[i]
            if cons_marks[b] is not None:
                s.wait("gpsimd", cons_marks[b][0], cons_marks[b][1])
            pv = s.val(sems3[b])
            if pv > 0:
                s.wait("gpsimd", sems3[b], pv)
            s.ldg(ws[b], chunk_ap(weights[wi][0], wi, n, o), sem=sems3[b])
            load_marks[i] = (sems3[b], s.val(sems3[b]))
            state["issued"] = i + 1

        consumed = 0
        for wi in range(3):
            wsrc, wdst, nn, no, nel = weights[wi]
            acw = accc_w[wi]
            for ci in range(nn * no):
                while state["issued"] - consumed < 3 and state["issued"] < NLC:
                    issue_one()
                b = consumed % 3
                m = load_marks[consumed]
                s.wait("vector", m[0], m[1])
                if ci == 0:
                    s.op("vector", lambda e, bb=ws[b], acw=acw: e.tensor_reduce(
                        out=acw, in_=bb, axis=AX.X, op=ALU.add,
                        apply_absolute_value=True))
                else:
                    s.op("vector", lambda e, bb=ws[b]: e.tensor_reduce(
                        out=redc, in_=bb, axis=AX.X, op=ALU.add,
                        apply_absolute_value=True))
                    s.chain("vector")
                    s.op("vector", lambda e, acw=acw: e.tensor_tensor(
                        out=acw, in0=acw, in1=redc, op=ALU.add))
                cons_marks[b] = s.chain("vector")
                consumed += 1
                yield
        for wi in range(3):
            wsrc, wdst, nn, no, nel = weights[wi]
            acw = accc_w[wi]
            s.wait("tensor", "vch", s.val("vch"))
            s.wait("tensor", "pst", s.val("pst"))
            s.op("tensor", lambda e, acw=acw: e.matmul(
                out=s.psum[7][0:1, 0:1], lhsT=acw, rhs=s.ones_col,
                start=True, stop=True), inc="stt")
            s.wait("vector", "stt", s.val("stt"))
            s.op("vector", lambda e, n=nel: e.tensor_scalar(
                out=cell, in0=s.psum[7][0:1, 0:1], scalar1=1.0 / n,
                scalar2=1e-5, op0=ALU.mult, op1=ALU.max))
            s.chain("vector")
            s.op("vector", lambda e: e.tensor_scalar(
                out=cell2, in0=cell, scalar1=0.5, scalar2=None,
                op0=ALU.mult), inc="sv")
            s.wait("tensor", "sv", s.val("sv"))
            s.op("tensor", lambda e: e.matmul(
                out=s.psum[7][:, 1:2], lhsT=s.ones_row, rhs=cell2,
                start=True, stop=True))
            s.op("tensor", lambda e: e.matmul(
                out=s.psum[7][:, 2:3], lhsT=s.ones_row, rhs=cell,
                start=True, stop=True), inc="stt")
            s.wait("vector", "stt", s.val("stt"))
            s.op("vector", lambda e, wi=wi: e.tensor_copy(
                out=thr_cols[wi], in_=s.psum[7][:, 1:2]))
            s.op("vector", lambda e, wi=wi: e.tensor_copy(
                out=mw_cols[wi], in_=s.psum[7][:, 2:3]))
            s.chain("vector")
            s.op("vector", lambda e, wi=wi: e.tensor_scalar(
                out=nthr_cols[wi], in0=thr_cols[wi], scalar1=-1.0,
                scalar2=None, op0=ALU.mult), inc="pst")
            s.bg_pst_mark = s.val("pst")
            nthr_mark = ("pst", s.val("pst"))
            yield
            for ci in range(nn * no):
                while state["issued"] - consumed < 3 and state["issued"] < NLC:
                    issue_one()
                b = consumed % 3
                qb = consumed % 2
                wi2, n, o = chunks[consumed]
                m = load_marks[consumed]
                for hf in range(2):
                    cs = slice(hf * 512, hf * 512 + 512)
                    s.wait("vector", m[0], m[1])
                    if ci == 0 and hf == 0:
                        s.wait("vector", nthr_mark[0], nthr_mark[1])
                    if hf == 0 and st_marks[qb] is not None:
                        s.wait("vector", "st", st_marks[qb])
                    s.op("vector", lambda e, bb=ws[b], wi=wi, cs=cs: e.tensor_scalar(
                        out=gA[:, cs], in0=bb[:, cs], scalar1=nthr_cols[wi],
                        scalar2=None, op0=ALU.is_lt))
                    s.chain("vector")
                    s.op("vector", lambda e, bb=ws[b], wi=wi, q=qb, cs=cs:
                         e.scalar_tensor_tensor(
                            out=iqb[q][:, cs], in0=bb[:, cs], scalar=thr_cols[wi],
                            in1=gA[:, cs], op0=ALU.is_gt, op1=ALU.subtract))
                    cmk = s.chain("vector")
                    if hf == 1:
                        cons_marks[b] = cmk
                        s.wait("sync", cmk[0], cmk[1])
                        s.sth(chunk_ap(wdst, wi, n, o), iqb[qb])
                        st_marks[qb] = s.val("st")
                        consumed += 1
                    yield

    def p_l0_ffn(self):
        s = self
        s.phase_barrier()
        s.alloc("mb2", T * 4, at=R_D)
        s.alloc("rb2", T * 4, at=R_D + 8 * KB)
        s.alloc("oscx", 512 * 4, at=R_D + 16 * KB)
        s.alloc("scx", 512 * 4, at=R_D + 18 * KB)
        s.alloc("ixs", 8 * 512 * 2, at=R_D + 20 * KB)
        s.alloc("am", 512 * 4, at=R_D + 28 * KB)
        s.alloc("cm", 512 * 4, at=R_D + 30 * KB)
        s.alloc("hbuf", 32 * 512 * 4, at=R_D + 32 * KB)
        s.alloc("scb", 512 * 4, at=188 * KB)
        s.alloc("oscb", 512 * 4, at=190 * KB)
        s.alloc("vt", 1024 * 4, at=192 * KB)
        s.alloc("dt", 1024 * 4, at=196 * KB)
        s.alloc("ihg", 1024 * 2, at=200 * KB)
        s.alloc("rep0", 512, at=202 * KB)
        s.alloc("rep1", 512, at=202 * KB + 512)
        s.alloc("sccols", 64, at=203 * KB)
        mb, rb = s.f32("mb2"), s.f32("rb2")
        oscx, scx = s.f32("oscx"), s.f32("scx")
        ixs = s.bf16("ixs")
        am, cm = s.f32("am"), s.f32("cm")
        hb = s.f32("hbuf")
        scb, oscb = s.f32("scb"), s.f32("oscb")
        vt, dt = s.f32("vt"), s.f32("dt")
        ihg = s.bf16("ihg")
        s.rep_aps = [s.f32("rep0"), s.f32("rep1")]
        s.dbg_sb_tmp = s.bf16("ixs") if False else s.pool.ap().bitcast(BF16)[:, (R_D + 20 * KB) // 2:][:, :16 * 512]
        sc_span = s.f32("sccols")
        s.sccol = sc_span[:, 0:1]
        s.sccol2 = sc_span[:, 1:2]
        p = s.pool.ap()
        mroff = s.mrow_off
        gbuf = [s.pool.ap().bitcast(BF16)[:, (mroff + i * 2048) // 2:][:, :1024]
                for i in range(4)]
        dbuf = [s.pool.ap().bitcast(BF16)[:, (mroff + i * 8192) // 2:][:, :4096]
                for i in range(2)]

        s.ldh(s.f32("x"), s.xpark)
        v = s.val("ldh")
        s.wait("vector", "ldh", v)
        s.wait("scalar", "ldh", v)
        s.wait("tensor", "ldh", v)
        s.x_parked = False

        s.stats_mb_rb(mb, rb, hb[:, 0:T], hb[:, T:2 * T])

        if s.prepass_gen is not None:
            for _ in s.prepass_gen:
                pass
            s.prepass_gen = None
        for sm in ("ldgA", "ldgB", "ldgC", "st"):
            v = s.val(sm)
            s.wait("vector", sm, v)
            s.wait("scalar", sm, v)
            s.wait("sync", sm, v)
            s.wait("tensor", sm, v)
        s.wait("tensor", "pst", s.val("pst"))
        mw_cols = s.mw_cols

        evac_marks = []
        for ti in range(NT):
            tsl = slice(ti * 512, (ti + 1) * 512)
            s.chain("vector")
            for c in range(NCH):
                st_ap = hb[:, c * 512:(c + 1) * 512]
                s.op("vector", lambda e, c=c, a=st_ap, tsl=tsl: e.tensor_tensor(
                    out=a, in0=s.xc(c)[:, tsl], in1=mb[:, tsl], op=ALU.subtract))
                s.chain("vector")
                amn = dt[:, 0:512]
                if c == 0:
                    s.op("vector", lambda e, a=st_ap: e.tensor_copy(out=am, in_=a))
                    s.chain("vector")
                    s.op("vector", lambda e, a=st_ap, amn=amn: e.tensor_copy(out=amn, in_=a))
                else:
                    s.op("vector", lambda e, a=st_ap: e.tensor_tensor(
                        out=am, in0=am, in1=a, op=ALU.max))
                    s.chain("vector")
                    s.op("vector", lambda e, a=st_ap, amn=amn: e.tensor_tensor(
                        out=amn, in0=amn, in1=a, op=ALU.min))
                s.chain("vector")
            amn = dt[:, 0:512]
            s.op("vector", lambda e, amn=amn: e.tensor_scalar(
                out=amn, in0=amn, scalar1=-1.0, scalar2=None, op0=ALU.mult))
            s.chain("vector")
            s.op("vector", lambda e, amn=amn: e.tensor_tensor(
                out=am, in0=am, in1=amn, op=ALU.max))
            amk = s.chain("vector")
            self.scales_from_colmax(am, scx, oscx, amk)
            s.op("vector", lambda e, tsl=tsl: e.tensor_tensor(
                out=oscx, in0=oscx, in1=rb[:, tsl], op=ALU.mult))
            s.chain("vector")
            s.op("vector", lambda e: e.tensor_scalar(
                out=oscx, in0=oscx, scalar1=1e-5, scalar2=1.0 / 127.0,
                op0=ALU.max, op1=ALU.mult))
            s.chain("vector")
            dAx = s.pool.ap().bitcast(BF16)[:, (R_D + 48 * KB) // 2:][:, :1024]
            dBx = s.pool.ap().bitcast(BF16)[:, (R_D + 50 * KB) // 2:][:, :1024]
            for pr in range(2):
                for c in range(4 * pr, 4 * pr + 4):
                    a = hb[:, c * 512:(c + 1) * 512]
                    s.op("vector", lambda e, c=c, a=a: e.tensor_tensor(
                        out=a, in0=a, in1=scx, op=ALU.mult))
                    s.chain("vector")
                g0, g1 = 2 * pr, 2 * pr + 1
                s.round_pair([
                    (hb[:, g0 * 1024:(g0 + 1) * 1024], vt, dAx,
                     ixs[:, g0 * 1024:(g0 + 1) * 1024]),
                    (hb[:, g1 * 1024:(g1 + 1) * 1024], dt, dBx,
                     ixs[:, g1 * 1024:(g1 + 1) * 1024])])
            ix_done = s.chain("vector")
            s.wait("tensor", ix_done[0], ix_done[1])
            s.wait("scalar", ix_done[0], ix_done[1])
            if s.dump == "ix0" and ti == 0:
                s.wait("sync", ix_done[0], ix_done[1])
                s.sth(s.dbg[:, 0:8 * 512], ixs)
                s.wait("vector", "st", s.val("st"))
                s.wait("tensor", "st", s.val("st"))

            s.op("vector", lambda e: e.memset(cm, 0.0))
            s.chain("vector")
            cmn = dt[:, 512:1024]
            s.op("vector", lambda e, cmn=cmn: e.memset(cmn, 0.0))
            s.chain("vector")
            ldmarks = {}
            ssm = {}
            for oc in range(ICH + 1):
                if oc < ICH:
                    gb, ub = gbuf[2 * (oc % 2)], gbuf[2 * (oc % 2) + 1]
                    if oc >= 2:
                        s.wait("sync", ldmarks[oc - 2][0], ldmarks[oc - 2][1])
                    else:
                        mk = s.chain("tensor")
                        s.wait("sync", mk[0], mk[1])
                    psem = "lha" if oc % 2 == 0 else "lhb"
                    s.ldh(gb, s.iwg_s.rearrange("(n p) o -> p n o", p=128)[:, :, oc * 128:(oc + 1) * 128], sem=psem)
                    s.ldh(ub, s.iwu_s.rearrange("(n p) o -> p n o", p=128)[:, :, oc * 128:(oc + 1) * 128], sem=psem)
                    s.wait("tensor", psem, s.val(psem))
                    if len(evac_marks) >= 2:
                        s.wait("tensor", "pst", evac_marks[-2])
                    bg, bu = s.psum[oc % 3], s.psum[3 + oc % 3]
                    for kc in range(NCH):
                        s.op("tensor", lambda e, kc=kc, b=bg, gb=gb: e.matmul(
                            out=b[:, 0:512], lhsT=gb[:, kc * 128:(kc + 1) * 128],
                            rhs=ixs[:, kc * 512:(kc + 1) * 512],
                            start=(kc == 0), stop=(kc == NCH - 1)))
                    for kc in range(NCH):
                        s.op("tensor", lambda e, kc=kc, b=bu, ub=ub: e.matmul(
                            out=b[:, 0:512], lhsT=ub[:, kc * 128:(kc + 1) * 128],
                            rhs=ixs[:, kc * 512:(kc + 1) * 512],
                            start=(kc == 0), stop=(kc == NCH - 1)),
                            inc="stt" if kc == NCH - 1 else None)
                    mk = s.chain("tensor")
                    ldmarks[oc] = mk
                    s.wait("vector", "stt", s.val("stt"))
                    gs = vt[:, (oc % 2) * 512:(oc % 2) * 512 + 512]
                    sil = scb if oc % 2 == 0 else oscb
                    s.op("vector", lambda e, b=bg, gs=gs: e.scalar_tensor_tensor(
                        out=gs, in0=b[:, 0:512], scalar=mw_cols[0], in1=oscx,
                        op0=ALU.mult, op1=ALU.mult), inc="pst")
                    mkv = s.chain("vector")
                    s.wait("scalar", mkv[0], mkv[1])
                    s.op("scalar", lambda e, gs=gs, sil=sil: e.activation(
                        out=sil, in_=gs, func=AF.Silu), inc="ss")
                    ssm[oc] = s.val("ss")
                if oc >= 1:
                    po = oc - 1
                    pbu = s.psum[3 + po % 3]
                    psil = scb if po % 2 == 0 else oscb
                    s.wait("vector", "ss", ssm[po])
                    s.op("vector", lambda e, po=po, b=pbu, sil=psil: e.tensor_tensor(
                        out=hb[:, po * 512:(po + 1) * 512], in0=b[:, 0:512],
                        in1=sil, op=ALU.mult), inc="pst")
                    evac_marks.append(s.val("pst"))
                    s.chain("vector")
                    s.op("vector", lambda e, po=po: e.tensor_tensor(
                        out=cm, in0=cm, in1=hb[:, po * 512:(po + 1) * 512], op=ALU.max))
                    s.chain("vector")
                    s.op("vector", lambda e, po=po, cmn=cmn: e.tensor_tensor(
                        out=cmn, in0=cmn, in1=hb[:, po * 512:(po + 1) * 512], op=ALU.min))
                    s.chain("vector")
            s.op("vector", lambda e, cmn=cmn: e.tensor_scalar(
                out=cmn, in0=cmn, scalar1=-1.0, scalar2=None, op0=ALU.mult))
            s.chain("vector")
            s.op("vector", lambda e, cmn=cmn: e.tensor_tensor(
                out=cm, in0=cm, in1=cmn, op=ALU.max))
            cmk = s.chain("vector")
            if s.dump == "h0" and ti == 0:
                s.wait("sync", cmk[0], cmk[1])
                s.op("vector", lambda e: e.tensor_copy(
                    out=s.dbg_sb_tmp, in_=hb[:, 0:16 * 512]), inc="sv")
                s.chain("vector")
                s.wait("sync", "sv", s.val("sv"))
                s.sth(s.dbg[:, 0:16 * 512], s.dbg_sb_tmp)
                s.wait("vector", "st", s.val("st"))
                s.wait("tensor", "st", s.val("st"))
            self.scales_from_colmax(cm, scb, oscb, cmk)
            s.op("vector", lambda e: e.tensor_tensor(
                out=oscb, in0=oscb, in1=oscx, op=ALU.mult))
            s.chain("vector")
            s.op("vector", lambda e: e.tensor_scalar(
                out=oscb, in0=oscb, scalar1=mw_cols[1], scalar2=1e-5,
                op0=ALU.mult, op1=ALU.max))
            s.chain("vector")
            s.op("vector", lambda e: e.tensor_scalar(
                out=oscb, in0=oscb, scalar1=mw_cols[2], scalar2=1.0 / 127.0,
                op0=ALU.mult, op1=ALU.mult))
            s.chain("vector")

            dstream_marks = {}
            pb_ = s.pool.ap().bitcast(BF16)
            dAh = pb_[:, (R_D + 20 * KB) // 2:][:, :1024]
            dBh = pb_[:, (R_D + 22 * KB) // 2:][:, :1024]
            ihgB = pb_[:, (R_D + 24 * KB) // 2:][:, :1024]
            for g in range(NCH):
                db = dbuf[g % 2]
                if g >= 2:
                    s.wait("sync", dstream_marks[g - 2][0], dstream_marks[g - 2][1])
                else:
                    mk = s.chain("tensor")
                    s.wait("sync", mk[0], mk[1])
                dsem = "lha" if g % 2 == 0 else "lhb"
                s.ldh(db, s.iwd_s.rearrange("(n p) o -> p n o", p=128)[:, 4 * g:4 * g + 4, :], sem=dsem)
                dlw = (dsem, s.val(dsem))
                for kc in range(4 * g, 4 * g + 4):
                    a = hb[:, kc * 512:(kc + 1) * 512]
                    s.op("vector", lambda e, a=a: e.tensor_tensor(
                        out=a, in0=a, in1=scb, op=ALU.mult))
                    s.chain("vector")
                mk = s.chain("tensor")
                s.wait("vector", mk[0], mk[1])
                s.round_pair([
                    (hb[:, (4 * g) * 512:(4 * g + 2) * 512], vt, dAh, ihg),
                    (hb[:, (4 * g + 2) * 512:(4 * g + 4) * 512], dt, dBh, ihgB)])
                rg = s.chain("vector")
                if s.dump == "ih0" and ti == 0 and g == 0:
                    s.wait("sync", rg[0], rg[1])
                    s.sth(s.dbg[:, 0:1024], ihg)
                    s.wait("vector", "st", s.val("st"))
                    s.wait("tensor", "st", s.val("st"))
                s.wait("tensor", rg[0], rg[1])
                s.wait("tensor", dlw[0], dlw[1])
                for oc8 in range(NCH):
                    for kci in range(4):
                        last = (g == NCH - 1 and kci == 3)
                        stt_ = (g == 0 and kci == 0)
                        rhs_ = ihg if kci < 2 else ihgB
                        c2 = kci % 2
                        s.op("tensor", lambda e, oc8=oc8, kci=kci, db=db, c2=c2,
                             rhs_=rhs_, st_=stt_, last=last: e.matmul(
                            out=s.psum[oc8][:, 0:512],
                            lhsT=db[:, kci * 1024 + oc8 * 128:kci * 1024 + (oc8 + 1) * 128],
                            rhs=rhs_[:, c2 * 512:(c2 + 1) * 512],
                            start=st_, stop=last),
                            inc="stt" if (oc8 == NCH - 1 and kci == 3) else None)
                mk = s.chain("tensor")
                dstream_marks[g] = mk
            s.wait("vector", "stt", s.val("stt"))
            if s.dump == "dps0" and ti == 0:
                s.op("vector", lambda e: e.tensor_copy(
                    out=s.dbg_sb_tmp[:, 0:512], in_=s.psum[0][:, 0:512]), inc="sv")
                s.chain("vector")
                s.wait("sync", "sv", s.val("sv"))
                s.sth(s.dbg[:, 0:512], s.dbg_sb_tmp[:, 0:512])
                s.wait("vector", "st", s.val("st"))
            for oc8 in range(NCH):
                s.op("vector", lambda e, oc8=oc8: e.tensor_tensor(
                    out=vt[:, 0:512], in0=s.psum[oc8][:, 0:512], in1=oscb, op=ALU.mult))
                s.chain("vector")
                s.op("vector", lambda e, oc8=oc8, tsl=tsl: e.tensor_tensor(
                    out=s.xc(oc8)[:, tsl], in0=s.xc(oc8)[:, tsl], in1=vt[:, 0:512],
                    op=ALU.add), inc="pst")
                s.chain("vector")
            evac_marks.append(s.val("pst"))
            dn = s.chain("vector")
            s.wait("tensor", dn[0], dn[1])

        mkv = s.chain("vector")
        s.wait("sync", mkv[0], mkv[1])
        s.ldh(hb[:, 0:NCH * T], s.attn_s)
        v2 = s.val("ldh")
        s.wait("vector", "ldh", v2)
        for c in range(NCH):
            s.op("vector", lambda e, c=c: e.tensor_tensor(
                out=s.xc(c), in0=s.xc(c), in1=hb[:, c * T:(c + 1) * T],
                op=ALU.add), inc="sv")
        s.chain("vector")
        s.pending_adds = []

    def scales_from_colmax(self, am_ap, sc_ap, osc_ap, amk):
        s = self
        s.wait("tensor", amk[0], amk[1])
        for t4 in range(4):
            bank = s.psum[4 + t4 % 2]
            s.op("tensor", lambda e, t4=t4, b=bank: e.transpose(
                out=b[:, 0:128], in_=am_ap[:, t4 * 128:(t4 + 1) * 128],
                identity=s.ident), inc="stt")
            s.wait("vector", "stt", s.val("stt"))
            s.op("vector", lambda e, b=bank: e.tensor_reduce(
                out=s.sccol, in_=b[:, 0:128], axis=AX.X, op=ALU.max))
            s.chain("vector")
            s.op("vector", lambda e: e.tensor_copy(out=s.sccol2, in_=s.sccol))
            s.chain("vector")
            s.op("vector", lambda e: e.tensor_scalar(
                out=s.sccol, in0=s.sccol, scalar1=1e-20, scalar2=None, op0=ALU.max))
            s.chain("vector")
            s.op("vector", lambda e: e.reciprocal(out=s.sccol, in_=s.sccol))
            s.chain("vector")
            s.op("vector", lambda e: e.tensor_scalar(
                out=s.sccol, in0=s.sccol, scalar1=127.0, scalar2=None, op0=ALU.mult))
            s.chain("vector")
            s.op("vector", lambda e: e.tensor_scalar(
                out=s.rep_aps[0], in0=s.ones128, scalar1=s.sccol, scalar2=None,
                op0=ALU.mult))
            s.chain("vector")
            s.op("vector", lambda e: e.tensor_scalar(
                out=s.rep_aps[1], in0=s.ones128, scalar1=s.sccol2, scalar2=None,
                op0=ALU.mult), inc="sv")
            rv = s.val("sv")
            s.wait("tensor", "sv", rv)
            b2 = s.psum[6]
            s.op("tensor", lambda e, b=b2: e.transpose(
                out=b[:, 0:128], in_=s.rep_aps[0], identity=s.ident))
            s.op("tensor", lambda e, b=b2: e.transpose(
                out=b[:, 128:256], in_=s.rep_aps[1], identity=s.ident), inc="stt")
            s.wait("vector", "stt", s.val("stt"))
            s.op("vector", lambda e, t4=t4, b=b2: e.tensor_copy(
                out=sc_ap[:, t4 * 128:(t4 + 1) * 128], in_=b[:, 0:128]))
            s.op("vector", lambda e, t4=t4, b=b2: e.tensor_copy(
                out=osc_ap[:, t4 * 128:(t4 + 1) * 128], in_=b[:, 128:256]), inc="pst")
            s.wait("tensor", "pst", s.val("pst"))
        s.chain("vector")


    def p_l1_attn(self):
        s = self
        s.phase_barrier()
        s.sth(s.xpark, s.f32("x"))
        s.x_parked = True
        pk = s.val("st")
        s.wait("scalar", "st", pk)
        s.wait("vector", "st", pk)
        s.alloc("qe", T * NCH * 2, at=R_X)
        s.alloc("ke", 16 * D * 2, at=R_X + 32 * KB)
        s.alloc("v1", 16 * D * 2, at=R_D)
        s.alloc("wstr", D * NCH * 2, at=R_E)
        s.alloc("sel8", 1024 * 4, at=R_E + 16 * KB)
        s.alloc("kvsb", 512 * 2, at=R_E + 20 * KB)
        s.alloc("ks", 8 * 16 * 2, at=R_E + 21 * KB)
        s.alloc("qks", T * 4, at=R_E + 22 * KB)
        s.alloc("rden", T * 4, at=R_E + 30 * KB)
        s.alloc("elu0", 512 * 4, at=R_E + 38 * KB)
        s.alloc("elu1", 512 * 4, at=R_E + 40 * KB)
        s.alloc("denb", 512 * 4, at=R_E + 42 * KB)
        s.alloc("ost2", 8 * 512 * 4, at=R_D)
        qe, ke, v1 = s.bf16("qe"), s.bf16("ke"), s.bf16("v1")
        wstr = s.bf16("wstr")
        sel8 = s.f32("sel8")[0:16, :]
        kvsb, ks = s.bf16("kvsb"), s.bf16("ks")
        qks = s.f32("qks")[0:16, :]
        rden = s.f32("rden")[0:16, :]
        m0, e1 = s.f32("elu0"), s.f32("elu1")
        denb = s.f32("denb")
        s.ldh(sel8, s.sel8d)
        s.wait("tensor", "ldh", s.val("ldh"))

        def elu_evac(bank, dst, w):
            s.op("vector", lambda e, b=bank, w=w: e.tensor_scalar(
                out=m0[:, 0:w], in0=b[:, 0:w], scalar1=0.0, scalar2=None,
                op0=ALU.min), inc="pst")
            mk = s.chain("vector")
            s.wait("scalar", mk[0], mk[1])
            s.op("scalar", lambda e, w=w: e.activation(
                out=e1[:, 0:w], in_=m0[:, 0:w], func=AF.Exp), inc="ss")
            s.wait("vector", "ss", s.val("ss"))
            s.op("vector", lambda e, b=bank, d=dst, w=w: e.scalar_tensor_tensor(
                out=d, in0=b[:, 0:w], scalar=0.0, in1=e1[:, 0:w],
                op0=ALU.max, op1=ALU.add), inc="pst")
            s.chain("vector")

        s.ldg(wstr, s.Wqkv.rearrange("(c p) o -> p c o", p=128)[:, :, 0:D], sem="ldgA")
        s.wait("tensor", "ldgA")
        kidx = 0
        for oc in range(NCH):
            for j in range(NT):
                bank = s.psum[kidx % 3]
                s.wait("tensor", "pst", s.val("pst") - 4)
                for kc in range(NCH):
                    s.op("tensor", lambda e, oc=oc, j=j, kc=kc, b=bank: e.matmul(
                        out=b[:, 0:512],
                        lhsT=wstr[:, kc * D + oc * 128:kc * D + (oc + 1) * 128],
                        rhs=s.xn_ap[:, kc * T + j * 512:kc * T + (j + 1) * 512],
                        start=(kc == 0), stop=(kc == NCH - 1)),
                        inc="stt" if kc == NCH - 1 else None)
                s.wait("vector", "stt", s.val("stt"))
                elu_evac(bank, qe[:, oc * T + j * 512:oc * T + (j + 1) * 512], 512)
                kidx += 1
        mk = s.chain("tensor")
        s.wait("gpsimd", mk[0], mk[1])
        s.ldg(wstr, s.Wqkv.rearrange("(c p) o -> p c o", p=128)[:, :, D:2 * D], sem="ldgB")
        s.wait("tensor", "ldgB")
        kidx = 0
        for a in range(16):
            for half in range(2):
                bank = s.psum[3 + kidx % 3]
                s.wait("tensor", "pst", s.val("pst") - 4)
                for kc in range(NCH):
                    s.op("tensor", lambda e, a=a, hf=half, kc=kc, b=bank: e.matmul(
                        out=b[:, 0:512],
                        lhsT=s.xn_ap[:, kc * T + a * 128:kc * T + (a + 1) * 128],
                        rhs=wstr[:, kc * D + hf * 512:kc * D + (hf + 1) * 512],
                        start=(kc == 0), stop=(kc == NCH - 1)),
                        inc="stt" if kc == NCH - 1 else None)
                s.wait("vector", "stt", s.val("stt"))
                elu_evac(bank, ke[:, a * D + half * 512:a * D + (half + 1) * 512], 512)
                kidx += 1
        mk = s.chain("tensor")
        s.wait("gpsimd", mk[0], mk[1])
        s.ldg(wstr, s.Wqkv.rearrange("(c p) o -> p c o", p=128)[:, :, 2 * D:3 * D], sem="ldgA")
        s.wait("tensor", "ldgA")
        kidx = 0
        ploop = s.val("pst")
        for a in range(16):
            for half in range(2):
                bank = s.psum[3 + kidx % 3]
                s.wait("tensor", "pst", max(s.val("pst") - 2, ploop))
                for kc in range(NCH):
                    s.op("tensor", lambda e, a=a, hf=half, kc=kc, b=bank: e.matmul(
                        out=b[:, 0:512],
                        lhsT=s.xn_ap[:, kc * T + a * 128:kc * T + (a + 1) * 128],
                        rhs=wstr[:, kc * D + hf * 512:kc * D + (hf + 1) * 512],
                        start=(kc == 0), stop=(kc == NCH - 1)),
                        inc="stt" if kc == NCH - 1 else None)
                s.wait("vector", "stt", s.val("stt"))
                s.op("vector", lambda e, a=a, hf=half, b=bank: e.tensor_copy(
                    out=v1[:, a * D + hf * 512:a * D + (hf + 1) * 512],
                    in_=b[:, 0:512]), inc="pst")
                kidx += 1
        prj = s.chain("vector")
        s.wait("tensor", prj[0], prj[1])

        s.op("vector", lambda e: e.memset(ks, 0.0), inc="pst")
        s.chain("vector")
        s.wait("tensor", "pst", s.val("pst"))
        for hp in range(8):
            bank = s.psum[4 + hp % 2]
            for parity in range(2):
                h = 2 * hp + parity
                base = 64 * parity
                for kt in range(16):
                    s.op("tensor", lambda e, h=h, kt=kt, b=bank, base=base: e.matmul(
                        out=b[base:base + 64, 0:64],
                        lhsT=ke[:, kt * D + h * 64:kt * D + h * 64 + 64],
                        rhs=v1[:, kt * D + h * 64:kt * D + h * 64 + 64],
                        start=(kt == 0), stop=(kt == 15)),
                        inc="stt" if kt == 15 else None)
            s.wait("vector", "stt", s.val("stt"))
            s.op("vector", lambda e, hp=hp, b=bank: e.tensor_copy(
                out=kvsb[:, hp * 64:(hp + 1) * 64], in_=b[:, 0:64]), inc="pst")
            if hp >= 1:
                s.wait("tensor", "pst", s.val("pst") - 1)
        for h in range(H):
            bank = s.psum[6]
            base = 64 * (h % 2)
            for kt in range(16):
                s.op("tensor", lambda e, h=h, kt=kt, b=bank, base=base: e.matmul(
                    out=b[base:base + 64, h:h + 1],
                    lhsT=ke[:, kt * D + h * 64:kt * D + h * 64 + 64],
                    rhs=s.onesb_col,
                    start=(kt == 0), stop=(kt == 15)),
                    inc="stt" if kt == 15 else None)
            s.wait("vector", "stt", s.val("stt"))
            s.op("vector", lambda e, h=h, b=bank, base=base: e.tensor_copy(
                out=ks[base:base + 64, (h // 2) * 16 + h:(h // 2) * 16 + h + 1],
                in_=b[base:base + 64, h:h + 1]), inc="pst")
            s.wait("tensor", "pst", s.val("pst"))
        for j in range(NT):
            bank = s.psum[7]
            for kc in range(NCH):
                s.op("tensor", lambda e, j=j, kc=kc, b=bank: e.matmul(
                    out=b[0:16, 0:512],
                    lhsT=ks[:, kc * 16:(kc + 1) * 16],
                    rhs=qe[:, kc * T + j * 512:kc * T + (j + 1) * 512],
                    start=(kc == 0), stop=(kc == NCH - 1)),
                    inc="stt" if kc == NCH - 1 else None)
            s.wait("vector", "stt", s.val("stt"))
            s.op("vector", lambda e, j=j, b=bank: e.tensor_scalar(
                out=qks[:, j * 512:(j + 1) * 512], in0=b[0:16, 0:512],
                scalar1=1e-6, scalar2=None, op0=ALU.add), inc="pst")
            s.wait("tensor", "pst", s.val("pst"))
        s.chain("vector")
        s.op("vector", lambda e: e.reciprocal(out=rden, in_=qks), inc="sv")
        s.chain("vector")
        rdv = s.val("sv")
        s.wait("tensor", "sv", rdv)

        af1 = s.bf16("ke")
        kidx = 0
        for hp in range(8):
            for j in range(NT):
                bank = s.psum[kidx % 2]
                bden = s.psum[2 + kidx % 2]
                s.wait("tensor", "pst", s.val("pst") - 2)
                for parity in range(2):
                    h = 2 * hp + parity
                    base = 64 * parity
                    s.op("tensor", lambda e, h=h, hp=hp, j=j, b=bank, base=base: e.matmul(
                        out=b[base:base + 64, 0:512],
                        lhsT=kvsb[base:base + 64, hp * 64:(hp + 1) * 64],
                        rhs=qe[base:base + 64, (h // 2) * T + j * 512:(h // 2) * T + (j + 1) * 512],
                        start=True, stop=True))
                s.op("tensor", lambda e, hp=hp, j=j, b=bden: e.matmul(
                    out=b[:, 0:512], lhsT=sel8[:, hp * 128:(hp + 1) * 128],
                    rhs=rden[:, j * 512:(j + 1) * 512],
                    start=True, stop=True), inc="stt")
                s.wait("vector", "stt", s.val("stt"))
                s.op("vector", lambda e, b=bden: e.tensor_copy(out=denb, in_=b[:, 0:512]), inc="pst")
                s.chain("vector")
                s.op("vector", lambda e, hp=hp, j=j, b=bank: e.tensor_tensor(
                    out=af1[:, hp * T + j * 512:hp * T + (j + 1) * 512],
                    in0=b[:, 0:512], in1=denb, op=ALU.mult), inc="pst")
                s.chain("vector")
                kidx += 1
        afd = s.chain("vector")
        s.wait("tensor", afd[0], afd[1])
        s.wait("gpsimd", afd[0], afd[1])

        s.ldg(wstr, s.Wout.rearrange("(c p) o -> p c o", p=128), sem="ldgB")
        s.wait("tensor", "ldgB")
        ost2 = s.f32("ost2")
        for j in range(NT):
            s.wait("vector", "st", s.val("st"))
            for oc in range(NCH):
                bank = s.psum[oc]
                for kc in range(NCH):
                    s.op("tensor", lambda e, oc=oc, j=j, kc=kc, b=bank: e.matmul(
                        out=b[:, 0:512],
                        lhsT=wstr[:, kc * D + oc * 128:kc * D + (oc + 1) * 128],
                        rhs=af1[:, kc * T + j * 512:kc * T + (j + 1) * 512],
                        start=(kc == 0), stop=(kc == NCH - 1)),
                        inc="stt" if kc == NCH - 1 else None)
                s.wait("vector", "stt", s.val("stt"))
                s.op("vector", lambda e, oc=oc, b=bank: e.tensor_scalar(
                    out=ost2[:, oc * 512:(oc + 1) * 512], in0=b[:, 0:512],
                    scalar1=s.bout_sb[:, oc:oc + 1], scalar2=None,
                    op0=ALU.add), inc="pst")
            s.wait("sync", "pst", s.val("pst"))
            s.sth(s.attn_s.rearrange("p (c t) -> p c t", c=NCH)[:, :, j * 512:(j + 1) * 512],
                  ost2.rearrange("p (c t) -> p c t", c=NCH))
            s.wait("tensor", "pst", s.val("pst"))
        s.pending_adds.append(s.attn_s)
        s.l1attn_st = s.val("st")


    def p_l1_ffn(self):
        s = self
        s.phase_barrier()

        def hchunk(kc):
            off = (R_X + kc * 4 * KB) if kc < 24 else (158 * KB + (kc - 24) * 4 * KB)
            return s.pool.ap().bitcast(BF16)[:, off // 2: off // 2 + T]

        bfspan = lambda off, n: s.pool.ap().bitcast(BF16)[:, off // 2: off // 2 + n]
        fspan = lambda off, n: s.pool.ap()[:, off // 4: off // 4 + n]
        wb1 = [bfspan(190 * KB + i * 2 * KB, 1024) for i in range(4)]
        sil = fspan(198 * KB, 512)
        us = fspan(200 * KB, 512)

        tmarks = {}
        emarks = []
        for oc in range(ICH):
            g0, u0 = wb1[2 * (oc % 2)], wb1[2 * (oc % 2) + 1]
            psem = "ldgA" if oc % 2 == 0 else "ldgB"
            if oc >= 2:
                s.wait("gpsimd", tmarks[oc - 2][0], tmarks[oc - 2][1])
            s.ldg(g0, s.w1T.rearrange("(c p) o -> p c o", p=128)[:, :, oc * 128:(oc + 1) * 128], sem=psem)
            s.ldg(u0, s.w2T.rearrange("(c p) o -> p c o", p=128)[:, :, oc * 128:(oc + 1) * 128], sem=psem)
            s.wait("tensor", psem, s.val(psem))
            for j in range(NT):
                n = oc * NT + j
                if n >= 2:
                    s.wait("tensor", "pss", emarks[n - 2][0])
                    s.wait("tensor", "pst", emarks[n - 2][1])
                bg, bu = s.psum[2 * (j % 2)], s.psum[2 * (j % 2) + 1]
                for kc in range(NCH):
                    s.op("tensor", lambda e, kc=kc, j=j, b=bg, g0=g0: e.matmul(
                        out=b[:, 0:512], lhsT=g0[:, kc * 128:(kc + 1) * 128],
                        rhs=s.xn_ap[:, kc * T + j * 512:kc * T + (j + 1) * 512],
                        start=(kc == 0), stop=(kc == NCH - 1)))
                for kc in range(NCH):
                    s.op("tensor", lambda e, kc=kc, j=j, b=bu, u0=u0: e.matmul(
                        out=b[:, 0:512], lhsT=u0[:, kc * 128:(kc + 1) * 128],
                        rhs=s.xn_ap[:, kc * T + j * 512:kc * T + (j + 1) * 512],
                        start=(kc == 0), stop=(kc == NCH - 1)),
                        inc="stt" if kc == NCH - 1 else None)
                s.wait("scalar", "stt", s.val("stt"))
                s.wait("scalar", "sv", s.val("sv"))
                s.op("scalar", lambda e, oc=oc, b=bg: e.activation(
                    out=sil, in_=b[:, 0:512], func=AF.Silu,
                    scale=s.s1_sb[:, oc:oc + 1]), inc="pss")
                pssv = s.val("pss")
                mks = s.chain("scalar")
                s.wait("vector", mks[0], mks[1])
                s.op("vector", lambda e, oc=oc, b=bu: e.tensor_scalar(
                    out=us, in0=b[:, 0:512], scalar1=s.s2_sb[:, oc:oc + 1],
                    scalar2=None, op0=ALU.mult), inc="pst")
                pstv = s.val("pst")
                emarks.append((pssv, pstv))
                s.chain("vector")
                s.op("vector", lambda e, oc=oc, j=j: e.tensor_tensor(
                    out=hchunk(oc)[:, j * 512:(j + 1) * 512], in0=sil, in1=us,
                    op=ALU.mult), inc="sv")
                s.chain("vector")
            mk = s.chain("tensor")
            tmarks[oc] = mk
        h1d = s.chain("vector")
        for eng in ("tensor", "sync", "gpsimd", "scalar"):
            s.wait(eng, h1d[0], h1d[1])

        def hidden_stage(wT, scol, dst_dram, buf_base):
            hb0 = bfspan(buf_base, 4096)
            hb1_ = bfspan(buf_base + 8 * KB, 4096)
            stg_ = bfspan(buf_base + 16 * KB, 2048)
            tm = {}
            gm = []
            for oc in range(ICH):
                hb_ = hb0 if oc % 2 == 0 else hb1_
                psem = "ldgA" if oc % 2 == 0 else "ldgB"
                if oc >= 2:
                    s.wait("gpsimd", tm[oc - 2][0], tm[oc - 2][1])
                s.ldg(hb_, wT.rearrange("(c p) o -> p c o", p=128)[:, :, oc * 128:(oc + 1) * 128], sem=psem)
                s.wait("tensor", psem, s.val(psem))
                for j in range(NT):
                    n = oc * NT + j
                    if n >= 2:
                        s.wait("tensor", "pss", gm[n - 2])
                    bank = s.psum[j % 2]
                    for kc in range(ICH):
                        s.op("tensor", lambda e, kc=kc, j=j, b=bank, hb_=hb_: e.matmul(
                            out=b[:, 0:512], lhsT=hb_[:, kc * 128:(kc + 1) * 128],
                            rhs=hchunk(kc)[:, j * 512:(j + 1) * 512],
                            start=(kc == 0), stop=(kc == ICH - 1)),
                            inc="stt" if kc == ICH - 1 else None)
                    s.wait("scalar", "stt", s.val("stt"))
                    s.op("scalar", lambda e, oc=oc, j=j, b=bank, sc=scol, stg_=stg_: e.activation(
                        out=stg_[:, j * 512:(j + 1) * 512], in_=b[:, 0:512],
                        func=AF.Gelu, scale=sc[:, oc:oc + 1]), inc="pss")
                    gm.append(s.val("pss"))
                mks = s.chain("scalar")
                s.wait("sync", mks[0], mks[1])
                s.sth(dst_dram.rearrange("p (c t) -> p c t", c=ICH)[:, oc, :], stg_)
                s.wait("scalar", "st", s.val("st"))
                mk = s.chain("tensor")
                tm[oc] = mk
            mkx = s.chain("scalar")
            for eng in ("sync", "vector", "tensor", "gpsimd"):
                s.wait(eng, mkx[0], mkx[1])

        hidden_stage(s.h1T, s.sh1_sb, s.h2_s, 126 * KB)
        s.wait("sync", "st", s.val("st"))
        s.ldh(bfspan(R_X, 24 * T), s.h2_s[:, 0:24 * T])
        s.ldh(bfspan(158 * KB, 8 * T), s.h2_s[:, 24 * T:32 * T])
        lv = s.val("ldh")
        s.wait("tensor", "ldh", lv)
        s.wait("sync", "ldh", lv)
        hidden_stage(s.h2T, s.sh2_sb, s.h3_s, 126 * KB)

        xh = fspan(R_D, NCH * 1024)
        ah = fspan(R_D + 32 * KB, NCH * 1024)

        def h3c(kc):
            return bfspan(R_X + kc * 2 * KB, 1024)

        for half in range(2):
            hsl = slice(half * 1024, (half + 1) * 1024)
            s.wait("sync", "st", s.val("st"))
            s.ldh(xh.rearrange("p (c t) -> p c t", c=NCH),
                  s.xpark.rearrange("p (c t) -> p c t", c=NCH)[:, :, hsl])
            s.ldh(ah.rearrange("p (c t) -> p c t", c=NCH),
                  s.attn_s.rearrange("p (c t) -> p c t", c=NCH)[:, :, hsl])
            for kc in range(ICH):
                s.ldh(h3c(kc), s.h3_s[:, kc * T + half * 1024:kc * T + half * 1024 + 1024],
                      sem="lha")
            lha_v = s.val("lha")
            s.wait("tensor", "lha", lha_v)
            ldv = s.val("ldh")
            s.wait("vector", "ldh", ldv)
            s.wait("tensor", "ldh", ldv)
            mkt = s.chain("tensor")
            s.wait("gpsimd", mkt[0], mkt[1])
            wmks = {}
            smarks = []
            for oc in range(NCH):
                hb_ = bfspan(156 * KB, 4096) if oc % 2 == 0 else bfspan(164 * KB, 4096)
                psem = "ldgA" if oc % 2 == 0 else "ldgB"
                if oc >= 2:
                    s.wait("gpsimd", wmks[oc - 2][0], wmks[oc - 2][1])
                s.ldg(hb_, s.w3T.rearrange("(c p) o -> p c o", p=128)[:, :, oc * 128:(oc + 1) * 128], sem=psem)
                s.wait("tensor", psem, s.val(psem))
                for j in range(2):
                    n = oc * 2 + j
                    if n >= 2:
                        s.wait("tensor", "pst", smarks[n - 2])
                    bank = s.psum[j % 2]
                    for kc in range(ICH):
                        s.op("tensor", lambda e, kc=kc, j=j, b=bank, hb_=hb_: e.matmul(
                            out=b[:, 0:512], lhsT=hb_[:, kc * 128:(kc + 1) * 128],
                            rhs=h3c(kc)[:, j * 512:(j + 1) * 512],
                            start=(kc == 0), stop=(kc == ICH - 1)),
                            inc="stt" if kc == ICH - 1 else None)
                    s.wait("vector", "stt", s.val("stt"))
                    s.op("vector", lambda e, oc=oc, j=j, b=bank, xh=xh: e.scalar_tensor_tensor(
                        out=xh[:, oc * 1024 + j * 512:oc * 1024 + (j + 1) * 512],
                        in0=b[:, 0:512], scalar=s.s3_sb[:, oc:oc + 1],
                        in1=xh[:, oc * 1024 + j * 512:oc * 1024 + (j + 1) * 512],
                        op0=ALU.mult, op1=ALU.add), inc="pst")
                    smarks.append(s.val("pst"))
                    s.chain("vector")
                    s.op("vector", lambda e, oc=oc, j=j, xh=xh, ah=ah: e.tensor_tensor(
                        out=xh[:, oc * 1024 + j * 512:oc * 1024 + (j + 1) * 512],
                        in0=xh[:, oc * 1024 + j * 512:oc * 1024 + (j + 1) * 512],
                        in1=ah[:, oc * 1024 + j * 512:oc * 1024 + (j + 1) * 512],
                        op=ALU.add), inc="sv")
                    s.chain("vector")
                mk = s.chain("tensor")
                wmks[oc] = mk
            fin = s.chain("vector")
            s.wait("sync", fin[0], fin[1])
            s.sth(s.xpark.rearrange("p (c t) -> p c t", c=NCH)[:, :, hsl],
                  xh.rearrange("p (c t) -> p c t", c=NCH))
            stv = s.val("st")
            s.wait("vector", "st", stv)
            s.wait("tensor", "st", stv)
            s.wait("sync", "st", stv)
        s.phase_barrier()
        s.ldh(s.f32("x"), s.xpark)
        lv2 = s.val("ldh")
        for eng in ("vector", "scalar", "tensor", "gpsimd"):
            s.wait(eng, "ldh", lv2)
        s.x_parked = False
        s.pending_adds = []

    def p_store(self):
        s = self
        s.phase_barrier()
        s.alloc("ostg", T * NCH * 4, at=R_D)
        o_tm = s.f32("ostg")
        if s.x_parked:
            s.ldh(s.f32("x"), s.xpark)
            for da in s.pending_adds:
                s.ldh(o_tm, da)
                v = s.val("ldh")
                s.wait("vector", "ldh", v)
                for c in range(NCH):
                    s.op("vector", lambda e, c=c: e.tensor_tensor(
                        out=s.xc(c), in0=s.xc(c), in1=o_tm[:, c * T:(c + 1) * T],
                        op=ALU.add), inc="sv")
                s.chain("vector")
            s.x_parked = False
            s.pending_adds = []
            s.wait("tensor", "sv", s.val("sv"))
            s.wait("tensor", "ldh", s.val("ldh"))
        base = s.val("pst")
        for a in range(16):
            for c in range(NCH):
                src = s.xc(c)[:, a * 128:(a + 1) * 128]
                dst = o_tm[:, a * D + c * 128: a * D + (c + 1) * 128]
                s.transpose_128(src, dst, base, a * NCH + c)
        s.wait("sync", "pst", s.val("pst"))
        s.sth(s.out.rearrange("(a p) d -> p a d", p=128), o_tm)
        s.wait("sync", "st", s.val("st"))


def host_inputs(ins, core):
    f = np.float32
    col = lambda v, nch: np.ascontiguousarray(np.asarray(v).reshape(nch, 128).T).astype(f)
    tr = lambda v: np.ascontiguousarray(np.asarray(v).T).astype(f)
    m = {
        "x_in": np.ascontiguousarray(np.asarray(ins["L0_attn_x"][core])).astype(f),
        "Wq": np.asarray(ins["L0_attn_Wq"], f), "Wk": np.asarray(ins["L0_attn_Wk"], f),
        "Wv": np.asarray(ins["L0_attn_Wv"], f), "Wm": np.asarray(ins["L0_attn_Wm"], f),
        "bq": col(ins["L0_attn_bq"], NCH), "bk": col(ins["L0_attn_bk"], NCH),
        "bv": np.asarray(ins["L0_attn_bv"], f).reshape(1, D), "bm": col(ins["L0_attn_bm"], NCH),
        "wgT": tr(ins["L0_ffn_w_gate"]), "wuT": tr(ins["L0_ffn_w_up"]),
        "wdT": tr(ins["L0_ffn_w_down"]),
        "Wqkv": np.asarray(ins["L1_attn_Wqkv"], f), "Wout": np.asarray(ins["L1_attn_Wout"], f),
        "bout": col(ins["L1_attn_bout"], NCH),
        "w1T": tr(ins["L1_ffn_w1_w"]), "w2T": tr(ins["L1_ffn_w2_w"]),
        "h1T": tr(ins["L1_ffn_h1_w"]), "h2T": tr(ins["L1_ffn_h2_w"]),
        "w3T": tr(ins["L1_ffn_w3_w"]),
        "s1c": col(ins["L1_ffn_w1_s"], ICH), "s2c": col(ins["L1_ffn_w2_s"], ICH),
        "sh1c": col(ins["L1_ffn_h1_s"], ICH), "sh2c": col(ins["L1_ffn_h2_s"], ICH),
        "s3c": col(ins["L1_ffn_w3_s"], NCH),
        "ident": np.eye(128, dtype=f),
        "identb": np.eye(128, dtype=ml_dtypes.bfloat16),
        "triu": np.triu(np.ones((128, 128), np.float32)).astype(ml_dtypes.bfloat16),
        "sel8": _sel8(),
        "ident4": np.tile(np.eye(128, dtype=f), (1, 4)),
    }
    return m


def np_softmax(x, axis=-1):
    m = np.max(x, axis=axis, keepdims=True)
    e = np.exp(x - m)
    return e / e.sum(axis=axis, keepdims=True)


def np_silu(x):
    return x / (1 + np.exp(-x))


def np_gelu_exact(x):
    from scipy.special import erf
    return 0.5 * x * (1 + erf(x / np.sqrt(2)))


def np_layernorm(x, eps=1e-5):
    m = x.mean(-1, keepdims=True)
    v = x.var(-1, keepdims=True)
    return (x - m) / np.sqrt(v + eps)


def ref_l0_attn(x, Wq, bq, Wk, bk, Wv, bv, Wm, bm):
    S = x.shape[0]
    def heads(w, b):
        return (x @ w + b).reshape(S, H, HS).transpose(1, 0, 2)
    q, k, v = heads(Wq, bq), heads(Wk, bk), heads(Wv, bv)
    sc = np.einsum("hqd,hkd->hqk", q, k) / np.sqrt(HS)
    mask = np.arange(S)[None, :] > np.arange(S)[:, None]
    sc = np.where(mask[None], -1e12, sc)
    p = np_softmax(sc, -1)
    a = np.einsum("hqk,hkd->hqd", p, v)
    a = a.transpose(1, 0, 2).reshape(S, D)
    return a @ Wm + bm


def ref_l0_ffn(x, w_gate, w_up, w_down):
    def wq_(w):
        sc = 1.0 / np.clip(np.mean(np.abs(w)), 1e-5, None)
        return np.clip(np.round(w * sc), -1, 1) / sc
    def aq_(t):
        sc = 127.0 / np.clip(np.max(np.abs(t), -1, keepdims=True), 1e-5, None)
        return np.clip(np.round(t * sc), -128, 127) / sc
    bl = lambda t, w: aq_(t) @ wq_(w).T
    g = np_silu(bl(x, w_gate))
    u = bl(x, w_up)
    return bl(g * u, w_down)


def ref_l1_attn(x, Wqkv, Wout, bout):
    S = x.shape[0]
    qkv = x @ Wqkv
    q, k, v = np.split(qkv, 3, -1)
    rs = lambda t: t.reshape(S, H, HS).transpose(1, 0, 2)
    q, k, v = rs(q), rs(k), rs(v)
    el = lambda t: np.where(t > 0, t + 1.0, np.exp(t))
    q, k = el(q), el(k)
    kv = np.einsum("hnd,hnf->hdf", k, v)
    ks = k.sum(1)
    qo = np.einsum("hnd,hdf->hnf", q, kv)
    qks = np.einsum("hnd,hd->hn", q, ks)
    o = qo / (qks[..., None] + 1e-6)
    o = o.transpose(1, 0, 2).reshape(S, D)
    return o @ Wout + bout


def ref_l1_ffn(x, ws):
    tl = lambda t, w, sc: (t @ w.T) * sc
    h = np_silu(tl(x, ws["w1_w"], ws["w1_s"])) * tl(x, ws["w2_w"], ws["w2_s"])
    h = np_gelu_exact(tl(h, ws["h1_w"], ws["h1_s"]))
    h = np_gelu_exact(tl(h, ws["h2_w"], ws["h2_s"]))
    return tl(h, ws["w3_w"], ws["w3_s"])


def _sel8():
    m = np.zeros((16, 1024), np.float32)
    for hp in range(8):
        for p in range(128):
            h = 2 * hp + (1 if p >= 64 else 0)
            m[h, 128 * hp + p] = 1.0
    return m


def kernel(**inputs):
    mk = MK(7, dump=None)
    nc = mk.build()
    shared = host_shared(inputs)
    in_maps = []
    for core in range(8):
        m = dict(shared)
        m["x_in"] = np.ascontiguousarray(np.asarray(inputs["L0_attn_x"][core])).astype(np.float32)
        in_maps.append(m)
    from concourse.bass_utils import run_bass_kernel_spmd
    res = run_bass_kernel_spmd(nc, in_maps, core_ids=list(range(8)))
    out = np.stack([np.asarray(r["out"], np.float32) for r in res.results], axis=0)
    return out


def host_shared(ins):
    f = np.float32
    col = lambda v, nch: np.ascontiguousarray(np.asarray(v).reshape(nch, 128).T).astype(f)
    tr = lambda v: np.ascontiguousarray(np.asarray(v).T).astype(f)
    return {
        "Wq": np.asarray(ins["L0_attn_Wq"], f), "Wk": np.asarray(ins["L0_attn_Wk"], f),
        "Wv": np.asarray(ins["L0_attn_Wv"], f), "Wm": np.asarray(ins["L0_attn_Wm"], f),
        "bq": col(ins["L0_attn_bq"], NCH), "bk": col(ins["L0_attn_bk"], NCH),
        "bv": np.asarray(ins["L0_attn_bv"], f).reshape(1, D), "bm": col(ins["L0_attn_bm"], NCH),
        "wgT": tr(ins["L0_ffn_w_gate"]), "wuT": tr(ins["L0_ffn_w_up"]),
        "wdT": tr(ins["L0_ffn_w_down"]),
        "Wqkv": np.asarray(ins["L1_attn_Wqkv"], f), "Wout": np.asarray(ins["L1_attn_Wout"], f),
        "bout": col(ins["L1_attn_bout"], NCH),
        "w1T": tr(ins["L1_ffn_w1_w"]), "w2T": tr(ins["L1_ffn_w2_w"]),
        "h1T": tr(ins["L1_ffn_h1_w"]), "h2T": tr(ins["L1_ffn_h2_w"]),
        "w3T": tr(ins["L1_ffn_w3_w"]),
        "s1c": col(ins["L1_ffn_w1_s"], ICH), "s2c": col(ins["L1_ffn_w2_s"], ICH),
        "sh1c": col(ins["L1_ffn_h1_s"], ICH), "sh2c": col(ins["L1_ffn_h2_s"], ICH),
        "s3c": col(ins["L1_ffn_w3_s"], NCH),
        "ident": np.eye(128, dtype=f),
        "identb": np.eye(128, dtype=ml_dtypes.bfloat16),
        "triu": np.triu(np.ones((128, 128), np.float32)).astype(ml_dtypes.bfloat16),
        "sel8": _sel8(),
        "ident4": np.tile(np.eye(128, dtype=f), (1, 4)),
    }
